# version 10
# baseline (speedup 1.0000x reference)
import numpy as np
import ml_dtypes
import concourse.bass as bass
import concourse.mybir as mybir
from concourse.bass_utils import run_bass_kernel_spmd

F32 = mybir.dt.float32
BF16 = mybir.dt.bfloat16
AF = mybir.ActivationFunctionType
ALU = mybir.AluOpType
AX = mybir.AxisListType.X

P = 128
D = 2048
T = 2048
B = 4
NCORE = 8
TOK = 1024
NT = TOK // P
NTF = T // P
KV = 512
NH = 16
HD = 128
NKVH = 4
INTER = 8192
MAGIC = float(3 * 2 ** 22)
NEG = -30000.0

X_CHUNKS = [0, 3, 4, 7, 8, 11, 12, 15]
Y_CHUNKS = [1, 2, 5, 6, 9, 10, 13, 14]
EXT = [1, 1, 2, 2, 3, 3, 4, 4]

_CACHE = {}
_PROF = False
_DEBUG = False
_SKIP_KNORM = False



def _rope_tables():
    inv = 1.0 / (10000.0 ** (np.arange(0, HD, 2, dtype=np.float64) / HD))
    pos = np.arange(T, dtype=np.float64)
    fr = pos[:, None] * inv[None, :]
    return np.cos(fr).astype(np.float32), np.sin(fr).astype(np.float32)


def _perm_deint():
    def perm(nheads):
        p = []
        for h in range(nheads):
            base = h * HD
            p += [base + 2 * i for i in range(HD // 2)]
            p += [base + 2 * i + 1 for i in range(HD // 2)]
        return np.array(p)
    return perm(NH), perm(NKVH)


def _masks_for(chunks):
    m = np.zeros((NT, P, 512), np.float32)
    for j, ch in enumerate(chunks):
        e = EXT[j]
        k0 = 512 * (e - 1)
        keys = k0 + np.arange(512)
        q = 128 * ch + np.arange(P)
        m[j] = np.where(keys[None, :] <= q[:, None], 0.0, NEG)
    return m


def _gap_tables(chunks):
    halves = [X_CHUNKS, Y_CHUNKS]
    meta = np.zeros((NT, 512), np.float32)
    for j, ch in enumerate(chunks):
        for r in range(2):
            for h in range(16):
                for c in range(8):
                    g = halves[r][c]
                    idx = r * 128 + h * 8 + c
                    if g < ch:
                        meta[j, idx] = 128 * (ch - g - 1)
                        meta[j, 256 + idx] = 0.0
                    else:
                        meta[j, idx] = 0.0
                        meta[j, 256 + idx] = -1e30
    return meta


def _prep_inputs(inputs):
    x = np.asarray(inputs["L0_attn_x"], np.float32)
    permq, permkv = _perm_deint()
    cos, sin = _rope_tables()
    cosp = cos[:, :]
    sinp = sin[:, :]

    W = {}
    W["wq"] = np.ascontiguousarray(np.asarray(inputs["L0_attn_Wq"], np.float32)[:, permq])
    W["wk"] = np.ascontiguousarray(np.asarray(inputs["L0_attn_Wk"], np.float32)[:, permkv])
    W["wv"] = np.ascontiguousarray(np.asarray(inputs["L0_attn_Wv"], np.float32))
    W["wo"] = np.ascontiguousarray(np.asarray(inputs["L0_attn_Wo"], np.float32))
    W["bq"] = np.asarray(inputs["L0_attn_bq"], np.float32)[permq].reshape(1, D)
    W["bk"] = np.asarray(inputs["L0_attn_bk"], np.float32)[permkv].reshape(1, KV)
    W["bv"] = np.asarray(inputs["L0_attn_bv"], np.float32).reshape(1, KV)
    W["bo"] = np.asarray(inputs["L0_attn_bo"], np.float32).reshape(1, D)
    W["alpha"] = np.asarray(inputs["L0_attn_alpha"], np.float32).reshape(1, NH)
    W["fw"] = np.ascontiguousarray(np.asarray(inputs["L0_ffn_weight"], np.float32))
    W["fb"] = np.asarray(inputs["L0_ffn_bias"], np.float32).reshape(1, D)
    wqkv_n = np.asarray(inputs["L1_attn_w_qkv"], np.float32)
    W["wqkv"] = np.ascontiguousarray(
        wqkv_n.reshape(D, 48, P).transpose(1, 0, 2).reshape(48 * D, P))
    wout_n = np.asarray(inputs["L1_attn_w_out"], np.float32)
    W["wout"] = np.ascontiguousarray(
        wout_n.reshape(D, 4, 512).transpose(1, 0, 2).reshape(4 * D, 512))
    W["decay"] = np.asarray(inputs["L1_attn_decay_param"], np.float32).reshape(1, NH)
    W["su"] = np.ascontiguousarray(
        np.asarray(inputs["L1_ffn_Wscale_ug"], np.float32)[:INTER].reshape(INTER // P, P).T)
    W["sg"] = np.ascontiguousarray(
        np.asarray(inputs["L1_ffn_Wscale_ug"], np.float32)[INTER:].reshape(INTER // P, P).T)
    W["sudp"] = np.ascontiguousarray(
        np.asarray(inputs["L1_ffn_SU_dp"], np.float32).reshape(INTER // P, P).T)
    W["suug"] = np.ascontiguousarray(
        np.asarray(inputs["L1_ffn_SU_ug"], np.float32).reshape(D // P, P).T)
    W["wdp"] = np.asarray(inputs["L1_ffn_Wscale_dp"], np.float32).reshape(1, D)
    wu_t = np.asarray(inputs["L1_ffn_W_up"], np.float32).T
    wg_t = np.asarray(inputs["L1_ffn_W_gate"], np.float32).T
    wd_t = np.asarray(inputs["L1_ffn_W_down"], np.float32).T
    W["wup"] = np.ascontiguousarray(
        wu_t.reshape(D, 64, P).transpose(1, 0, 2).reshape(64 * D, P))
    W["wgate"] = np.ascontiguousarray(
        wg_t.reshape(D, 64, P).transpose(1, 0, 2).reshape(64 * D, P))
    W["wdown"] = np.ascontiguousarray(
        wd_t.reshape(64, P, 4, 512).transpose(0, 2, 1, 3).reshape(64 * 4 * P, 512))
    hh = np.ones((1, 1), np.float32)
    for _ in range(7):
        hh = np.block([[hh, hh], [hh, -hh]])
    W["hbf"] = np.ascontiguousarray(hh / 4096.0)

    in_maps = []
    for c in range(NCORE):
        b = c // 2
        chunks = X_CHUNKS if c % 2 == 0 else Y_CHUNKS
        rows = np.concatenate([np.arange(128 * ch, 128 * ch + 128) for ch in chunks])
        m = dict(W)
        m["x_own"] = np.ascontiguousarray(x[b][rows])
        m["x_full"] = np.ascontiguousarray(x[b])
        m["cos_own"] = np.ascontiguousarray(cosp[rows])
        m["sin_own"] = np.ascontiguousarray(sinp[rows])
        m["cos_full"] = np.ascontiguousarray(cosp)
        m["sin_full"] = np.ascontiguousarray(sinp)
        m["masks"] = _masks_for(chunks)
        m["gapexp"] = _gap_tables(chunks)
        in_maps.append(m)
    return in_maps


def _scatter_output(results):
    out = np.zeros((B, T, D), np.float32)
    for c in range(NCORE):
        b = c // 2
        chunks = X_CHUNKS if c % 2 == 0 else Y_CHUNKS
        y = results[c]["y"]
        for j, ch in enumerate(chunks):
            out[b, 128 * ch:128 * ch + 128, :] = y[128 * j:128 * j + 128, :]
    return out



class Sem:

    def __init__(self, nc, name):
        self.h = nc.alloc_semaphore(name)
        self.n = 0

    def inc(self, instr, n=1):
        instr.then_inc(self.h, n)
        self.n += n
        return instr

    def dma(self, eng, *a, **kw):
        eng.wait_ge(self.h, self.n)
        ins = eng.dma_start(*a, **kw)
        ins.then_inc(self.h, 16)
        self.n += 16
        return ins

    def wait(self, eng, at=None):
        eng.wait_ge(self.h, self.n if at is None else at)


class SB:

    def __init__(self, nc):
        self.nc = nc
        self.base = (nc.sbuf_base + 63) // 64 * 64
        self.limit = nc.sbuf_top
        self.off = self.base
        self.n = 0

    def at(self, off_bytes, name, shape, dtype):
        self.n += 1
        h = self.nc.alloc_sbuf_tensor_at(
            f"{name}_{self.n}", list(shape), dtype, offset=self.base + off_bytes)
        nb = int(np.prod(shape[1:])) * (2 if dtype == BF16 else 4)
        assert self.base + off_bytes + nb <= self.limit, \
            f"SBUF overflow {name}: {off_bytes}+{nb}"
        return h


KB = 1024

OFF_R = 0
OFF_JUNK = 64 * KB
OFF_ROWSB = 68 * KB
OFF_ROWSB2 = 72 * KB
OFF_ROWSB3 = 76 * KB
OFF_ROWSF = 78 * KB + 512
OFF_IDENT = 80 * KB + 512
OFF_ABC = 81 * KB
OFF_MASK = 81 * KB + 512
OFF_OVL = 89 * KB + 512

O_KRT = 0
O_VBF = 16 * KB
O_QRT = 32 * KB
O_YT = 64 * KB
O_ROPE = 96 * KB
O_FREE = 108 * KB

PT_W = [1024] * 4 + [768] * 4 + [512] * 4 + [256] * 4
PT_OFF = [0]
for _w in PT_W[:-1]:
    PT_OFF.append(PT_OFF[-1] + _w)
PT_COLS = PT_OFF[-1] + PT_W[-1] + 128
AV_JMIN = [0] * 4 + [2] * 4 + [4] * 4 + [6] * 4


def build_program():
    nc = bass.Bass("TRN2", target_bir_lowering=False)

    _shapes = dict(
        x_own=(TOK, D), x_full=(T, D), cos_own=(TOK, 64), sin_own=(TOK, 64),
        cos_full=(T, 64), sin_full=(T, 64), masks=(NT, P, 512),
        gapexp=(NT, 512),
        wq=(D, D), wk=(D, KV), wv=(D, KV), wo=(D, D),
        bq=(1, D), bk=(1, KV), bv=(1, KV), bo=(1, D), alpha=(1, NH),
        fw=(D, D), fb=(1, D),
        wqkv=(48 * D, P), wout=(4 * D, 512), decay=(1, NH),
        su=(P, 64), sg=(P, 64), sudp=(P, 64), suug=(P, 16), wdp=(1, D),
        wup=(64 * D, P), wgate=(64 * D, P), wdown=(64 * 4 * P, 512), hbf=(P, P),
    )
    _declared = {}

    class _In:
        def __getattr__(self, name):
            if name not in _declared:
                _declared[name] = nc.dram_tensor(
                    name, list(_shapes[name]), F32, kind="ExternalInput").ap()
            return _declared[name]

    IN = _In()
    y_out = nc.dram_tensor("y", [TOK, D], F32, kind="ExternalOutput").ap()
    wsband = nc.dram_tensor("wsband", [P, 4], F32).ap()
    if _DEBUG:
        dbg_krT = nc.dram_tensor("dbg_krT", [P, NKVH * T], F32, kind="ExternalOutput").ap()
        dbg_v = nc.dram_tensor("dbg_v", [P, NTF * KV], F32, kind="ExternalOutput").ap()
        dbg_qrT = nc.dram_tensor("dbg_qrT", [P, NH * TOK], F32, kind="ExternalOutput").ap()
        dbg_yT = nc.dram_tensor("dbg_yT", [P, NH * TOK], F32, kind="ExternalOutput").ap()
        dbg_lnbf = nc.dram_tensor("dbg_lnbf", [P, D], F32, kind="ExternalOutput").ap()
        dbg_lnfT = nc.dram_tensor("dbg_lnfT", [P, D], F32, kind="ExternalOutput").ap()
        dbg_wk = nc.dram_tensor("dbg_wk", [P, 16 * KV], F32, kind="ExternalOutput").ap()
        dbg_kst = nc.dram_tensor("dbg_kst", [P, KV], F32, kind="ExternalOutput").ap()
        dbg_xf = nc.dram_tensor("dbg_xf", [P, D], F32, kind="ExternalOutput").ap()
        dbg_xqT = nc.dram_tensor("dbg_xqT", [P, 16 * 1024], F32, kind="ExternalOutput").ap()
        dbg_wqT = nc.dram_tensor("dbg_wqT", [P, 16 * 512], F32, kind="ExternalOutput").ap()
        dbg_rows = nc.dram_tensor("dbg_rows", [P, KV], F32, kind="ExternalOutput").ap()

    ident_dram = nc.inline_tensor(np.eye(P, dtype=ml_dtypes.bfloat16), name="identc").ap()

    sb = SB(nc)
    R = sb.at(OFF_R, "R", (P, NT * D), F32)
    junk = sb.at(OFF_JUNK, "junk", (P, D), BF16)
    rows_bf = sb.at(OFF_ROWSB, "rows_bf", (P, D), BF16)
    rows_bf2 = sb.at(OFF_ROWSB2, "rows_bf2", (P, D), BF16)
    rows_bf3 = sb.at(OFF_ROWSB3, "rows_bf3", (P, 1280), BF16)
    rows_f = sb.at(OFF_ROWSF, "rows_f", (P, KV), F32)
    ident = sb.at(OFF_IDENT, "ident", (P, P), BF16)
    alpha_bc = sb.at(OFF_ABC, "alpha_bc", (P, NH), F32)
    negaa = sb.at(OFF_ABC + 64, "negaa", (P, NH), F32)
    masksb = sb.at(OFF_MASK, "masksb", (P, NT * 512), BF16)

    bq_row = rows_bf[0:1, 0:D]
    bo_row = rows_bf2[0:1, 0:D]
    bk_row = rows_bf3[0:1, 0:KV]
    bv_row = rows_bf3[0:1, KV:2 * KV]
    onesb_row = rows_bf3[0:1, 2 * KV:2 * KV + P]
    alpha_row = rows_f[0:1, 0:NH]
    onesf_row = rows_f[0:1, 16:16 + P]
    c_s1 = [rows_f[:, 256 + i:257 + i] for i in range(2)]
    c_s2 = [rows_f[:, 258 + i:259 + i] for i in range(2)]
    c_m = [rows_f[:, 260 + i:261 + i] for i in range(2)]
    c_var = [rows_f[:, 262 + i:263 + i] for i in range(2)]
    c_rstd = [rows_f[:, 264 + i:265 + i] for i in range(2)]
    c_nmr = [rows_f[:, 266 + i:267 + i] for i in range(2)]
    c_t0 = [rows_f[:, 268 + i:269 + i] for i in range(2)]
    c_rn = [rows_f[:, 272 + 16 * i:288 + 16 * i] for i in range(2)]
    mxc = [rows_f[:, 304 + 4 * i:308 + 4 * i] for i in range(2)]
    laccc = [rows_f[:, 312 + 4 * i:316 + 4 * i] for i in range(2)]
    c_negmax = [rows_f[:, 320 + i:321 + i] for i in range(2)]
    c_l = [rows_f[:, 322 + i:323 + i] for i in range(2)]
    c_rl = [rows_f[:, 324 + i:325 + i] for i in range(2)]
    c_mk = rows_f[:, 326:327]
    c_nt1 = [rows_f[:, 328 + i:329 + i] for i in range(2)]
    c_nt2 = [rows_f[:, 330 + i:331 + i] for i in range(2)]

    krT = sb.at(OFF_OVL + O_KRT, "krT", (P, NKVH * T), BF16)
    vbf = sb.at(OFF_OVL + O_VBF, "vbf", (P, NTF * KV), BF16)
    qrT = sb.at(OFF_OVL + O_QRT, "qrT", (P, NH * TOK), BF16)
    yT = sb.at(OFF_OVL + O_YT, "yT", (P, NH * TOK), BF16)
    rqcos = sb.at(OFF_OVL + O_ROPE, "rqcos", (P, NT * 64), F32)
    rqsin = sb.at(OFF_OVL + O_ROPE + 2 * KB, "rqsin", (P, NT * 64), F32)
    rkcos = sb.at(OFF_OVL + O_ROPE + 4 * KB, "rkcos", (P, NTF * 64), F32)
    rksin = sb.at(OFF_OVL + O_ROPE + 8 * KB, "rksin", (P, NTF * 64), F32)

    kvo = OFF_OVL + O_QRT
    xf2 = [sb.at(kvo + i * 8 * KB, f"xf{i}", (P, D), F32) for i in range(2)]
    lnbf2 = [sb.at(kvo + 16 * KB + i * 4 * KB, f"lnbf{i}", (P, D), BF16) for i in range(2)]
    lnfT2 = [sb.at(kvo + 24 * KB + i * 4 * KB, f"lnfT{i}", (P, D), BF16) for i in range(2)]
    wkbf = sb.at(kvo + 32 * KB, "wkbf", (P, 16 * KV), BF16)
    wvbf = sb.at(kvo + 48 * KB, "wvbf", (P, 16 * KV), BF16)
    kst2 = [sb.at(OFF_OVL + O_FREE + i * 2 * KB, f"kst{i}", (P, KV), F32) for i in range(2)]
    ksq = sb.at(OFF_OVL + O_FREE + 4 * KB, "ksq", (P, KV), F32)
    rtA = sb.at(OFF_OVL + O_FREE + 6 * KB, "rtA", (P, KV), F32)
    rtB = sb.at(OFF_OVL + O_FREE + 8 * KB, "rtB", (P, KV), F32)
    kbf = sb.at(OFF_OVL + O_FREE + 6 * KB, "kbf", (P, KV), BF16)

    qo = OFF_OVL + O_YT
    wqhalf = sb.at(qo, "wqhalf", (P, 16 * 1024), BF16)
    qlnbf = sb.at(OFF_JUNK, "qlnbf", (P, D), BF16)
    qlnfT = sb.at(OFF_OVL + O_FREE, "qlnfT", (P, D), BF16)
    qstage = sb.at(OFF_OVL + O_FREE, "qstage", (P, 1024), F32)
    qsq = sb.at(OFF_OVL + O_FREE + 4 * KB, "qsq", (P, KV), F32)
    qr1 = sb.at(OFF_OVL + O_FREE + 6 * KB, "qr1", (P, KV), F32)
    qr2 = sb.at(OFF_OVL + O_FREE + 8 * KB, "qr2", (P, KV), F32)
    qbf = sb.at(OFF_OVL + O_FREE + 4 * KB, "qbf", (P, 1024), BF16)

    Pbuf = sb.at(OFF_JUNK, "Pbuf", (P, D), BF16)
    PTbuf = sb.at(OFF_OVL + O_ROPE, "PTbuf", (P, PT_COLS), BF16)

    wobf2 = [sb.at(OFF_OVL + O_ROPE, "wobf0", (P, 16 * 512), BF16),
             sb.at(OFF_OVL + O_QRT, "wobf1", (P, 16 * 512), BF16)]

    f0_ln = sb.at(OFF_OVL + 0, "f0_ln", (P, D), F32)
    f0_xq = sb.at(OFF_OVL + 8 * KB, "f0_xq", (P, D), BF16)
    f0_tmp = sb.at(OFF_OVL + 8 * KB, "f0_tmp", (P, KV), F32)
    f0_xqT = sb.at(OFF_OVL + 12 * KB, "f0_xqT", (P, 16 * 1024), BF16)
    f0_wst = sb.at(OFF_OVL + 44 * KB, "f0_wst", (P, 4 * D), F32)
    f0_wqn = sb.at(OFF_OVL + 76 * KB, "f0_wqn", (P, 4 * D), BF16)
    f0_wsbc = sb.at(OFF_OVL + 92 * KB, "f0_wsbc", (P, KV), F32)
    f0_bbc = sb.at(OFF_OVL + 94 * KB, "f0_bbc", (P, D), F32)
    f0_wsr = sb.at(OFF_OVL + 94 * KB, "f0_wsr", (P, KV), F32)
    f0_wqT = sb.at(OFF_OVL + 102 * KB, "f0_wqT", (P, 16 * 512), BF16)

    LD = Sem(nc, "LD")
    WD = Sem(nc, "WD")
    TE = Sem(nc, "TE")
    VE = Sem(nc, "VE")
    SE = Sem(nc, "SE")
    ST = Sem(nc, "ST")
    WD2 = Sem(nc, "WD2")
    CC = Sem(nc, "CC")

    blk = nc.Block()
    block = blk.__enter__()

    def on_sync(f):
        block.sync(f)

    def on_gp(f):
        block.gpsimd(f)

    def on_vec(f):
        block.vector(f)

    def on_scl(f):
        block.scalar(f)

    def on_ten(f):
        block.tensor(f)

    _cur_scope = [None]

    def scope(name):
        if not _PROF:
            return
        if _cur_scope[0] is not None:
            _cur_scope[0].__exit__(None, None, None)
        cm = nc.named_scope(name)
        cm.__enter__()
        _cur_scope[0] = cm

    scope("p0")
    @on_sync
    def _(s):
        for j in range(NT):
            LD.dma(s, out=R[:, j * D:(j + 1) * D],
                               in_=IN.x_own[j * P:(j + 1) * P, :])
        LD.dma(s, out=ident[:, :], in_=ident_dram)
        LD.dma(s, out=alpha_row, in_=IN.alpha)
        LD.dma(s, 
            out=rqcos.ap().rearrange("p (j d) -> p j d", d=64),
            in_=IN.cos_own.rearrange("(j p) d -> p j d", p=P))
        LD.dma(s, 
            out=rqsin.ap().rearrange("p (j d) -> p j d", d=64),
            in_=IN.sin_own.rearrange("(j p) d -> p j d", p=P))
        LD.dma(s, 
            out=rkcos.ap().rearrange("p (j d) -> p j d", d=64),
            in_=IN.cos_full.rearrange("(j p) d -> p j d", p=P))
        LD.dma(s, 
            out=rksin.ap().rearrange("p (j d) -> p j d", d=64),
            in_=IN.sin_full.rearrange("(j p) d -> p j d", p=P))

    @on_gp
    def _(g):
        WD.dma(g, out=bq_row, in_=IN.bq)
        WD.dma(g, out=bk_row, in_=IN.bk)
        WD.dma(g, out=bv_row, in_=IN.bv)
        WD.dma(g, out=bo_row, in_=IN.bo)
        WD.dma(g, out=masksb.ap().rearrange("p (j k) -> p j k", k=512),
                           in_=IN.masks.rearrange("j p k -> p j k"))
        WD.dma(g, out=wkbf.ap().rearrange("p (c f) -> p c f", f=KV),
                           in_=IN.wk.rearrange("(c p) f -> p c f", p=P))
        WD.dma(g, out=wvbf.ap().rearrange("p (c f) -> p c f", f=KV),
                           in_=IN.wv.rearrange("(c p) f -> p c f", p=P))

    chk_ld0 = LD.n
    chk_wd0 = WD.n

    @on_vec
    def _(v):
        v.wait_ge(LD.h, chk_ld0)
        v.memset(onesb_row, 1.0)
        v.memset(onesf_row, 1.0)
        v.drain()
        VE.inc(v.tensor_scalar(out=onesf_row, in0=onesf_row, scalar1=1.0,
                               scalar2=None, op0=ALU.mult))

    with nc.psum_tensor("pc0", [P, NH], F32) as pc0:
        @on_ten
        def _(t):
            t.wait_ge(VE.h, VE.n)
            t.wait_ge(LD.h, chk_ld0)
            TE.inc(t.matmul(pc0[:, :], onesf_row, alpha_row, start=True, stop=True))

        @on_vec
        def _(v):
            v.wait_ge(TE.h, TE.n)
            v.tensor_copy(alpha_bc[:, :], pc0[:, :])
            v.drain()
            VE.inc(v.scalar_tensor_tensor(out=negaa[:, :], in0=alpha_bc[:, :],
                                          scalar=-1.0, in1=alpha_bc[:, :],
                                          op0=ALU.mult, op1=ALU.min))

    scope("p1_kv")
    with nc.psum_tensor("trp", [P, D], BF16) as trp, \
         nc.psum_tensor("kps", [P, KV], F32) as kps, \
         nc.psum_tensor("kps2", [P, KV], F32) as kps2, \
         nc.psum_tensor("vps", [P, KV], F32) as vps, \
         nc.psum_tensor("vps2", [P, KV], F32) as vps2:

        kpsl = [kps, kps2]
        vpsl = [vps, vps2]
        xf_free = [0, 0]
        lnbf_free = [0, 0]
        lnfT_free = [0, 0]
        trp_free = [0, 0]
        kps_free = [0, 0]
        vps_free = [0, 0]
        ktr_free = [0, 0]
        LNEPS = 1e-5

        for ft in range(NTF):
            bi = ft % 2

            @on_sync
            def _(s, ft=ft, bi=bi):
                s.wait_ge(SE.h, xf_free[bi])
                LD.dma(s, out=xf2[bi][:, :],
                                   in_=IN.x_full[ft * P:(ft + 1) * P, :])
            chk_x = LD.n

            @on_vec
            def _(v, bi=bi, chk_x=chk_x):
                v.wait_ge(LD.h, chk_x)
                VE.inc(v.reduce_sum(out=c_s1[bi], in_=xf2[bi][:, :], axis=AX))
            chk_v1 = VE.n

            @on_scl
            def _(scl, bi=bi, chk_x=chk_x):
                scl.wait_ge(LD.h, chk_x)
                SE.inc(scl.activation(out=junk[:, :], in_=xf2[bi][:, :],
                                      func=AF.Square, accum_out=c_s2[bi]))
            chk_sq = SE.n

            @on_vec
            def _(v, bi=bi, chk_sq=chk_sq):
                v.wait_ge(SE.h, chk_sq)
                v.drain()
                v.tensor_scalar(out=c_m[bi], in0=c_s1[bi], scalar1=1.0 / D, scalar2=None,
                                op0=ALU.mult)
                v.tensor_scalar(out=c_var[bi], in0=c_s2[bi], scalar1=1.0 / D,
                                scalar2=1e-5, op0=ALU.mult, op1=ALU.add)
                v.drain()
                v.scalar_tensor_tensor(out=c_t0[bi], in0=c_m[bi], scalar=1.0, in1=c_m[bi],
                                       op0=ALU.mult, op1=ALU.mult)
                v.drain()
                VE.inc(v.tensor_sub(c_var[bi], c_var[bi], c_t0[bi]))
            chk_v2 = VE.n

            @on_scl
            def _(scl, bi=bi, chk_v2=chk_v2):
                scl.wait_ge(VE.h, chk_v2)
                SE.inc(scl.activation(out=c_rstd[bi], in_=c_var[bi], func=AF.Sqrt))
            chk_s2 = SE.n

            @on_vec
            def _(v, bi=bi, chk_s2=chk_s2):
                v.wait_ge(SE.h, chk_s2)
                v.drain()
                v.reciprocal(c_nt1[bi], c_rstd[bi])
                v.drain()
                v.tensor_mul(c_nt2[bi], c_var[bi], c_nt1[bi])
                v.drain()
                v.tensor_add(c_rstd[bi], c_rstd[bi], c_nt2[bi])
                v.drain()
                v.tensor_scalar(out=c_rstd[bi], in0=c_rstd[bi], scalar1=0.5,
                                scalar2=None, op0=ALU.mult)
                v.drain()
                v.reciprocal(c_rstd[bi], c_rstd[bi])
                v.drain()
                VE.inc(v.scalar_tensor_tensor(out=c_nmr[bi], in0=c_m[bi], scalar=-1.0,
                                              in1=c_rstd[bi], op0=ALU.mult, op1=ALU.mult))
            chk_v3 = VE.n

            @on_scl
            def _(scl, bi=bi, chk_v3=chk_v3):
                scl.wait_ge(VE.h, chk_v3)
                scl.wait_ge(VE.h, lnbf_free[bi])
                SE.inc(scl.activation(out=lnbf2[bi][:, :], in_=xf2[bi][:, :],
                                      func=AF.Identity, bias=c_nmr[bi], scale=c_rstd[bi]))
            chk_ln = SE.n
            xf_free[bi] = SE.n

            @on_ten
            def _(t, bi=bi, chk_ln=chk_ln):
                t.wait_ge(SE.h, chk_ln)
                t.wait_ge(VE.h, trp_free[0])
                t.wait_ge(VE.h, trp_free[1])
                for c in range(16):
                    i = t.transpose(trp[:, c * P:(c + 1) * P],
                                    lnbf2[bi][:, c * P:(c + 1) * P], ident[:, :])
                TE.inc(i)
            chk_tr = TE.n

            @on_vec
            def _(v, bi=bi, chk_tr=chk_tr):
                v.wait_ge(TE.h, chk_tr)
                v.wait_ge(TE.h, lnfT_free[bi])
                VE.inc(v.tensor_copy(lnfT2[bi][:, :], trp[:, :]))
            chk_lnT = VE.n
            trp_free[0] = VE.n
            trp_free[1] = VE.n
            lnbf_free[bi] = VE.n

            @on_ten
            def _(t, ft=ft, bi=bi, chk_lnT=chk_lnT):
                t.wait_ge(VE.h, chk_lnT)
                t.wait_ge(WD.h, chk_wd0)
                t.wait_ge(VE.h, kps_free[bi])
                t.wait_ge(SE.h, vps_free[bi])
                for c in range(16):
                    t.matmul(kpsl[bi][:, :], lnfT2[bi][:, c * P:(c + 1) * P],
                             wkbf[:, c * KV:(c + 1) * KV], start=(c == 0), stop=False)
                t.matmul(kpsl[bi][:, :], onesb_row, bk_row, start=False, stop=True)
                for c in range(16):
                    t.matmul(vpsl[bi][:, :], lnfT2[bi][:, c * P:(c + 1) * P],
                             wvbf[:, c * KV:(c + 1) * KV], start=(c == 0), stop=False)
                i = t.matmul(vpsl[bi][:, :], onesb_row, bv_row, start=False, stop=True)
                TE.inc(i)
            chk_kv = TE.n
            lnfT_free[bi] = TE.n

            @on_scl
            def _(scl, ft=ft, bi=bi, chk_kv=chk_kv):
                scl.wait_ge(TE.h, chk_kv)
                SE.inc(scl.copy(vbf[:, ft * KV:(ft + 1) * KV], vpsl[bi][:, :]))
            vps_free[bi] = SE.n

            @on_vec
            def _(v, bi=bi, chk_kv=chk_kv):
                v.wait_ge(TE.h, chk_kv)
                VE.inc(v.tensor_copy(kst2[bi][:, :], kpsl[bi][:, :]))
            chk_kev = VE.n
            kps_free[bi] = VE.n

            @on_scl
            def _(scl, bi=bi, chk_kev=chk_kev):
                scl.wait_ge(VE.h, chk_kev)
                SE.inc(scl.activation(out=ksq[:, :], in_=kst2[bi][:, :], func=AF.Square))
            chk_ksq = SE.n

            @on_vec
            def _(v, bi=bi, ft=ft, chk_ksq=chk_ksq):
                v.wait_ge(SE.h, chk_ksq)
                VE.inc(v.reduce_sum(
                    out=c_rn[bi][:, 0:NKVH],
                    in_=ksq.ap().rearrange("p (g d) -> p g d", g=NKVH), axis=AX))
            chk_rn = VE.n

            @on_vec
            def _(v, bi=bi, chk_rn=chk_rn):
                v.drain()
                VE.inc(v.tensor_scalar(out=c_rn[bi][:, 0:NKVH], in0=c_rn[bi][:, 0:NKVH],
                                       scalar1=1e-12, scalar2=None, op0=ALU.max))
            chk_rnf = VE.n

            @on_scl
            def _(scl, bi=bi, chk_rnf=chk_rnf):
                scl.wait_ge(VE.h, chk_rnf)
                SE.inc(scl.activation(out=c_rn[bi][:, 4:4 + NKVH], in_=c_rn[bi][:, 0:NKVH],
                                      func=AF.Sqrt))
            chk_rs = SE.n

            @on_vec
            def _(v, bi=bi, ft=ft, chk_rs=chk_rs, ktf=ktr_free[bi]):
                v.wait_ge(SE.h, chk_rs)
                v.wait_ge(TE.h, ktf)
                if _SKIP_KNORM:
                    VE.inc(v.tensor_copy(kbf[:, :], kst2[bi][:, :]))
                    return
                v.reciprocal(c_rn[bi][:, 8:8 + NKVH], c_rn[bi][:, 4:4 + NKVH])
                v.drain()
                v.tensor_mul(c_rn[bi][:, 8:8 + NKVH], c_rn[bi][:, 0:NKVH],
                             c_rn[bi][:, 8:8 + NKVH])
                v.drain()
                v.tensor_add(c_rn[bi][:, 4:4 + NKVH], c_rn[bi][:, 4:4 + NKVH],
                             c_rn[bi][:, 8:8 + NKVH])
                v.drain()
                v.tensor_scalar(out=c_rn[bi][:, 4:4 + NKVH], in0=c_rn[bi][:, 4:4 + NKVH],
                                scalar1=0.5, scalar2=1e-6, op0=ALU.mult, op1=ALU.add)
                v.drain()
                v.reciprocal(c_rn[bi][:, 4:4 + NKVH], c_rn[bi][:, 4:4 + NKVH])
                v.drain()
                k3 = kst2[bi].ap().rearrange("p (g d) -> p g d", g=NKVH)
                v.tensor_tensor(
                    out=k3, in0=k3,
                    in1=c_rn[bi][:, 4:4 + NKVH, None].broadcast_to([P, NKVH, HD]),
                    op=ALU.mult)
                v.drain()
                x1 = k3[:, :, 0:64]
                x2 = k3[:, :, 64:128]
                cT = rkcos[:, ft * 64:(ft + 1) * 64][:, None, :].broadcast_to([P, NKVH, 64])
                sT = rksin[:, ft * 64:(ft + 1) * 64][:, None, :].broadcast_to([P, NKVH, 64])
                rA3 = rtA.ap().rearrange("p (g d) -> p g d", g=NKVH)[:, :, 0:64]
                rB3 = rtB.ap().rearrange("p (g d) -> p g d", g=NKVH)[:, :, 0:64]
                rC3 = ksq.ap().rearrange("p (g d) -> p g d", g=NKVH)[:, :, 0:64]
                v.tensor_mul(rA3, x1, cT)
                v.tensor_mul(rB3, x2, sT)
                v.tensor_mul(rC3, x1, sT)
                v.drain()
                v.tensor_sub(x1, rA3, rB3)
                v.drain()
                v.tensor_mul(rA3, x2, cT)
                v.drain()
                v.tensor_add(x2, rC3, rA3)
                v.drain()
                VE.inc(v.tensor_copy(kbf[:, :], kst2[bi][:, :]))
            chk_kbf = VE.n

            @on_ten
            def _(t, ft=ft, chk_kbf=chk_kbf):
                t.wait_ge(VE.h, chk_kbf)
                t.wait_ge(VE.h, trp_free[0])
                for g in range(NKVH):
                    i = t.transpose(trp[:, g * P:(g + 1) * P],
                                    kbf[:, g * P:(g + 1) * P], ident[:, :])
                TE.inc(i)
            chk_ktr = TE.n
            ktr_free[bi] = TE.n

            @on_vec
            def _(v, ft=ft, chk_ktr=chk_ktr):
                v.wait_ge(TE.h, chk_ktr)
                VE.inc(v.tensor_copy(
                    krT.ap().rearrange("p (g t) -> p g t", g=NKVH)[:, :, ft * P:(ft + 1) * P],
                    trp[:, 0:NKVH * P].rearrange("p (g q) -> p g q", g=NKVH)))
            trp_free[0] = VE.n

    def phase_barrier():
        te_n, ve_n, se_n, ld_n, wd_n, st_n = TE.n, VE.n, SE.n, LD.n, WD.n, ST.n
        wd2_n = WD2.n

        def bw(e):
            e.wait_ge(TE.h, te_n)
            e.wait_ge(VE.h, ve_n)
            e.wait_ge(SE.h, se_n)
            e.wait_ge(LD.h, ld_n)
            e.wait_ge(WD.h, wd_n)
            e.wait_ge(WD2.h, wd2_n)
            e.wait_ge(ST.h, st_n)

        on_sync(bw)
        on_gp(bw)
        on_vec(bw)
        on_scl(bw)
        on_ten(bw)

    WDs = [WD, WD2]

    phase_barrier()

    if _DEBUG:
        @on_gp
        def _(g):
            ST.dma(g, out=dbg_lnbf, in_=lnbf2[1][:, :])
            ST.dma(g, out=dbg_lnfT, in_=lnfT2[1][:, :])
            ST.dma(g, out=dbg_wk, in_=wkbf[:, :])
            ST.dma(g, out=dbg_kst, in_=kst2[1][:, :])
            ST.dma(g, out=dbg_xf, in_=xf2[1][:, :])
            ST.wait(g)
        phase_barrier()

    scope("p2_q")
    with nc.psum_tensor("trq", [P, D], BF16) as trq, \
         nc.psum_tensor("qps", [P, 1024], F32) as qps, \
         nc.psum_tensor("qps2", [P, 1024], F32) as qps2:
        qpsl = [qps, qps2]
        lnT_free = 0
        trq_free = 0
        qln_free_te = 0
        qps_free = [0, 0]
        qsq_free_te = 0
        wq_used = 0

        for hf in range(2):
            @on_gp
            def _(g, hf=hf, wq_used=wq_used):
                if hf > 0:
                    g.wait_ge(TE.h, wq_used)
                WD.dma(g, 
                    out=wqhalf.ap().rearrange("p (c f) -> p c f", f=1024),
                    in_=IN.wq[:, hf * 1024:(hf + 1) * 1024]
                        .rearrange("(c p) f -> p c f", p=P))
            chk_wq = WD.n

            for j in range(NT):
                bi = j % 2
                @on_vec
                def _(v, j=j, bi=bi):
                    VE.inc(v.reduce_sum(out=c_s1[bi], in_=R[:, j * D:(j + 1) * D],
                                        axis=AX))

                @on_scl
                def _(scl, j=j, bi=bi, tf=qln_free_te):
                    scl.wait_ge(TE.h, tf)
                    SE.inc(scl.activation(out=qlnbf[:, :], in_=R[:, j * D:(j + 1) * D],
                                          func=AF.Square, accum_out=c_s2[bi]))
                chk_sq = SE.n

                @on_vec
                def _(v, bi=bi, chk_sq=chk_sq):
                    v.wait_ge(SE.h, chk_sq)
                    v.drain()
                    v.tensor_scalar(out=c_m[bi], in0=c_s1[bi], scalar1=1.0 / D,
                                    scalar2=None, op0=ALU.mult)
                    v.tensor_scalar(out=c_var[bi], in0=c_s2[bi], scalar1=1.0 / D,
                                    scalar2=1e-5, op0=ALU.mult, op1=ALU.add)
                    v.drain()
                    v.scalar_tensor_tensor(out=c_t0[bi], in0=c_m[bi], scalar=1.0,
                                           in1=c_m[bi], op0=ALU.mult, op1=ALU.mult)
                    v.drain()
                    VE.inc(v.tensor_sub(c_var[bi], c_var[bi], c_t0[bi]))
                chk_v2 = VE.n

                @on_scl
                def _(scl, bi=bi, chk_v2=chk_v2):
                    scl.wait_ge(VE.h, chk_v2)
                    SE.inc(scl.activation(out=c_rstd[bi], in_=c_var[bi], func=AF.Sqrt))
                chk_s2 = SE.n

                @on_vec
                def _(v, bi=bi, chk_s2=chk_s2):
                    v.wait_ge(SE.h, chk_s2)
                    v.drain()
                    v.reciprocal(c_nt1[bi], c_rstd[bi])
                    v.drain()
                    v.tensor_mul(c_nt2[bi], c_var[bi], c_nt1[bi])
                    v.drain()
                    v.tensor_add(c_rstd[bi], c_rstd[bi], c_nt2[bi])
                    v.drain()
                    v.tensor_scalar(out=c_rstd[bi], in0=c_rstd[bi], scalar1=0.5,
                                    scalar2=None, op0=ALU.mult)
                    v.drain()
                    v.reciprocal(c_rstd[bi], c_rstd[bi])
                    v.drain()
                    VE.inc(v.scalar_tensor_tensor(out=c_nmr[bi], in0=c_m[bi],
                                                  scalar=-1.0, in1=c_rstd[bi],
                                                  op0=ALU.mult, op1=ALU.mult))
                chk_v3 = VE.n

                @on_scl
                def _(scl, j=j, bi=bi, chk_v3=chk_v3):
                    scl.wait_ge(VE.h, chk_v3)
                    SE.inc(scl.activation(out=qlnbf[:, :],
                                          in_=R[:, j * D:(j + 1) * D],
                                          func=AF.Identity, bias=c_nmr[bi],
                                          scale=c_rstd[bi]))
                chk_ln = SE.n

                @on_ten
                def _(t, bi=bi, chk_ln=chk_ln, trf=trq_free):
                    t.wait_ge(SE.h, chk_ln)
                    t.wait_ge(VE.h, trf)
                    for c in range(16):
                        i = t.transpose(trq[:, c * P:(c + 1) * P],
                                        qlnbf[:, c * P:(c + 1) * P], ident[:, :])
                    TE.inc(i)
                chk_tr = TE.n
                qln_free_te = TE.n

                @on_vec
                def _(v, bi=bi, chk_tr=chk_tr, lf=lnT_free, sf=qsq_free_te):
                    v.wait_ge(TE.h, chk_tr)
                    v.wait_ge(TE.h, lf)
                    v.wait_ge(TE.h, sf)
                    VE.inc(v.tensor_copy(qlnfT[:, :], trq[:, :]))
                chk_lnT = VE.n
                trq_free = VE.n

                @on_ten
                def _(t, bi=bi, hf=hf, chk_lnT=chk_lnT, chk_wq=chk_wq,
                      qf=qps_free[bi]):
                    t.wait_ge(VE.h, chk_lnT)
                    t.wait_ge(WD.h, chk_wq)
                    t.wait_ge(VE.h, qf)
                    for h2 in range(2):
                        sl = slice(h2 * KV, h2 * KV + KV)
                        for c in range(16):
                            t.matmul(qpsl[bi][:, sl],
                                     qlnfT[:, c * P:(c + 1) * P],
                                     wqhalf[:, c * 1024 + h2 * KV:
                                            c * 1024 + h2 * KV + KV],
                                     start=(c == 0), stop=False)
                        i = t.matmul(qpsl[bi][:, sl], onesb_row,
                                     bq_row[:, hf * 1024 + h2 * KV:
                                            hf * 1024 + h2 * KV + KV],
                                     start=False, stop=True)
                    TE.inc(i)
                chk_qmm = TE.n
                lnT_free = TE.n
                wq_used = TE.n

                @on_vec
                def _(v, bi=bi, chk_qmm=chk_qmm):
                    v.wait_ge(TE.h, chk_qmm)
                    VE.inc(v.tensor_copy(qstage[:, :], qpsl[bi][:, :]))
                chk_qev = VE.n
                qps_free[bi] = VE.n

                @on_scl
                def _(scl, bi=bi, chk_qev=chk_qev, qsqf=qsq_free_te):
                    scl.wait_ge(VE.h, chk_qev)
                    scl.wait_ge(TE.h, qsqf)
                    SE.inc(scl.activation(out=qsq[:, :], in_=qstage[:, 0:512],
                                          func=AF.Square))
                chk_qsq1 = SE.n

                @on_vec
                def _(v, bi=bi, chk_qsq1=chk_qsq1):
                    v.wait_ge(SE.h, chk_qsq1)
                    VE.inc(v.reduce_sum(
                        out=c_rn[bi][:, 0:4],
                        in_=qsq.ap().rearrange("p (h d) -> p h d", h=4), axis=AX))
                chk_rn1 = VE.n

                @on_scl
                def _(scl, bi=bi, chk_rn1=chk_rn1):
                    scl.wait_ge(VE.h, chk_rn1)
                    SE.inc(scl.activation(out=qsq[:, :], in_=qstage[:, 512:1024],
                                          func=AF.Square))
                chk_qsq2 = SE.n

                @on_vec
                def _(v, bi=bi, chk_qsq2=chk_qsq2):
                    v.wait_ge(SE.h, chk_qsq2)
                    VE.inc(v.reduce_sum(
                        out=c_rn[bi][:, 4:8],
                        in_=qsq.ap().rearrange("p (h d) -> p h d", h=4), axis=AX))
                chk_rn = VE.n

                @on_vec
                def _(v, bi=bi, chk_rn=chk_rn):
                    v.drain()
                    VE.inc(v.tensor_scalar(out=c_rn[bi][:, 0:8], in0=c_rn[bi][:, 0:8],
                                           scalar1=1e-12, scalar2=None, op0=ALU.max))
                chk_rnf = VE.n

                @on_scl
                def _(scl, bi=bi, chk_rnf=chk_rnf):
                    scl.wait_ge(VE.h, chk_rnf)
                    SE.inc(scl.activation(out=c_rn[bi][:, 8:16],
                                          in_=c_rn[bi][:, 0:8], func=AF.Sqrt))
                chk_rs = SE.n

                @on_vec
                def _(v, j=j, bi=bi, hf=hf, chk_rs=chk_rs):
                    v.wait_ge(SE.h, chk_rs)
                    v.reciprocal(qr1[:, 0:8], c_rn[bi][:, 8:16])
                    v.drain()
                    v.tensor_mul(qr1[:, 0:8], c_rn[bi][:, 0:8], qr1[:, 0:8])
                    v.drain()
                    v.tensor_add(c_rn[bi][:, 8:16], c_rn[bi][:, 8:16], qr1[:, 0:8])
                    v.drain()
                    v.tensor_scalar(out=c_rn[bi][:, 8:16], in0=c_rn[bi][:, 8:16],
                                    scalar1=0.5, scalar2=1e-6, op0=ALU.mult, op1=ALU.add)
                    v.drain()
                    v.reciprocal(c_rn[bi][:, 8:16], c_rn[bi][:, 8:16])
                    v.drain()
                    v.tensor_mul(c_rn[bi][:, 8:16], c_rn[bi][:, 8:16],
                                 alpha_bc[:, hf * 8:hf * 8 + 8])
                    v.drain()
                    q3 = qstage.ap().rearrange("p (h d) -> p h d", h=8)
                    v.tensor_tensor(
                        out=q3, in0=q3,
                        in1=c_rn[bi][:, 8:16, None].broadcast_to([P, 8, HD]),
                        op=ALU.mult)
                    v.drain()
                    x1 = q3[:, :, 0:64]
                    x2 = q3[:, :, 64:128]
                    cT = rqcos[:, j * 64:(j + 1) * 64][:, None, :].broadcast_to([P, 8, 64])
                    sT = rqsin[:, j * 64:(j + 1) * 64][:, None, :].broadcast_to([P, 8, 64])
                    rA3 = qr1.ap().rearrange("p (h d) -> p h d", h=8)[:, :, 0:64]
                    rB3 = qr2.ap().rearrange("p (h d) -> p h d", h=8)[:, :, 0:64]
                    rC3 = qsq.ap().rearrange("p (h d) -> p h d", h=8)[:, :, 0:64]
                    v.tensor_mul(rA3, x1, cT)
                    v.tensor_mul(rB3, x2, sT)
                    v.tensor_mul(rC3, x1, sT)
                    v.drain()
                    v.tensor_sub(x1, rA3, rB3)
                    v.drain()
                    v.tensor_mul(rA3, x2, cT)
                    v.drain()
                    v.tensor_add(x2, rC3, rA3)
                    v.drain()
                    VE.inc(v.tensor_copy(qbf[:, :], qstage[:, :]))
                chk_qbf = VE.n

                @on_ten
                def _(t, bi=bi, chk_qbf=chk_qbf, trf=trq_free):
                    t.wait_ge(VE.h, chk_qbf)
                    t.wait_ge(VE.h, trf)
                    for c in range(8):
                        i = t.transpose(trq[:, 1024 + c * P:1024 + (c + 1) * P],
                                        qbf[:, c * P:(c + 1) * P], ident[:, :])
                    TE.inc(i)
                chk_qtr = TE.n
                qsq_free_te = TE.n

                @on_vec
                def _(v, j=j, hf=hf, chk_qtr=chk_qtr):
                    v.wait_ge(TE.h, chk_qtr)
                    VE.inc(v.tensor_copy(
                        qrT.ap().rearrange("p (h t) -> p h t", h=NH)
                           [:, hf * 8:hf * 8 + 8, j * P:(j + 1) * P],
                        trq[:, 1024:2048].rearrange("p (h q) -> p h q", h=8)))
                trq_free = VE.n

    phase_barrier()

    scope("p3_attn")
    with nc.psum_tensor("S0", [P, 512], F32) as S0, \
         nc.psum_tensor("S1", [P, 512], F32) as S1, \
         nc.psum_tensor("S2", [P, 512], F32) as S2, \
         nc.psum_tensor("S3", [P, 512], F32) as S3, \
         nc.psum_tensor("PTp", [P, D], BF16) as PTp, \
         nc.psum_tensor("Yp", [P, 1024], F32) as Yp:
        Sb = [S0, S1, S2, S3]
        sbank_free = [0, 0, 0, 0]
        ring = 0
        Pb_free = 0
        PTp_free = 0
        Y_free = 0

        for h in range(NH):
            g = h // 4
            for j in range(NT):
                bi = j % 2
                E = EXT[j]
                banks = [(ring + e) % 4 for e in range(E)]
                ring = (ring + E) % 4

                @on_ten
                def _(t, h=h, j=j, g=g, E=E, banks=banks,
                      frees=[sbank_free[b] for b in banks]):
                    for b, f in zip(banks, frees):
                        t.wait_ge(SE.h, f)
                    for e in range(E):
                        i = t.matmul(
                            Sb[banks[e]][:, :],
                            qrT[:, h * TOK + j * P:h * TOK + (j + 1) * P],
                            krT[:, g * T + e * 512:g * T + (e + 1) * 512],
                            start=True, stop=True)
                    TE.inc(i)
                chk_smm = TE.n

                @on_vec
                def _(v, j=j, E=E, banks=banks, chk_smm=chk_smm):
                    v.wait_ge(TE.h, chk_smm)
                    VE.inc(v.tensor_tensor(out=Sb[banks[E - 1]][:, :],
                                           in0=Sb[banks[E - 1]][:, :],
                                           in1=masksb[:, j * 512:(j + 1) * 512],
                                           op=ALU.add))
                chk_mx = VE.n

                @on_scl
                def _(scl, h=h, bi=bi, E=E, banks=banks, chk_mx=chk_mx,
                      pbf=Pb_free):
                    scl.wait_ge(VE.h, chk_mx)
                    scl.wait_ge(TE.h, pbf)
                    for e in range(E):
                        i = scl.activation(out=Pbuf[:, e * 512:(e + 1) * 512],
                                           in_=Sb[banks[e]][:, :], func=AF.Exp,
                                           bias=negaa[:, h:h + 1],
                                           accum_out=laccc[bi][:, e:e + 1])
                    SE.inc(i)
                chk_exp = SE.n
                for b in banks:
                    sbank_free[b] = SE.n

                @on_vec
                def _(v, bi=bi, E=E, chk_exp=chk_exp):
                    v.wait_ge(SE.h, chk_exp)
                    v.reduce_sum(out=c_l[bi], in_=laccc[bi][:, 0:E], axis=AX)
                    v.drain()
                    v.reciprocal(c_rl[bi], c_l[bi])
                    v.drain()
                    VE.inc(v.tensor_scalar(out=Pbuf[:, 0:E * 512],
                                           in0=Pbuf[:, 0:E * 512],
                                           scalar1=c_rl[bi], scalar2=None,
                                           op0=ALU.mult))
                chk_ps = VE.n

                @on_ten
                def _(t, E=E, chk_ps=chk_ps, ptf=PTp_free):
                    t.wait_ge(VE.h, chk_ps)
                    t.wait_ge(VE.h, ptf)
                    for s in range(4 * E):
                        i = t.transpose(PTp[:, s * P:(s + 1) * P],
                                        Pbuf[:, s * P:(s + 1) * P], ident[:, :])
                    TE.inc(i)
                chk_pt = TE.n
                Pb_free = TE.n

                @on_vec
                def _(v, j=j, E=E, chk_pt=chk_pt):
                    v.wait_ge(TE.h, chk_pt)
                    for grp in range(E):
                        s0 = 4 * grp
                        w = PT_W[s0]
                        dst0 = PT_OFF[s0] + (j - AV_JMIN[s0]) * P
                        src = PTp[:, s0 * P:(s0 + 4) * P].rearrange(
                            "p (s q) -> p s q", s=4)
                        dst = PTbuf[:, dst0:dst0 + 4 * w].rearrange(
                            "p (s r) -> p s r", s=4)[:, :, 0:P]
                        i = v.tensor_copy(dst, src)
                    VE.inc(i)
                PTp_free = VE.n
            chk_ptall = VE.n

            @on_ten
            def _(t, h=h, g=g, chk_ptall=chk_ptall, yf=Y_free):
                t.wait_ge(VE.h, chk_ptall)
                t.wait_ge(VE.h, yf)
                first = [True, True]
                for s in range(16):
                    jm = AV_JMIN[s]
                    ranges = []
                    if jm * P < 512:
                        ranges.append((jm * P, 512))
                        ranges.append((512, 1024))
                    else:
                        ranges.append((jm * P, 1024))
                    for (a, b) in ranges:
                        bk = 0 if a < 512 else 1
                        st_flag = first[bk]
                        first[bk] = False
                        stop_flag = (s == 7 and bk == 0) or (s == 15 and bk == 1)
                        i = t.matmul(
                            Yp[:, a:b],
                            vbf[:, s * KV + g * P:s * KV + (g + 1) * P],
                            PTbuf[:, PT_OFF[s] + a - jm * P:
                                  PT_OFF[s] + b - jm * P],
                            start=st_flag, stop=stop_flag)
                TE.inc(i)
            chk_av = TE.n

            @on_vec
            def _(v, h=h, chk_av=chk_av):
                v.wait_ge(TE.h, chk_av)
                VE.inc(v.tensor_copy(yT[:, h * TOK:(h + 1) * TOK], Yp[:, :]))
            Y_free = VE.n

    phase_barrier()

    if _DEBUG:
        @on_gp
        def _(g):
            ST.dma(g, out=dbg_krT, in_=krT[:, :])
            ST.dma(g, out=dbg_v, in_=vbf[:, :])
            ST.dma(g, out=dbg_qrT, in_=qrT[:, :])
            ST.dma(g, out=dbg_yT, in_=yT[:, :])
            ST.wait(g)
        phase_barrier()

    scope("p4_oproj")
    with nc.psum_tensor("op0", [P, 512], F32) as op0q, \
         nc.psum_tensor("op1", [P, 512], F32) as op1q, \
         nc.psum_tensor("op2", [P, 512], F32) as op2q, \
         nc.psum_tensor("op3", [P, 512], F32) as op3q:
        opsl = [op0q, op1q, op2q, op3q]
        ops_free = [0, 0, 0, 0]
        wo_used = [0, 0]

        for qq in range(4):
            wbuf = qq % 2

            @on_gp
            def _(g, qq=qq, wbuf=wbuf, wu=wo_used[wbuf]):
                g.wait_ge(TE.h, wu)
                WD.dma(g, 
                    out=wobf2[wbuf].ap().rearrange("p (c f) -> p c f", f=512),
                    in_=IN.wo[:, qq * 512:(qq + 1) * 512]
                        .rearrange("(c p) f -> p c f", p=P))
            chk_wo = WD.n

            for i8 in range(NT):
                pb = i8 % 4

                @on_ten
                def _(t, qq=qq, i8=i8, pb=pb, wbuf=wbuf, chk_wo=chk_wo,
                      pf=ops_free[pb]):
                    t.wait_ge(WD.h, chk_wo)
                    t.wait_ge(VE.h, pf)
                    for c in range(16):
                        t.matmul(opsl[pb][:, :],
                                 yT[:, c * TOK + i8 * P:c * TOK + (i8 + 1) * P],
                                 wobf2[wbuf][:, c * 512:(c + 1) * 512],
                                 start=(c == 0), stop=False)
                    i = t.matmul(opsl[pb][:, :], onesb_row,
                                 bo_row[:, qq * 512:(qq + 1) * 512],
                                 start=False, stop=True)
                    TE.inc(i)
                chk_op = TE.n
                wo_used[wbuf] = TE.n

                @on_vec
                def _(v, qq=qq, i8=i8, pb=pb, chk_op=chk_op):
                    v.wait_ge(TE.h, chk_op)
                    sl = R[:, i8 * D + qq * 512:i8 * D + (qq + 1) * 512]
                    VE.inc(v.tensor_tensor(out=sl, in0=sl, in1=opsl[pb][:, :],
                                           op=ALU.add))
                ops_free[pb] = VE.n

    phase_barrier()

    scope("p5_ffn0")
    xs_cols = rows_f[:, 336:344]
    rxs_cols = rows_f[:, 344:352]
    ws_cols = rows_f[:, 352:356]
    rws_cols = rows_f[:, 356:360]
    wsrow = f0_wsr[0:1, 0:KV]
    fbrow = f0_wst[0:1, 0:D]

    with nc.psum_tensor("f0tr", [P, D], BF16) as f0tr, \
         nc.psum_tensor("f0bc", [P, KV], F32) as f0bc, \
         nc.psum_tensor("f0p0", [P, KV], F32) as f0p0, \
         nc.psum_tensor("f0p1", [P, KV], F32) as f0p1, \
         nc.psum_tensor("f0p2", [P, KV], F32) as f0p2, \
         nc.psum_tensor("f0p3", [P, KV], F32) as f0p3:
        f0ps = [f0p0, f0p1, f0p2, f0p3]
        f0ps_free = [0, 0, 0, 0]

        @on_sync
        def _(s):
            LD.dma(s, out=fbrow, in_=IN.fb)
        chk_fb = LD.n

        bc_ev = 0
        for qq in range(4):
            @on_ten
            def _(t, qq=qq, chk_fb=chk_fb, ev=bc_ev):
                t.wait_ge(LD.h, chk_fb)
                t.wait_ge(VE.h, ev)
                TE.inc(t.matmul(f0bc[:, :], onesf_row,
                                fbrow[:, qq * 512:(qq + 1) * 512],
                                start=True, stop=True))
            chk_b = TE.n

            @on_vec
            def _(v, qq=qq, chk_b=chk_b):
                v.wait_ge(TE.h, chk_b)
                VE.inc(v.tensor_copy(f0_bbc[:, qq * 512:(qq + 1) * 512], f0bc[:, :]))
            bc_ev = VE.n
        chk_bbc = VE.n

        ln_free_te = 0
        for j in range(NT):
            bi = j % 2
            @on_vec
            def _(v, j=j, bi=bi):
                VE.inc(v.reduce_sum(out=c_s1[bi], in_=R[:, j * D:(j + 1) * D], axis=AX))

            @on_scl
            def _(scl, j=j, bi=bi):
                SE.inc(scl.activation(out=qlnbf[:, :], in_=R[:, j * D:(j + 1) * D],
                                      func=AF.Square, accum_out=c_s2[bi]))
            chk_sq = SE.n

            @on_vec
            def _(v, bi=bi, chk_sq=chk_sq):
                v.wait_ge(SE.h, chk_sq)
                v.drain()
                v.tensor_scalar(out=c_m[bi], in0=c_s1[bi], scalar1=1.0 / D,
                                scalar2=None, op0=ALU.mult)
                v.tensor_scalar(out=c_var[bi], in0=c_s2[bi], scalar1=1.0 / D,
                                scalar2=1e-5, op0=ALU.mult, op1=ALU.add)
                v.drain()
                v.scalar_tensor_tensor(out=c_t0[bi], in0=c_m[bi], scalar=1.0,
                                       in1=c_m[bi], op0=ALU.mult, op1=ALU.mult)
                v.drain()
                VE.inc(v.tensor_sub(c_var[bi], c_var[bi], c_t0[bi]))
            chk_v2 = VE.n

            @on_scl
            def _(scl, bi=bi, chk_v2=chk_v2):
                scl.wait_ge(VE.h, chk_v2)
                SE.inc(scl.activation(out=c_rstd[bi], in_=c_var[bi], func=AF.Sqrt))
            chk_s2 = SE.n

            @on_vec
            def _(v, bi=bi, chk_s2=chk_s2):
                v.wait_ge(SE.h, chk_s2)
                v.drain()
                v.reciprocal(c_nt1[bi], c_rstd[bi])
                v.drain()
                v.tensor_mul(c_nt2[bi], c_var[bi], c_nt1[bi])
                v.drain()
                v.tensor_add(c_rstd[bi], c_rstd[bi], c_nt2[bi])
                v.drain()
                v.tensor_scalar(out=c_rstd[bi], in0=c_rstd[bi], scalar1=0.5,
                                scalar2=None, op0=ALU.mult)
                v.drain()
                v.reciprocal(c_rstd[bi], c_rstd[bi])
                v.drain()
                VE.inc(v.scalar_tensor_tensor(out=c_nmr[bi], in0=c_m[bi], scalar=-1.0,
                                              in1=c_rstd[bi], op0=ALU.mult,
                                              op1=ALU.mult))
            chk_v3 = VE.n

            @on_scl
            def _(scl, j=j, bi=bi, chk_v3=chk_v3, lf=ln_free_te):
                scl.wait_ge(VE.h, chk_v3)
                scl.wait_ge(TE.h, lf)
                SE.inc(scl.activation(out=f0_ln[:, :], in_=R[:, j * D:(j + 1) * D],
                                      func=AF.Identity, bias=c_nmr[bi],
                                      scale=c_rstd[bi]))
            chk_ln = SE.n

            @on_vec
            def _(v, j=j, bi=bi, chk_ln=chk_ln, xf=ln_free_te):
                v.wait_ge(SE.h, chk_ln)
                v.wait_ge(TE.h, xf)
                v.reduce_max(out=xs_cols[:, j:j + 1], in_=f0_ln[:, :], axis=AX,
                             apply_absolute_value=True)
                v.drain()
                v.tensor_scalar(out=xs_cols[:, j:j + 1], in0=xs_cols[:, j:j + 1],
                                scalar1=1.0 / 127.0, scalar2=1e-12,
                                op0=ALU.mult, op1=ALU.max)
                v.drain()
                v.reciprocal(rxs_cols[:, j:j + 1], xs_cols[:, j:j + 1])
                v.drain()
                v.tensor_scalar(out=f0_ln[:, :], in0=f0_ln[:, :],
                                scalar1=rxs_cols[:, j:j + 1], scalar2=MAGIC,
                                op0=ALU.mult, op1=ALU.add)
                v.drain()
                VE.inc(v.tensor_scalar(out=f0_xq[:, :], in0=f0_ln[:, :],
                                       scalar1=-MAGIC, scalar2=None, op0=ALU.add))
            chk_xq = VE.n

            @on_ten
            def _(t, j=j, chk_xq=chk_xq):
                t.wait_ge(VE.h, chk_xq)
                for c in range(16):
                    i = t.transpose(f0tr[:, c * P:(c + 1) * P],
                                    f0_xq[:, c * P:(c + 1) * P], ident[:, :])
                TE.inc(i)
            chk_xtr = TE.n
            ln_free_te = TE.n

            @on_vec
            def _(v, j=j, chk_xtr=chk_xtr):
                v.wait_ge(TE.h, chk_xtr)
                VE.inc(v.tensor_copy(
                    f0_xqT.ap().rearrange("p (c t) -> p c t", c=16)[:, :, j * P:(j + 1) * P],
                    f0tr.ap().rearrange("p (c q) -> p c q", c=16)))

        for j in range(NT):
            @on_vec
            def _(v, j=j, chk_bbc=chk_bbc):
                v.wait_ge(VE.h, chk_bbc)
                VE.inc(v.tensor_tensor(out=R[:, j * D:(j + 1) * D],
                                       in0=R[:, j * D:(j + 1) * D],
                                       in1=f0_bbc[:, :], op=ALU.add))

        wst_free = 0
        fb_done_te = TE.n
        wqn_free = 0
        for ot in range(4):
            @on_gp
            def _(g, ot=ot, wf=wst_free, fbd=fb_done_te):
                g.wait_ge(VE.h, wf)
                g.wait_ge(TE.h, fbd)
                WD.dma(g, 
                    out=f0_wst.ap().rearrange("p (c d) -> p c d", c=4),
                    in_=IN.fw[ot * 512:(ot + 1) * 512, :]
                        .rearrange("(c p) d -> p c d", p=P))
            chk_w = WD.n

            @on_vec
            def _(v, ot=ot, chk_w=chk_w, qf=wqn_free):
                v.wait_ge(WD.h, chk_w)
                v.wait_ge(TE.h, qf)
                for c in range(4):
                    v.reduce_max(out=ws_cols[:, c:c + 1],
                                 in_=f0_wst[:, c * D:(c + 1) * D], axis=AX,
                                 apply_absolute_value=True)
                v.drain()
                v.tensor_scalar(out=ws_cols[:, :], in0=ws_cols[:, :],
                                scalar1=1e-8, scalar2=1.0 / 127.0,
                                op0=ALU.max, op1=ALU.mult)
                v.drain()
                v.reciprocal(rws_cols[:, :], ws_cols[:, :])
                v.drain()
                for c in range(4):
                    v.tensor_scalar(out=f0_wst[:, c * D:(c + 1) * D],
                                    in0=f0_wst[:, c * D:(c + 1) * D],
                                    scalar1=rws_cols[:, c:c + 1], scalar2=MAGIC,
                                    op0=ALU.mult, op1=ALU.add)
                v.drain()
                for c in range(4):
                    i = v.tensor_scalar(out=f0_wqn[:, c * D:(c + 1) * D],
                                        in0=f0_wst[:, c * D:(c + 1) * D],
                                        scalar1=-MAGIC, scalar2=None, op0=ALU.add)
                VE.inc(i)
            chk_wq2 = VE.n
            wst_free = VE.n

            @on_gp
            def _(g, ot=ot, chk_wq2=chk_wq2):
                g.wait_ge(VE.h, chk_wq2)
                WD.dma(g, out=wsband, in_=ws_cols)
                g.wait_ge(WD.h, WD.n + 0)
                WD.dma(g, 
                    out=wsrow.rearrange("o (c p) -> o c p", c=4),
                    in_=wsband.rearrange("p c -> c p").unsqueeze(0),
                    allow_slow_non_contiguous=True)
            chk_wsr = WD.n

            @on_ten
            def _(t, chk_wsr=chk_wsr):
                t.wait_ge(WD.h, chk_wsr)
                t.wait_ge(VE.h, VE.n)
                TE.inc(t.matmul(f0bc[:, :], onesf_row, wsrow, start=True, stop=True))
            chk_wsbc = TE.n

            @on_vec
            def _(v, chk_wsbc=chk_wsbc):
                v.wait_ge(TE.h, chk_wsbc)
                VE.inc(v.tensor_copy(f0_wsbc[:, :], f0bc[:, :]))
            chk_wsbc2 = VE.n

            f0tr_free = VE.n
            for oc in range(4):
                @on_ten
                def _(t, ot=ot, oc=oc, chk_wq2=chk_wq2, trf=f0tr_free):
                    t.wait_ge(VE.h, chk_wq2)
                    t.wait_ge(VE.h, trf)
                    for dc in range(16):
                        i = t.transpose(
                            f0tr[:, dc * P:(dc + 1) * P],
                            f0_wqn[:, oc * D + dc * P:oc * D + (dc + 1) * P],
                            ident[:, :])
                    TE.inc(i)
                chk_wtr = TE.n

                @on_vec
                def _(v, oc=oc, chk_wtr=chk_wtr):
                    v.wait_ge(TE.h, chk_wtr)
                    VE.inc(v.tensor_copy(
                        f0_wqT.ap().rearrange("p (d o) -> p d o", d=16)[:, :, oc * P:(oc + 1) * P],
                        f0tr.ap().rearrange("p (d q) -> p d q", d=16)))
                f0tr_free = VE.n
            chk_wqT = VE.n
            wqn_free = TE.n

            for i8 in range(NT):
                pb = i8 % 4

                @on_ten
                def _(t, i8=i8, pb=pb, chk_wqT=chk_wqT, pf=f0ps_free[pb]):
                    t.wait_ge(VE.h, chk_wqT)
                    t.wait_ge(VE.h, pf)
                    for c in range(16):
                        i = t.matmul(f0ps[pb][:, :],
                                     f0_xqT[:, c * 1024 + i8 * P:c * 1024 + (i8 + 1) * P],
                                     f0_wqT[:, c * 512:(c + 1) * 512],
                                     start=(c == 0), stop=(c == 15))
                    TE.inc(i)
                chk_mm = TE.n

                @on_vec
                def _(v, ot=ot, i8=i8, pb=pb, chk_mm=chk_mm, chk_wsbc2=chk_wsbc2):
                    v.wait_ge(TE.h, chk_mm)
                    v.wait_ge(VE.h, chk_wsbc2)
                    v.drain()
                    v.scalar_tensor_tensor(out=f0_tmp[:, :], in0=f0ps[pb][:, :],
                                           scalar=xs_cols[:, i8:i8 + 1],
                                           in1=f0_wsbc[:, :],
                                           op0=ALU.mult, op1=ALU.mult)
                    v.drain()
                    sl = R[:, i8 * D + ot * 512:i8 * D + (ot + 1) * 512]
                    VE.inc(v.tensor_tensor(out=sl, in0=sl, in1=f0_tmp[:, :],
                                           op=ALU.add))
                f0ps_free[pb] = VE.n

    if _DEBUG:
        @on_gp
        def _(g):
            g.wait_ge(VE.h, VE.n); g.wait_ge(TE.h, TE.n); g.wait_ge(SE.h, SE.n)
            ST.dma(g, out=dbg_xqT, in_=f0_xqT[:, :])
            ST.dma(g, out=dbg_wqT, in_=f0_wqT[:, :])
            ST.dma(g, out=dbg_rows, in_=rows_f[:, :])
            ST.wait(g)

    phase_barrier()

    scope("p6_lattn")
    a1_lnT = sb.at(OFF_OVL + 0, "a1_lnT", (P, 16 * 512), BF16)
    a1_e = sb.at(OFF_OVL + 0, "a1_e", (P, 16 * 512), BF16)
    a1_wo = sb.at(OFF_OVL + 0, "a1_wo", (P, 16 * 512), BF16)
    a1_q = sb.at(OFF_OVL + 16 * KB, "a1_q", (P, 16 * 512), F32)
    a1_k = sb.at(OFF_OVL + 48 * KB, "a1_k", (P, 16 * 512), F32)
    a1_o = sb.at(OFF_OVL + 48 * KB, "a1_o", (P, 16 * 512), BF16)
    a1_v = sb.at(OFF_OVL + 80 * KB, "a1_v", (P, 16 * 512), F32)
    a1_wst = [sb.at(OFF_MASK + i * 4 * KB, f"a1_wst{i}", (P, 16 * P), BF16)
              for i in range(2)]
    a1_meta = sb.at(OFF_OVL + 112 * KB, "a1_meta", (P, 512), F32)
    a1_hrow = sb.at(OFF_OVL + 0, "a1_hrow", (16, 16 * P), F32)
    a1_hsel = sb.at(OFF_OVL + 116 * KB, "a1_hsel", (P, 256), BF16)
    a1_pub = sb.at(OFF_OVL + 114 * KB, "a1_pub", (P, 256), F32)
    a1_sta = sb.at(OFF_ROWSB, "a1_sta", (P, 512), F32)
    a1_ini = sb.at(OFF_OVL + 115 * KB, "a1_ini", (P, 256), F32)
    a1_t1 = sb.at(OFF_JUNK, "a1_t1", (P, 512), F32)
    a1_t2 = sb.at(OFF_JUNK + 2 * KB, "a1_t2", (P, 512), F32)
    a1_scr = sb.at(OFF_JUNK, "a1_scr", (P, 1024), F32)
    dcol = alpha_bc
    lnd_row = rows_f[0:1, 400:416]
    gxrow = rows_f[0:1, 144:400]

    hsel_np = np.zeros((P, 256), dtype=ml_dtypes.bfloat16)
    for _h in range(16):
        hsel_np[:, _h * 16 + _h] = 1.0
    hrow_np = np.zeros((16, 16 * P), dtype=np.float32)
    for _h in range(16):
        hrow_np[_h, _h * P:(_h + 1) * P] = 1.0
    hsel_dram = nc.inline_tensor(hsel_np, name="hselc").ap()
    hrow_dram = nc.inline_tensor(hrow_np, name="hrowc").ap()

    a1_spill = nc.dram_tensor("a1_spill", [3 * 2048, 512], F32).ap()
    cc_in = nc.dram_tensor("cc_in", [P, 256], F32).ap()
    cc_out = nc.dram_tensor("cc_out", [2 * P, 256], F32).ap()

    with nc.psum_tensor("a1tr", [P, D], BF16) as a1tr, \
         nc.psum_tensor("a1p0", [P, 512], F32) as a1p0, \
         nc.psum_tensor("a1p1", [P, 512], F32) as a1p1, \
         nc.psum_tensor("a1p2", [P, 512], F32) as a1p2, \
         nc.psum_tensor("a1p3", [P, 512], F32) as a1p3, \
         nc.psum_tensor("a1dn", [P, 512], F32) as a1dn, \
         nc.psum_tensor("a1bc", [P, 512], F32) as a1bc:
        a1ps = [a1p0, a1p1, a1p2, a1p3]
        a1ps_free = [0, 0, 0, 0]

        @on_sync
        def _(s):
            LD.dma(s, out=rows_f[0:1, 432:448], in_=IN.decay)
        chk_dp = LD.n

        @on_scl
        def _(scl, chk_dp=chk_dp):
            scl.wait_ge(LD.h, chk_dp)
            SE.inc(scl.activation(out=rows_f[0:1, 448:464], in_=rows_f[0:1, 432:448],
                                  func=AF.Sigmoid))
        chk_sg = SE.n

        @on_vec
        def _(v, chk_sg=chk_sg):
            v.wait_ge(SE.h, chk_sg)
            VE.inc(v.tensor_scalar(out=rows_f[0:1, 464:480], in0=rows_f[0:1, 448:464],
                                   scalar1=1e-30, scalar2=None, op0=ALU.max))
        chk_dmx = VE.n

        @on_scl
        def _(scl, chk_dmx=chk_dmx):
            scl.wait_ge(VE.h, chk_dmx)
            SE.inc(scl.activation(out=lnd_row, in_=rows_f[0:1, 464:480], func=AF.Ln))
        chk_lnd = SE.n

        @on_ten
        def _(t, chk_sg=chk_sg):
            t.wait_ge(SE.h, chk_sg)
            t.wait_ge(VE.h, VE.n)
            TE.inc(t.matmul(a1dn[:, 0:16], onesf_row, rows_f[0:1, 448:464],
                            start=True, stop=True))
        chk_dc = TE.n

        @on_vec
        def _(v, chk_dc=chk_dc):
            v.wait_ge(TE.h, chk_dc)
            VE.inc(v.tensor_copy(dcol[:, :], a1dn[:, 0:16]))
        chk_dcol = VE.n

        wu = [0, 0]
        spill_reload_chk = 0
        for half in range(2):
            trf = VE.n
            lnf = TE.n
            for jj in range(4):
                j = half * 4 + jj
                bi = j % 2
                @on_vec
                def _(v, j=j, bi=bi, srk=spill_reload_chk):
                    if j % 4 == 0:
                        v.wait_ge(ST.h, srk)
                    VE.inc(v.reduce_sum(out=c_s1[bi], in_=R[:, j * D:(j + 1) * D],
                                        axis=AX))

                @on_scl
                def _(scl, j=j, bi=bi, lf=lnf, hb=VE.n):
                    scl.wait_ge(TE.h, lf)
                    scl.wait_ge(VE.h, hb)
                    SE.inc(scl.activation(out=qlnbf[:, :], in_=R[:, j * D:(j + 1) * D],
                                          func=AF.Square, accum_out=c_s2[bi]))
                chk_sq = SE.n

                @on_vec
                def _(v, bi=bi, chk_sq=chk_sq):
                    v.wait_ge(SE.h, chk_sq)
                    v.drain()
                    v.tensor_scalar(out=c_m[bi], in0=c_s1[bi], scalar1=1.0 / D,
                                    scalar2=None, op0=ALU.mult)
                    v.tensor_scalar(out=c_var[bi], in0=c_s2[bi], scalar1=1.0 / D,
                                    scalar2=1e-5, op0=ALU.mult, op1=ALU.add)
                    v.drain()
                    v.scalar_tensor_tensor(out=c_t0[bi], in0=c_m[bi], scalar=1.0,
                                           in1=c_m[bi], op0=ALU.mult, op1=ALU.mult)
                    v.drain()
                    VE.inc(v.tensor_sub(c_var[bi], c_var[bi], c_t0[bi]))
                chk_v2 = VE.n

                @on_scl
                def _(scl, bi=bi, chk_v2=chk_v2):
                    scl.wait_ge(VE.h, chk_v2)
                    SE.inc(scl.activation(out=c_rstd[bi], in_=c_var[bi], func=AF.Sqrt))
                chk_s2 = SE.n

                @on_vec
                def _(v, bi=bi, chk_s2=chk_s2):
                    v.wait_ge(SE.h, chk_s2)
                    v.drain()
                    v.reciprocal(c_nt1[bi], c_rstd[bi])
                    v.drain()
                    v.tensor_mul(c_nt2[bi], c_var[bi], c_nt1[bi])
                    v.drain()
                    v.tensor_add(c_rstd[bi], c_rstd[bi], c_nt2[bi])
                    v.drain()
                    v.tensor_scalar(out=c_rstd[bi], in0=c_rstd[bi], scalar1=0.5,
                                    scalar2=None, op0=ALU.mult)
                    v.drain()
                    v.reciprocal(c_rstd[bi], c_rstd[bi])
                    v.drain()
                    VE.inc(v.scalar_tensor_tensor(out=c_nmr[bi], in0=c_m[bi],
                                                  scalar=-1.0, in1=c_rstd[bi],
                                                  op0=ALU.mult, op1=ALU.mult))
                chk_v3 = VE.n

                @on_scl
                def _(scl, j=j, bi=bi, chk_v3=chk_v3, lf=lnf):
                    scl.wait_ge(VE.h, chk_v3)
                    scl.wait_ge(TE.h, lf)
                    SE.inc(scl.activation(out=qlnbf[:, :],
                                          in_=R[:, j * D:(j + 1) * D],
                                          func=AF.Identity, bias=c_nmr[bi],
                                          scale=c_rstd[bi]))
                chk_ln = SE.n

                @on_ten
                def _(t, chk_ln=chk_ln, trf=trf):
                    t.wait_ge(SE.h, chk_ln)
                    t.wait_ge(VE.h, trf)
                    for c in range(16):
                        i = t.transpose(a1tr[:, c * P:(c + 1) * P],
                                        qlnbf[:, c * P:(c + 1) * P], ident[:, :])
                    TE.inc(i)
                chk_tr = TE.n
                lnf = TE.n

                @on_vec
                def _(v, jj=jj, chk_tr=chk_tr):
                    v.wait_ge(TE.h, chk_tr)
                    VE.inc(v.tensor_copy(
                        a1_lnT.ap().rearrange("p (c t) -> p c t", c=16)
                             [:, :, jj * P:(jj + 1) * P],
                        a1tr.ap().rearrange("p (c q) -> p c q", c=16)))
                trf = VE.n
            chk_lnTall = VE.n

            for fc in range(48):
                wb = fc % 2
                pb = fc % 4

                @on_gp
                def _(g, half=half, fc=fc, wb=wb, wuv=wu[wb], srk=spill_reload_chk):
                    g.wait_ge(TE.h, wuv)
                    g.wait_ge(ST.h, srk)
                    WDs[wb].dma(g, 
                        out=a1_wst[wb].ap().rearrange("p (c f) -> p c f", f=P),
                        in_=IN.wqkv[fc * D:(fc + 1) * D, :]
                            .rearrange("(c p) f -> p c f", p=P))
                chk_wl = WDs[wb].n

                @on_ten
                def _(t, fc=fc, wb=wb, pb=pb, chk_wl=chk_wl, chk_lnTall=chk_lnTall,
                      pf=a1ps_free[pb]):
                    t.wait_ge(WDs[wb].h, chk_wl)
                    t.wait_ge(VE.h, chk_lnTall)
                    t.wait_ge(VE.h, pf)
                    for c in range(16):
                        i = t.matmul(a1ps[pb][:, :],
                                     a1_wst[wb][:, c * P:(c + 1) * P],
                                     a1_lnT[:, c * 512:(c + 1) * 512],
                                     start=(c == 0), stop=(c == 15))
                    TE.inc(i)
                chk_mm = TE.n
                wu[wb] = TE.n

                if fc < 32:
                    dstbuf = a1_q if fc < 16 else a1_k
                    dc = fc % 16

                    @on_vec
                    def _(v, pb=pb, chk_mm=chk_mm):
                        v.wait_ge(TE.h, chk_mm)
                        v.drain()
                        v.tensor_scalar(out=a1_t1[:, :], in0=a1ps[pb][:, :],
                                        scalar1=0.0, scalar2=None, op0=ALU.max)
                        v.drain()
                        VE.inc(v.tensor_sub(a1_t2[:, :], a1ps[pb][:, :], a1_t1[:, :]))
                    chk_p1 = VE.n

                    @on_scl
                    def _(scl, dc=dc, dstbuf=dstbuf, chk_p1=chk_p1):
                        scl.wait_ge(VE.h, chk_p1)
                        SE.inc(scl.activation(out=dstbuf[:, dc * 512:(dc + 1) * 512],
                                              in_=a1_t2[:, :], func=AF.Exp))
                    chk_p2 = SE.n

                    @on_vec
                    def _(v, dc=dc, dstbuf=dstbuf, chk_p2=chk_p2):
                        v.wait_ge(SE.h, chk_p2)
                        sl = dstbuf[:, dc * 512:(dc + 1) * 512]
                        VE.inc(v.tensor_tensor(out=sl, in0=sl, in1=a1_t1[:, :],
                                               op=ALU.add))
                    a1ps_free[pb] = VE.n
                else:
                    dc = fc - 32

                    @on_vec
                    def _(v, pb=pb, dc=dc, chk_mm=chk_mm):
                        v.wait_ge(TE.h, chk_mm)
                        VE.inc(v.tensor_copy(a1_v[:, dc * 512:(dc + 1) * 512],
                                             a1ps[pb][:, :]))
                    a1ps_free[pb] = VE.n
            chk_qkv = VE.n

            for dc in range(16):
                @on_vec
                def _(v, dc=dc, chk_qkv=chk_qkv):
                    if dc == 0:
                        v.wait_ge(VE.h, chk_qkv)
                        v.drain()
                    sl = a1_v[:, dc * 512:(dc + 1) * 512]
                    VE.inc(v.tensor_tensor(out=sl,
                                           in0=a1_k[:, dc * 512:(dc + 1) * 512],
                                           in1=sl, op=ALU.mult))
            chk_kv = VE.n

            @on_vec
            def _(v, half=half, chk_kv=chk_kv, chk_dcol=chk_dcol):
                v.wait_ge(VE.h, chk_kv)
                v.wait_ge(VE.h, chk_dcol)
                jf = a1_scr.ap()
                for s, buf in ((0, a1_v), (1, a1_k)):
                    for cc in range(4):
                        gc = half * 4 + cc
                        for hg in range(2):
                            v.drain()
                            for hh in range(8):
                                h = hg * 8 + hh
                                v.tensor_tensor_scan(
                                    out=jf[:, hh * P:(hh + 1) * P],
                                    data0=dcol[:, h:h + 1].broadcast_to([P, P]),
                                    data1=buf[:, h * 512 + cc * P:
                                              h * 512 + (cc + 1) * P],
                                    initial=0.0, op0=ALU.mult, op1=ALU.add)
                            v.drain()
                            src8 = jf.rearrange("p (a b) -> p a b", a=8)[:, :, P - 1:P]
                            dst8 = a1_pub.ap().rearrange(
                                "p (s h c) -> p s h c", s=2, h=16)[
                                :, s:s + 1, hg * 8:hg * 8 + 8, gc:gc + 1]
                            v.tensor_copy(dst8, src8.unsqueeze(1))
                i = v.tensor_scalar(out=a1_pub[:, 0:1], in0=a1_pub[:, 0:1],
                                    scalar1=1.0, scalar2=None, op0=ALU.mult)
                v.drain()
                VE.inc(i)
            chk_pub = VE.n

            if half == 0:
                @on_sync
                def _(s, chk=chk_pub):
                    s.wait_ge(VE.h, chk)
                    ST.dma(s, out=a1_spill[0:2048, :], in_=a1_q[:, :])
                    ST.dma(s, out=a1_spill[2048:4096, :], in_=a1_k[:, :])
                    ST.dma(s, out=a1_spill[4096:6144, :], in_=a1_v[:, :])
                spill_reload_chk = ST.n

        chk_pub_all = VE.n

        @on_gp
        def _(g, chk=chk_pub_all):
            g.wait_ge(VE.h, chk)
            WD.dma(g, out=cc_in, in_=a1_pub[:, :])
            g.wait_ge(WD.h, WD.n)
            g.collective_compute(
                "AllGather", ALU.bypass,
                replica_groups=[[0, 1], [2, 3], [4, 5], [6, 7]],
                ins=[cc_in], outs=[cc_out],
            ).then_inc(CC.h, 1)
            CC.n += 1
            g.wait_ge(CC.h, CC.n)
            WD.dma(g, out=a1_sta[:, 0:256], in_=cc_out[0:P, :])
            WD.dma(g, out=a1_sta[:, 256:512], in_=cc_out[P:2 * P, :])
        chk_sta = WD.n

        meta_chk = 0
        for j in range(NT):
            @on_sync
            def _(s, j=j, mk=meta_chk):
                s.wait_ge(TE.h, wu[0])
                s.wait_ge(TE.h, wu[1])
                s.wait_ge(VE.h, mk)
                LD.dma(s, out=a1_meta[0:1, :],
                                   in_=IN.gapexp[j:j + 1, :])
            chk_meta = LD.n

            @on_vec
            def _(v, j=j, chk_meta=chk_meta, chk_lnd=chk_lnd):
                v.wait_ge(LD.h, chk_meta)
                v.wait_ge(SE.h, chk_lnd)
                lr = lnd_row[:, None, :, None].broadcast_to([1, 2, 16, 8])
                v.tensor_tensor(
                    out=gxrow.rearrange("o (r h c) -> o r h c", r=2, h=16),
                    in0=a1_meta[0:1, 0:256].rearrange("o (r h c) -> o r h c", r=2, h=16),
                    in1=lr, op=ALU.mult)
                v.drain()
                VE.inc(v.tensor_tensor(out=gxrow, in0=gxrow,
                                       in1=a1_meta[0:1, 256:512], op=ALU.add))
            chk_gx = VE.n

            @on_scl
            def _(scl, chk_gx=chk_gx):
                scl.wait_ge(VE.h, chk_gx)
                SE.inc(scl.activation(out=gxrow, in_=gxrow, func=AF.Exp))
            chk_cf = SE.n

            @on_ten
            def _(t, chk_cf=chk_cf, ev=VE.n):
                t.wait_ge(SE.h, chk_cf)
                t.wait_ge(VE.h, ev)
                TE.inc(t.matmul(a1bc[:, 0:256], onesf_row, gxrow,
                                start=True, stop=True))
            chk_cbc = TE.n

            @on_vec
            def _(v, j=j, chk_cbc=chk_cbc, chk_sta=chk_sta):
                v.wait_ge(TE.h, chk_cbc)
                v.wait_ge(WD.h, chk_sta)
                for s in range(2):
                    sl = a1_sta.ap().rearrange("p (r s h c) -> p r s h c", r=2, s=2,
                                               h=16)[:, :, s, :, :]
                    v.tensor_tensor(
                        out=a1_scr[:, 0:256].rearrange("p (r h c) -> p r h c",
                                                       r=2, h=16),
                        in0=sl, in1=a1bc[:, 0:256].rearrange(
                            "p (r h c) -> p r h c", r=2, h=16),
                        op=ALU.mult)
                    v.drain()
                    v.reduce_sum(out=a1_scr[:, 256:288],
                                 in_=a1_scr[:, 0:256].rearrange(
                                     "p (g c) -> p g c", g=32), axis=AX)
                    v.drain()
                    v.tensor_add(a1_ini[:, j * 32 + s * 16:j * 32 + (s + 1) * 16],
                                 a1_scr[:, 256:272], a1_scr[:, 272:288])
                    v.drain()
                i = v.tensor_scalar(out=a1_ini[:, j * 32:j * 32 + 1],
                                    in0=a1_ini[:, j * 32:j * 32 + 1],
                                    scalar1=1.0, scalar2=None, op0=ALU.mult)
                VE.inc(i)
            meta_chk = VE.n
        chk_ini = VE.n

        oproj_wu = 0
        for half in (1, 0):
            @on_sync
            def _(s, te=TE.n, ve=VE.n):
                s.wait_ge(TE.h, te)
                s.wait_ge(VE.h, ve)
                LD.dma(s, out=a1_hsel[:, :], in_=hsel_dram)
            chk_hs = LD.n
            if half == 0:
                @on_sync
                def _(s, ou=oproj_wu, vv=VE.n, ss=SE.n):
                    s.wait_ge(TE.h, ou)
                    s.wait_ge(VE.h, vv)
                    s.wait_ge(SE.h, ss)
                    LD.dma(s, out=a1_q[:, :], in_=a1_spill[0:2048, :])
                    LD.dma(s, out=a1_k[:, :], in_=a1_spill[2048:4096, :])
                    LD.dma(s, out=a1_v[:, :], in_=a1_spill[4096:6144, :])
                chk_rel = LD.n
            else:
                chk_rel = LD.n

            @on_vec
            def _(v, half=half, chk_rel=chk_rel, chk_ini=chk_ini):
                v.wait_ge(LD.h, chk_rel)
                v.wait_ge(VE.h, chk_ini)
                for s, buf in ((0, a1_v), (1, a1_k)):
                    for cc in range(4):
                        j = half * 4 + cc
                        for h in range(16):
                            sl = buf[:, h * 512 + cc * P:h * 512 + (cc + 1) * P]
                            v.tensor_tensor_scan(
                                out=sl,
                                data0=dcol[:, h:h + 1].broadcast_to([P, P]),
                                data1=sl,
                                initial=a1_ini[:, j * 32 + s * 16 + h:
                                               j * 32 + s * 16 + h + 1],
                                op0=ALU.mult, op1=ALU.add)
                v.drain()
                i = v.tensor_scalar(out=c_t0[0], in0=c_t0[0],
                                    scalar1=1.0, scalar2=None, op0=ALU.mult)
                VE.inc(i)
            chk_ts = VE.n

            for dc in range(16):
                @on_vec
                def _(v, dc=dc, chk_ts=chk_ts, ou=oproj_wu):
                    if dc == 0:
                        v.wait_ge(VE.h, chk_ts)
                        v.wait_ge(TE.h, ou)
                    VE.inc(v.tensor_tensor(out=a1_e[:, dc * 512:(dc + 1) * 512],
                                           in0=a1_q[:, dc * 512:(dc + 1) * 512],
                                           in1=a1_k[:, dc * 512:(dc + 1) * 512],
                                           op=ALU.mult))
            chk_e = VE.n

            @on_ten
            def _(t, chk_e=chk_e, chk_hs=chk_hs, dnf=VE.n):
                t.wait_ge(VE.h, chk_e)
                t.wait_ge(LD.h, chk_hs)
                for h in range(16):
                    i = t.matmul(a1dn[0:16, :], a1_hsel[:, h * 16:(h + 1) * 16],
                                 a1_e[:, h * 512:(h + 1) * 512],
                                 start=(h == 0), stop=(h == 15))
                TE.inc(i)
            chk_dn = TE.n

            @on_sync
            def _(s, chk_dn=chk_dn):
                s.wait_ge(TE.h, chk_dn)
                LD.dma(s, out=a1_hrow[:, :], in_=hrow_dram)
            chk_hr = LD.n

            @on_vec
            def _(v, chk_dn=chk_dn):
                v.wait_ge(TE.h, chk_dn)
                v.tensor_copy(a1_t2[0:16, :], a1dn[0:16, :])
                v.drain()
                v.tensor_scalar(out=a1_t2[0:16, :], in0=a1_t2[0:16, :],
                                scalar1=1e-6, scalar2=None, op0=ALU.max)
                v.drain()
                VE.inc(v.reciprocal(a1_t2[0:16, :], a1_t2[0:16, :]))
            chk_rd = VE.n

            for h in range(16):
                @on_ten
                def _(t, h=h, chk_rd=chk_rd, bf=VE.n, chk_hr=chk_hr):
                    if h == 0:
                        t.wait_ge(LD.h, chk_hr)
                    t.wait_ge(VE.h, chk_rd)
                    t.wait_ge(VE.h, bf)
                    TE.inc(t.matmul(a1bc[:, :], a1_hrow[:, h * P:(h + 1) * P],
                                    a1_t2[0:16, :], start=True, stop=True))
                chk_bc = TE.n

                @on_vec
                def _(v, h=h, chk_bc=chk_bc):
                    v.wait_ge(TE.h, chk_bc)
                    sl = a1_q[:, h * 512:(h + 1) * 512]
                    v.tensor_tensor(out=sl, in0=sl,
                                    in1=a1_v[:, h * 512:(h + 1) * 512], op=ALU.mult)
                    v.drain()
                    v.tensor_tensor(out=sl, in0=sl, in1=a1bc[:, :], op=ALU.mult)
                    v.drain()
                    VE.inc(v.tensor_copy(a1_o[:, h * 512:(h + 1) * 512], sl))
            chk_out = VE.n

            wo_used = TE.n
            for oq in range(4):
                @on_gp
                def _(g, oq=oq, wo_used=wo_used, chk_out=chk_out):
                    g.wait_ge(TE.h, wo_used)
                    g.wait_ge(VE.h, chk_out)
                    WDs[oq % 2].dma(g, 
                        out=a1_wo.ap().rearrange("p (c f) -> p c f", f=512),
                        in_=IN.wout[oq * D:(oq + 1) * D, :]
                            .rearrange("(c p) f -> p c f", p=P))
                chk_wo = WDs[oq % 2].n

                for it in range(4):
                    i8 = half * 4 + it
                    pb = it % 4

                    @on_ten
                    def _(t, oq=oq, it=it, pb=pb, oq2=oq % 2, chk_wo=chk_wo,
                          chk_out=chk_out, pf=a1ps_free[pb]):
                        t.wait_ge(WDs[oq2].h, chk_wo)
                        t.wait_ge(VE.h, chk_out)
                        t.wait_ge(VE.h, pf)
                        for c in range(16):
                            i = t.matmul(a1ps[pb][:, :],
                                         a1_o[:, c * 512 + it * P:c * 512 + (it + 1) * P],
                                         a1_wo[:, c * 512:(c + 1) * 512],
                                         start=(c == 0), stop=(c == 15))
                        TE.inc(i)
                    chk_om = TE.n
                    wo_used = TE.n

                    @on_vec
                    def _(v, i8=i8, oq=oq, pb=pb, chk_om=chk_om):
                        v.wait_ge(TE.h, chk_om)
                        sl = R[:, i8 * D + oq * 512:i8 * D + (oq + 1) * 512]
                        VE.inc(v.tensor_tensor(out=sl, in0=sl, in1=a1ps[pb][:, :],
                                               op=ALU.add))
                    a1ps_free[pb] = VE.n
            oproj_wu = TE.n

    phase_barrier()

    scope("p7_ffn1")
    r_spill = nc.dram_tensor("r_spill", [TOK, D], F32).ap()
    SQ2 = float(np.sqrt(2.0))
    SC_DN = float(64.0 * np.sqrt(2.0))

    x1stage = sb.at(0, "x1stage", (P, 16 * 512), F32)
    h7 = [sb.at(0, "h7a", (P, 64 * 512), BF16),
          sb.at(147296, "h7b", (P, 64 * 512), BF16)]
    ln3 = sb.at(OFF_MASK, "ln3", (P, 4 * D), BF16)
    jdump = sb.at(64 * KB, "jdump7", (P, D), BF16)
    zt7 = [sb.at(97 * KB + 512 + i * KB, f"zt7_{i}", (P, 512), BF16)
           for i in range(2)]
    bscr = sb.at(99 * KB + 512, "bscr", (P, 512), F32)
    bscrb = sb.at(99 * KB + 512, "bscrb", (P, 512), BF16)
    x1T = [sb.at(101 * KB + 512 + i * 16 * KB, f"x1T{i}", (P, 16 * 512), BF16)
           for i in range(2)]
    sil7 = [sb.at(64 * KB + i * 2 * KB, f"sil7_{i}", (P, 512), F32)
            for i in range(2)]
    utmp7 = sb.at(68 * KB, "utmp7", (P, 512), F32)
    etmp7 = [sb.at(64 * KB + i * 2 * KB, f"etmp7_{i}", (P, 512), F32)
             for i in range(2)]
    su7 = sb.at(70 * KB, "su7", (P, 64), F32)
    sg64 = sb.at(70 * KB + 256, "sg64", (P, 64), F32)
    sudp7 = sb.at(70 * KB + 512, "sudp7", (P, 64), F32)
    suug7 = sb.at(70 * KB + 768, "suug7", (P, 16), F32)
    hbf7 = sb.at(71 * KB, "hbf7", (P, P), BF16)
    hbf27 = sb.at(71 * KB + 256, "hbf27", (P, P), BF16)
    wupb = [sb.at(OFF_MASK + d * 4 * KB, f"wupb{d}", (P, 16 * P), BF16)
            for d in range(2)]
    wgab = [sb.at(OFF_MASK + 8 * KB + d * 4 * KB, f"wgab{d}", (P, 16 * P), BF16)
            for d in range(2)]
    wdb7 = [sb.at(97 * KB + 512 + d * KB, f"wdb7_{d}", (P, 512), BF16)
            for d in range(2)]
    castb = [sb.at(OFF_MASK + d * 4 * KB, f"castb{d}", (P, 4 * 512), BF16)
             for d in range(2)]
    wdn_bf = nc.dram_tensor("wdn_bf", [64 * 4 * P, 512], BF16).ap()
    wdprow = sb.at(101 * KB + 512, "wdprow", (1, D), F32)
    wdpbc = sb.at(101 * KB + 512, "wdpbc", (P, D), F32)
    rrel7 = sb.at(109 * KB + 512, "rrel7", (P, 16 * 512), F32)

    with nc.psum_tensor("ztp0", [P, 512], BF16) as ztp0, \
         nc.psum_tensor("ztp1", [P, 512], BF16) as ztp1, \
         nc.psum_tensor("fps0", [P, 512], F32) as fps0, \
         nc.psum_tensor("fps1", [P, 512], F32) as fps1, \
         nc.psum_tensor("psU0", [P, 512], F32) as psU0, \
         nc.psum_tensor("psU1", [P, 512], F32) as psU1, \
         nc.psum_tensor("psG0", [P, 512], F32) as psG0, \
         nc.psum_tensor("psG1", [P, 512], F32) as psG1:
        ztp = [ztp0, ztp1]
        fps = [fps0, fps1]
        psU = [psU0, psU1]
        psG = [psG0, psG1]

        @on_sync
        def _(s):
            for j in range(NT):
                LD.dma(s, out=r_spill[j * P:(j + 1) * P, :],
                       in_=R[:, j * D:(j + 1) * D])
            LD.dma(s, out=su7[:, :], in_=IN.su)
            LD.dma(s, out=sg64[:, :], in_=IN.sg)
            LD.dma(s, out=sudp7[:, :], in_=IN.sudp)
            LD.dma(s, out=suug7[:, :], in_=IN.suug)
        chk_spill = LD.n

        @on_gp
        def _(g):
            WD.dma(g, out=hbf7[:, :], in_=IN.hbf)
        chk_hbf = WD.n

        @on_vec
        def _(v, chk_spill=chk_spill, chk_hbf=chk_hbf):
            v.wait_ge(LD.h, chk_spill)
            v.wait_ge(WD.h, chk_hbf)
            v.tensor_scalar(out=sg64[:, :], in0=sg64[:, :], scalar1=64.0,
                            scalar2=None, op0=ALU.mult)
            v.tensor_scalar(out=hbf27[:, :], in0=hbf7[:, :], scalar1=0.5,
                            scalar2=None, op0=ALU.mult)
            v.drain()
            VE.inc(v.tensor_scalar(out=c_t0[0], in0=c_t0[0], scalar1=1.0,
                                   scalar2=None, op0=ALU.mult))
        chk_c7 = VE.n

        x1T_done = [0, 0]
        ztf = [0, 0]
        zsl = [0, 0]
        fpf = [0, 0]
        tr_prev = 0
        for half in range(2):
            for jj in range(4):
                j = half * 4 + jj
                bi = j % 2

                @on_vec
                def _(v, j=j, bi=bi):
                    VE.inc(v.reduce_sum(out=c_s1[bi], in_=R[:, j * D:(j + 1) * D],
                                        axis=AX))

                @on_scl
                def _(scl, j=j, bi=bi):
                    SE.inc(scl.activation(out=jdump[:, :],
                                          in_=R[:, j * D:(j + 1) * D],
                                          func=AF.Square, accum_out=c_s2[bi]))
                chk_sq = SE.n

                @on_vec
                def _(v, bi=bi, chk_sq=chk_sq):
                    v.wait_ge(SE.h, chk_sq)
                    v.drain()
                    v.tensor_scalar(out=c_m[bi], in0=c_s1[bi], scalar1=1.0 / D,
                                    scalar2=None, op0=ALU.mult)
                    v.tensor_scalar(out=c_var[bi], in0=c_s2[bi], scalar1=1.0 / D,
                                    scalar2=1e-5, op0=ALU.mult, op1=ALU.add)
                    v.drain()
                    v.scalar_tensor_tensor(out=c_t0[bi], in0=c_m[bi], scalar=1.0,
                                           in1=c_m[bi], op0=ALU.mult, op1=ALU.mult)
                    v.drain()
                    VE.inc(v.tensor_sub(c_var[bi], c_var[bi], c_t0[bi]))
                chk_v2 = VE.n

                @on_scl
                def _(scl, bi=bi, chk_v2=chk_v2):
                    scl.wait_ge(VE.h, chk_v2)
                    SE.inc(scl.activation(out=c_rstd[bi], in_=c_var[bi],
                                          func=AF.Sqrt))
                chk_s2 = SE.n

                @on_vec
                def _(v, bi=bi, chk_s2=chk_s2):
                    v.wait_ge(SE.h, chk_s2)
                    v.drain()
                    v.reciprocal(c_nt1[bi], c_rstd[bi])
                    v.drain()
                    v.tensor_mul(c_nt2[bi], c_var[bi], c_nt1[bi])
                    v.drain()
                    v.tensor_add(c_rstd[bi], c_rstd[bi], c_nt2[bi])
                    v.drain()
                    v.tensor_scalar(out=c_rstd[bi], in0=c_rstd[bi], scalar1=0.5,
                                    scalar2=None, op0=ALU.mult)
                    v.drain()
                    v.reciprocal(c_rstd[bi], c_rstd[bi])
                    v.drain()
                    VE.inc(v.scalar_tensor_tensor(out=c_nmr[bi], in0=c_m[bi],
                                                  scalar=-1.0, in1=c_rstd[bi],
                                                  op0=ALU.mult, op1=ALU.mult))
                chk_v3 = VE.n

                @on_scl
                def _(scl, j=j, jj=jj, bi=bi, chk_v3=chk_v3, lf=tr_prev):
                    scl.wait_ge(VE.h, chk_v3)
                    scl.wait_ge(TE.h, lf)
                    SE.inc(scl.activation(out=ln3[:, jj * D:(jj + 1) * D],
                                          in_=R[:, j * D:(j + 1) * D],
                                          func=AF.Identity, bias=c_nmr[bi],
                                          scale=c_rstd[bi]))
            chk_ln4 = SE.n

            for c in range(16):
                sl = c % 2

                @on_ten
                def _(t, c=c, sl=sl, chk_ln4=chk_ln4, ztfree=ztf[sl]):
                    t.wait_ge(SE.h, chk_ln4)
                    t.wait_ge(VE.h, ztfree)
                    for jj in range(4):
                        i = t.transpose(ztp[sl][:, jj * P:(jj + 1) * P],
                                        ln3[:, jj * D + c * P:jj * D + (c + 1) * P],
                                        ident[:, :])
                    TE.inc(i)
                chk_tr = TE.n

                @on_vec
                def _(v, c=c, sl=sl, chk_tr=chk_tr, zfree=zsl[sl]):
                    v.wait_ge(TE.h, max(chk_tr, zfree))
                    VE.inc(v.tensor_scalar(out=zt7[sl][:, :], in0=ztp[sl][:, :],
                                           scalar1=suug7[:, c:c + 1], scalar2=None,
                                           op0=ALU.mult))
                ztf[sl] = VE.n
                chk_zev = VE.n

                @on_ten
                def _(t, half=half, c=c, sl=sl, chk_zev=chk_zev,
                      ffree=fpf[sl], chk_hbf=chk_hbf):
                    if half == 0 and c == 0:
                        t.wait_ge(WD.h, chk_hbf)
                    t.wait_ge(VE.h, max(chk_zev, ffree))
                    TE.inc(t.matmul(fps[sl][:, :], hbf7[:, :], zt7[sl][:, :],
                                    start=True, stop=True))
                zsl[sl] = TE.n
                chk_fm = TE.n

                @on_vec
                def _(v, c=c, sl=sl, chk_fm=chk_fm):
                    v.wait_ge(TE.h, chk_fm)
                    VE.inc(v.tensor_scalar(out=x1stage[:, c * 512:(c + 1) * 512],
                                           in0=fps[sl][:, :], scalar1=SQ2,
                                           scalar2=None, op0=ALU.mult))
                fpf[sl] = VE.n
            tr_prev = TE.n

            @on_vec
            def _(v, half=half):
                v.drain()
                for bit in (1, 2, 4):
                    for a in range(16):
                        if a & bit:
                            continue
                        b = a ^ bit
                        v.tensor_sub(x1stage[:, b * 512:(b + 1) * 512],
                                     x1stage[:, a * 512:(a + 1) * 512],
                                     x1stage[:, b * 512:(b + 1) * 512])
                    v.drain()
                    for a in range(16):
                        if a & bit:
                            continue
                        b = a ^ bit
                        v.scalar_tensor_tensor(
                            out=x1stage[:, a * 512:(a + 1) * 512],
                            in0=x1stage[:, a * 512:(a + 1) * 512], scalar=2.0,
                            in1=x1stage[:, b * 512:(b + 1) * 512],
                            op0=ALU.mult, op1=ALU.subtract)
                    v.drain()
                for a in range(8):
                    b = a + 8
                    A = x1stage[:, a * 512:(a + 1) * 512]
                    Bs = x1stage[:, b * 512:(b + 1) * 512]
                    v.tensor_sub(x1T[half][:, b * 512:(b + 1) * 512], A, Bs)
                    v.tensor_add(x1T[half][:, a * 512:(a + 1) * 512], A, Bs)
                v.drain()
                VE.inc(v.tensor_scalar(out=c_t0[0], in0=c_t0[0], scalar1=1.0,
                                       scalar2=None, op0=ALU.mult))
            x1T_done[half] = VE.n

        tr_all = TE.n
        x1_all = max(x1T_done[0], x1T_done[1])

        scope("p7_ug")
        wub_used = [tr_all, tr_all]
        psUf = [0, 0]
        psGf = [0, 0]
        silf = [0, 0]
        for o in range(64):
            d = o % 2

            @on_gp
            def _(g, o=o, d=d, wu=wub_used[d]):
                g.wait_ge(TE.h, wu)
                WDs[d].dma(g, out=wupb[d].ap().rearrange("p (k f) -> p k f", f=P),
                           in_=IN.wup[o * D:(o + 1) * D, :]
                               .rearrange("(k p) f -> p k f", p=P))
                WDs[d].dma(g, out=wgab[d].ap().rearrange("p (k f) -> p k f", f=P),
                           in_=IN.wgate[o * D:(o + 1) * D, :]
                               .rearrange("(k p) f -> p k f", p=P))
            chk_w = WDs[d].n

            for half in range(2):
                s_i = o * 2 + half
                bb = s_i % 2

                @on_ten
                def _(t, o=o, half=half, d=d, bb=bb, chk_w=chk_w,
                      uf=psUf[bb], gf=psGf[bb], x1k=x1_all):
                    t.wait_ge(WDs[d].h, chk_w)
                    t.wait_ge(VE.h, max(uf, x1k))
                    t.wait_ge(SE.h, gf)
                    for k in range(16):
                        t.matmul(psU[bb][:, :], wupb[d][:, k * P:(k + 1) * P],
                                 x1T[half][:, k * 512:(k + 1) * 512],
                                 start=(k == 0), stop=(k == 15))
                    for k in range(16):
                        i = t.matmul(psG[bb][:, :], wgab[d][:, k * P:(k + 1) * P],
                                     x1T[half][:, k * 512:(k + 1) * 512],
                                     start=(k == 0), stop=(k == 15))
                    TE.inc(i)
                wub_used[d] = TE.n
                chk_ug = TE.n

                @on_scl
                def _(scl, o=o, half=half, bb=bb, chk_ug=chk_ug, sf=silf[bb],
                      chk_c7=chk_c7):
                    if o == 0 and half == 0:
                        scl.wait_ge(VE.h, chk_c7)
                    scl.wait_ge(TE.h, chk_ug)
                    scl.wait_ge(VE.h, sf)
                    SE.inc(scl.activation(out=sil7[bb][:, :], in_=psG[bb][:, :],
                                          func=AF.Silu, scale=sg64[:, o:o + 1]))
                psGf[bb] = SE.n
                chk_sil = SE.n

                @on_vec
                def _(v, o=o, half=half, bb=bb, chk_ug=chk_ug, chk_sil=chk_sil):
                    v.wait_ge(TE.h, chk_ug)
                    v.drain()
                    v.tensor_scalar(out=utmp7[:, :], in0=psU[bb][:, :],
                                    scalar1=su7[:, o:o + 1], scalar2=64.0,
                                    op0=ALU.mult, op1=ALU.mult)
                    v.wait_ge(SE.h, chk_sil)
                    v.drain()
                    VE.inc(v.tensor_tensor(
                        out=h7[half][:, o * 512:(o + 1) * 512],
                        in0=utmp7[:, :], in1=sil7[bb][:, :], op=ALU.mult))
                psUf[bb] = VE.n
                silf[bb] = VE.n
        ug_all = TE.n

        scope("p7_fwht2")
        hd2b = [fps0, fps1, psU0, psU1]
        hd2fv = [fpf[0], fpf[1], psUf[0], psUf[1]]
        hd2fs = [0, 0, 0, 0]
        hd2_done = [0, 0]
        bf_all = 0
        for half in range(2):
            @on_vec
            def _(v, half=half):
                v.drain()
                for c in range(64):
                    v.tensor_scalar(out=h7[half][:, c * 512:(c + 1) * 512],
                                    in0=h7[half][:, c * 512:(c + 1) * 512],
                                    scalar1=sudp7[:, c:c + 1], scalar2=None,
                                    op0=ALU.mult)
                v.drain()
                for bit in (1, 2, 4, 8, 16, 32):
                    for a in range(64):
                        if a & bit:
                            continue
                        b = a ^ bit
                        v.tensor_sub(h7[half][:, b * 512:(b + 1) * 512],
                                     h7[half][:, a * 512:(a + 1) * 512],
                                     h7[half][:, b * 512:(b + 1) * 512])
                    v.drain()
                    for a in range(64):
                        if a & bit:
                            continue
                        b = a ^ bit
                        v.scalar_tensor_tensor(
                            out=h7[half][:, a * 512:(a + 1) * 512],
                            in0=h7[half][:, a * 512:(a + 1) * 512], scalar=2.0,
                            in1=h7[half][:, b * 512:(b + 1) * 512],
                            op0=ALU.mult, op1=ALU.subtract)
                    v.drain()
                VE.inc(v.tensor_scalar(out=c_t0[1], in0=c_t0[1], scalar1=1.0,
                                       scalar2=None, op0=ALU.mult))
            chk_bf = VE.n
            bf_all = VE.n

            for c in range(64):
                sl = c % 4

                @on_ten
                def _(t, half=half, c=c, sl=sl, chk_bf=chk_bf,
                      fv=hd2fv[sl], fs=hd2fs[sl]):
                    t.wait_ge(VE.h, max(chk_bf, fv))
                    t.wait_ge(SE.h, fs)
                    TE.inc(t.matmul(hd2b[sl][:, :], hbf27[:, :],
                                    h7[half][:, c * 512:(c + 1) * 512],
                                    start=True, stop=True))
                chk_h2 = TE.n

                @on_scl
                def _(scl, half=half, c=c, sl=sl, chk_h2=chk_h2):
                    scl.wait_ge(TE.h, chk_h2)
                    SE.inc(scl.copy(h7[half][:, c * 512:(c + 1) * 512],
                                    hd2b[sl][:, :]))
                hd2fs[sl] = SE.n
            hd2_done[half] = SE.n

        @on_sync
        def _(s, ug_all=ug_all):
            s.wait_ge(TE.h, ug_all)
            LD.dma(s, out=wdprow[0:1, :], in_=IN.wdp)
        chk_wdp = LD.n

        bc_se0 = hd2fs[0]
        bc_ev7 = 0
        for qq in range(4):
            @on_ten
            def _(t, qq=qq, chk_wdp=chk_wdp, ev=bc_ev7, se0=bc_se0):
                t.wait_ge(LD.h, chk_wdp)
                t.wait_ge(SE.h, se0)
                t.wait_ge(VE.h, ev)
                TE.inc(t.matmul(fps0[:, :], onesf_row,
                                wdprow[0:1, qq * 512:(qq + 1) * 512],
                                start=True, stop=True))
            chk_bm = TE.n

            @on_vec
            def _(v, qq=qq, chk_bm=chk_bm):
                v.wait_ge(TE.h, chk_bm)
                VE.inc(v.tensor_copy(wdpbc[:, qq * 512:(qq + 1) * 512],
                                     fps0[:, :]))
            bc_ev7 = VE.n
        chk_wbc = VE.n

        cast_ld = [0, 0]
        cast_at = []
        for i in range(64):
            d = i % 2

            @on_gp
            def _(g, i=i, d=d, lf=cast_ld[d], ug_all=ug_all):
                g.wait_ge(TE.h, ug_all)
                g.wait_ge(LD.h, lf)
                WDs[d].dma(g, out=castb[d].ap().rearrange("p (c f) -> p c f",
                                                          f=512),
                           in_=IN.wdown[i * 512:(i + 1) * 512, :]
                               .rearrange("(c p) f -> p c f", p=P))
            chk_ci = WDs[d].n

            @on_sync
            def _(s, i=i, d=d, chk_ci=chk_ci):
                s.wait_ge(WDs[d].h, chk_ci)
                LD.dma(s, out=wdn_bf[i * 512:(i + 1) * 512, :]
                           .rearrange("(c p) f -> p c f", p=P),
                       in_=castb[d].ap().rearrange("p (c f) -> p c f", f=512))
            cast_ld[d] = LD.n
            cast_at.append(LD.n)

    scope("p7_down")
    with nc.psum_tensor("psD0", [P, 512], F32) as psD0, \
         nc.psum_tensor("psD1", [P, 512], F32) as psD1, \
         nc.psum_tensor("psD2", [P, 512], F32) as psD2, \
         nc.psum_tensor("psD3", [P, 512], F32) as psD3, \
         nc.psum_tensor("psD4", [P, 512], F32) as psD4, \
         nc.psum_tensor("psD5", [P, 512], F32) as psD5, \
         nc.psum_tensor("psD6", [P, 512], F32) as psD6, \
         nc.psum_tensor("psD7", [P, 512], F32) as psD7:
        psD = [psD0, psD1, psD2, psD3, psD4, psD5, psD6, psD7]
        hd2_all = max(hd2_done[0], hd2_done[1])
        wd_used = [ug_all, ug_all]
        st_free = [0, 0]
        ep_ve = chk_wbc
        rr_free = [0, 0]
        for oh in range(4):
            rp = oh % 2

            @on_sync
            def _(s, oh=oh, rp=rp, ug_all=ug_all, epf=rr_free[rp]):
                s.wait_ge(TE.h, ug_all)
                s.wait_ge(VE.h, epf)
                for i8 in range(NT):
                    LD.dma(s, out=rrel7[:, (rp * 8 + i8) * 512:
                                        (rp * 8 + i8 + 1) * 512],
                           in_=r_spill[i8 * P:(i8 + 1) * P,
                                       oh * 512:(oh + 1) * 512])
            chk_rr = LD.n

            for k in range(64):
                d = k % 2

                @on_gp
                def _(g, oh=oh, k=k, d=d, wu=wd_used[d], bf_all=bf_all,
                      ca=cast_at[k], first=(oh == 0 and k < 2)):
                    g.wait_ge(TE.h, wu)
                    g.wait_ge(LD.h, ca)
                    if first:
                        g.wait_ge(VE.h, bf_all)
                    WDs[d].dma(g, out=wdb7[d][:, :],
                               in_=wdn_bf[(k * 4 + oh) * P:
                                          (k * 4 + oh + 1) * P, :])
                chk_wd = WDs[d].n

                @on_ten
                def _(t, k=k, d=d, chk_wd=chk_wd, hd=hd2_all, ep=ep_ve):
                    t.wait_ge(WDs[d].h, chk_wd)
                    if k == 0:
                        t.wait_ge(SE.h, hd)
                        t.wait_ge(VE.h, ep)
                    for half in range(2):
                        for tt in range(4):
                            i = t.matmul(
                                psD[half * 4 + tt][:, :],
                                h7[half][:, k * 512 + tt * P:
                                         k * 512 + (tt + 1) * P],
                                wdb7[d][:, :],
                                start=(k == 0), stop=(k == 63))
                    TE.inc(i)
                wd_used[d] = TE.n
            chk_acc = TE.n

            for i8 in range(NT):
                par = i8 % 2

                @on_vec
                def _(v, oh=oh, i8=i8, par=par, rp=rp, chk_acc=chk_acc,
                      chk_rr=chk_rr, stf=st_free[par]):
                    v.wait_ge(TE.h, chk_acc)
                    v.wait_ge(LD.h, chk_rr)
                    v.wait_ge(ST.h, stf)
                    v.scalar_tensor_tensor(
                        out=etmp7[par][:, :], in0=psD[i8][:, :], scalar=SC_DN,
                        in1=wdpbc[:, oh * 512:(oh + 1) * 512],
                        op0=ALU.mult, op1=ALU.mult)
                    v.drain()
                    VE.inc(v.tensor_tensor(
                        out=etmp7[par][:, :], in0=etmp7[par][:, :],
                        in1=rrel7[:, (rp * 8 + i8) * 512:
                                  (rp * 8 + i8 + 1) * 512],
                        op=ALU.add))
                chk_ep = VE.n

                @on_sync
                def _(s, oh=oh, i8=i8, par=par, chk_ep=chk_ep):
                    s.wait_ge(VE.h, chk_ep)
                    ST.dma(s, out=y_out[i8 * P:(i8 + 1) * P,
                                        oh * 512:(oh + 1) * 512],
                           in_=etmp7[par][:, :])
                st_free[par] = ST.n
            ep_ve = VE.n
            rr_free[rp] = VE.n

    @on_sync
    def _(s):
        ST.wait(s)

    @on_gp
    def _(g):
        ST.wait(g)

    if _cur_scope[0] is not None:
        _cur_scope[0].__exit__(None, None, None)
    blk.__exit__(None, None, None)
    return nc, set(_declared)


def kernel(**inputs):
    in_maps = _prep_inputs(inputs)
    if "nc" not in _CACHE:
        _CACHE["nc"], _CACHE["used"] = build_program()
    nc, used = _CACHE["nc"], _CACHE["used"]
    in_maps = [{k: v for k, v in m.items() if k in used} for m in in_maps]
    res = run_bass_kernel_spmd(nc, in_maps, core_ids=list(range(NCORE)))
    return _scatter_output(res.results)


# version 11 (final)
# speedup vs baseline: 1.0434x; 1.0434x over previous
import numpy as np
import ml_dtypes
import concourse.bass as bass
import concourse.mybir as mybir
from concourse.bass_utils import run_bass_kernel_spmd

F32 = mybir.dt.float32
BF16 = mybir.dt.bfloat16
AF = mybir.ActivationFunctionType
ALU = mybir.AluOpType
AX = mybir.AxisListType.X

P = 128
D = 2048
T = 2048
B = 4
NCORE = 8
TOK = 1024
NT = TOK // P
NTF = T // P
KV = 512
NH = 16
HD = 128
NKVH = 4
INTER = 8192
MAGIC = float(3 * 2 ** 22)
NEG = -30000.0

X_CHUNKS = [0, 3, 4, 7, 8, 11, 12, 15]
Y_CHUNKS = [1, 2, 5, 6, 9, 10, 13, 14]
EXT = [1, 1, 2, 2, 3, 3, 4, 4]

_CACHE = {}
_PROF = False
_DEBUG = False
_SKIP_KNORM = False



def _rope_tables():
    inv = 1.0 / (10000.0 ** (np.arange(0, HD, 2, dtype=np.float64) / HD))
    pos = np.arange(T, dtype=np.float64)
    fr = pos[:, None] * inv[None, :]
    return np.cos(fr).astype(np.float32), np.sin(fr).astype(np.float32)


def _perm_deint():
    def perm(nheads):
        p = []
        for h in range(nheads):
            base = h * HD
            p += [base + 2 * i for i in range(HD // 2)]
            p += [base + 2 * i + 1 for i in range(HD // 2)]
        return np.array(p)
    return perm(NH), perm(NKVH)


def _masks_for(chunks):
    m = np.zeros((NT, P, 512), np.float32)
    for j, ch in enumerate(chunks):
        e = EXT[j]
        k0 = 512 * (e - 1)
        keys = k0 + np.arange(512)
        q = 128 * ch + np.arange(P)
        m[j] = np.where(keys[None, :] <= q[:, None], 0.0, NEG)
    return m


def _gap_tables(chunks):
    halves = [X_CHUNKS, Y_CHUNKS]
    meta = np.zeros((NT, 512), np.float32)
    for j, ch in enumerate(chunks):
        for r in range(2):
            for h in range(16):
                for c in range(8):
                    g = halves[r][c]
                    idx = r * 128 + h * 8 + c
                    if g < ch:
                        meta[j, idx] = 128 * (ch - g - 1)
                        meta[j, 256 + idx] = 0.0
                    else:
                        meta[j, idx] = 0.0
                        meta[j, 256 + idx] = -1e30
    return meta


def _prep_inputs(inputs):
    x = np.asarray(inputs["L0_attn_x"], np.float32)
    permq, permkv = _perm_deint()
    cos, sin = _rope_tables()
    cosp = cos[:, :]
    sinp = sin[:, :]

    W = {}
    W["wq"] = np.ascontiguousarray(np.asarray(inputs["L0_attn_Wq"], np.float32)[:, permq])
    W["wk"] = np.ascontiguousarray(np.asarray(inputs["L0_attn_Wk"], np.float32)[:, permkv])
    W["wv"] = np.ascontiguousarray(np.asarray(inputs["L0_attn_Wv"], np.float32))
    W["wo"] = np.ascontiguousarray(np.asarray(inputs["L0_attn_Wo"], np.float32))
    W["bq"] = np.asarray(inputs["L0_attn_bq"], np.float32)[permq].reshape(1, D)
    W["bk"] = np.asarray(inputs["L0_attn_bk"], np.float32)[permkv].reshape(1, KV)
    W["bv"] = np.asarray(inputs["L0_attn_bv"], np.float32).reshape(1, KV)
    W["bo"] = np.asarray(inputs["L0_attn_bo"], np.float32).reshape(1, D)
    W["alpha"] = np.asarray(inputs["L0_attn_alpha"], np.float32).reshape(1, NH)
    W["fw"] = np.ascontiguousarray(np.asarray(inputs["L0_ffn_weight"], np.float32))
    W["fb"] = np.asarray(inputs["L0_ffn_bias"], np.float32).reshape(1, D)
    wqkv_n = np.asarray(inputs["L1_attn_w_qkv"], np.float32)
    W["wqkv"] = np.ascontiguousarray(
        wqkv_n.reshape(D, 48, P).transpose(1, 0, 2).reshape(48 * D, P))
    wout_n = np.asarray(inputs["L1_attn_w_out"], np.float32)
    W["wout"] = np.ascontiguousarray(
        wout_n.reshape(D, 4, 512).transpose(1, 0, 2).reshape(4 * D, 512))
    W["decay"] = np.asarray(inputs["L1_attn_decay_param"], np.float32).reshape(1, NH)
    W["su"] = np.ascontiguousarray(
        np.asarray(inputs["L1_ffn_Wscale_ug"], np.float32)[:INTER].reshape(INTER // P, P).T)
    W["sg"] = np.ascontiguousarray(
        np.asarray(inputs["L1_ffn_Wscale_ug"], np.float32)[INTER:].reshape(INTER // P, P).T)
    W["sudp"] = np.ascontiguousarray(
        np.asarray(inputs["L1_ffn_SU_dp"], np.float32).reshape(INTER // P, P).T)
    W["suug"] = np.ascontiguousarray(
        np.asarray(inputs["L1_ffn_SU_ug"], np.float32).reshape(D // P, P).T)
    W["wdp"] = np.asarray(inputs["L1_ffn_Wscale_dp"], np.float32).reshape(1, D)
    wu_t = np.asarray(inputs["L1_ffn_W_up"], np.float32).T
    wg_t = np.asarray(inputs["L1_ffn_W_gate"], np.float32).T
    wd_t = np.asarray(inputs["L1_ffn_W_down"], np.float32).T
    W["wup"] = np.ascontiguousarray(
        wu_t.reshape(D, 64, P).transpose(1, 0, 2).reshape(64 * D, P))
    W["wgate"] = np.ascontiguousarray(
        wg_t.reshape(D, 64, P).transpose(1, 0, 2).reshape(64 * D, P))
    W["wdown"] = np.ascontiguousarray(
        wd_t.reshape(64, P, 4, 512).transpose(0, 2, 1, 3).reshape(64 * 4 * P, 512))
    hh = np.ones((1, 1), np.float32)
    for _ in range(7):
        hh = np.block([[hh, hh], [hh, -hh]])
    W["hbf"] = np.ascontiguousarray(hh / 4096.0)

    in_maps = []
    for c in range(NCORE):
        b = c // 2
        chunks = X_CHUNKS if c % 2 == 0 else Y_CHUNKS
        rows = np.concatenate([np.arange(128 * ch, 128 * ch + 128) for ch in chunks])
        m = dict(W)
        m["x_own"] = np.ascontiguousarray(x[b][rows])
        m["x_full"] = np.ascontiguousarray(x[b])
        m["cos_own"] = np.ascontiguousarray(cosp[rows])
        m["sin_own"] = np.ascontiguousarray(sinp[rows])
        m["cos_full"] = np.ascontiguousarray(cosp)
        m["sin_full"] = np.ascontiguousarray(sinp)
        m["masks"] = _masks_for(chunks)
        m["gapexp"] = _gap_tables(chunks)
        in_maps.append(m)
    return in_maps


def _scatter_output(results):
    out = np.zeros((B, T, D), np.float32)
    for c in range(NCORE):
        b = c // 2
        chunks = X_CHUNKS if c % 2 == 0 else Y_CHUNKS
        y = results[c]["y"]
        for j, ch in enumerate(chunks):
            out[b, 128 * ch:128 * ch + 128, :] = y[128 * j:128 * j + 128, :]
    return out



class Sem:

    def __init__(self, nc, name):
        self.h = nc.alloc_semaphore(name)
        self.n = 0

    def inc(self, instr, n=1):
        instr.then_inc(self.h, n)
        self.n += n
        return instr

    def dma(self, eng, *a, **kw):
        eng.wait_ge(self.h, self.n)
        ins = eng.dma_start(*a, **kw)
        ins.then_inc(self.h, 16)
        self.n += 16
        return ins

    def wait(self, eng, at=None):
        eng.wait_ge(self.h, self.n if at is None else at)


class SB:

    def __init__(self, nc):
        self.nc = nc
        self.base = (nc.sbuf_base + 63) // 64 * 64
        self.limit = nc.sbuf_top
        self.off = self.base
        self.n = 0

    def at(self, off_bytes, name, shape, dtype):
        self.n += 1
        h = self.nc.alloc_sbuf_tensor_at(
            f"{name}_{self.n}", list(shape), dtype, offset=self.base + off_bytes)
        nb = int(np.prod(shape[1:])) * (2 if dtype == BF16 else 4)
        assert self.base + off_bytes + nb <= self.limit, \
            f"SBUF overflow {name}: {off_bytes}+{nb}"
        return h


KB = 1024

OFF_R = 0
OFF_JUNK = 64 * KB
OFF_ROWSB = 68 * KB
OFF_ROWSB2 = 72 * KB
OFF_ROWSB3 = 76 * KB
OFF_ROWSF = 78 * KB + 512
OFF_IDENT = 80 * KB + 512
OFF_ABC = 81 * KB
OFF_MASK = 81 * KB + 512
OFF_OVL = 89 * KB + 512

O_KRT = 0
O_VBF = 16 * KB
O_QRT = 32 * KB
O_YT = 64 * KB
O_ROPE = 96 * KB
O_FREE = 108 * KB

PT_W = [1024] * 4 + [768] * 4 + [512] * 4 + [256] * 4
PT_OFF = [0]
for _w in PT_W[:-1]:
    PT_OFF.append(PT_OFF[-1] + _w)
PT_COLS = PT_OFF[-1] + PT_W[-1] + 128
AV_JMIN = [0] * 4 + [2] * 4 + [4] * 4 + [6] * 4


def build_program():
    nc = bass.Bass("TRN2", target_bir_lowering=False)

    _shapes = dict(
        x_own=(TOK, D), x_full=(T, D), cos_own=(TOK, 64), sin_own=(TOK, 64),
        cos_full=(T, 64), sin_full=(T, 64), masks=(NT, P, 512),
        gapexp=(NT, 512),
        wq=(D, D), wk=(D, KV), wv=(D, KV), wo=(D, D),
        bq=(1, D), bk=(1, KV), bv=(1, KV), bo=(1, D), alpha=(1, NH),
        fw=(D, D), fb=(1, D),
        wqkv=(48 * D, P), wout=(4 * D, 512), decay=(1, NH),
        su=(P, 64), sg=(P, 64), sudp=(P, 64), suug=(P, 16), wdp=(1, D),
        wup=(64 * D, P), wgate=(64 * D, P), wdown=(64 * 4 * P, 512), hbf=(P, P),
    )
    _declared = {}

    class _In:
        def __getattr__(self, name):
            if name not in _declared:
                _declared[name] = nc.dram_tensor(
                    name, list(_shapes[name]), F32, kind="ExternalInput").ap()
            return _declared[name]

    IN = _In()
    y_out = nc.dram_tensor("y", [TOK, D], F32, kind="ExternalOutput").ap()
    wsband = nc.dram_tensor("wsband", [P, 4], F32).ap()
    if _DEBUG:
        dbg_krT = nc.dram_tensor("dbg_krT", [P, NKVH * T], F32, kind="ExternalOutput").ap()
        dbg_v = nc.dram_tensor("dbg_v", [P, NTF * KV], F32, kind="ExternalOutput").ap()
        dbg_qrT = nc.dram_tensor("dbg_qrT", [P, NH * TOK], F32, kind="ExternalOutput").ap()
        dbg_yT = nc.dram_tensor("dbg_yT", [P, NH * TOK], F32, kind="ExternalOutput").ap()
        dbg_lnbf = nc.dram_tensor("dbg_lnbf", [P, D], F32, kind="ExternalOutput").ap()
        dbg_lnfT = nc.dram_tensor("dbg_lnfT", [P, D], F32, kind="ExternalOutput").ap()
        dbg_wk = nc.dram_tensor("dbg_wk", [P, 16 * KV], F32, kind="ExternalOutput").ap()
        dbg_kst = nc.dram_tensor("dbg_kst", [P, KV], F32, kind="ExternalOutput").ap()
        dbg_xf = nc.dram_tensor("dbg_xf", [P, D], F32, kind="ExternalOutput").ap()
        dbg_xqT = nc.dram_tensor("dbg_xqT", [P, 16 * 1024], F32, kind="ExternalOutput").ap()
        dbg_wqT = nc.dram_tensor("dbg_wqT", [P, 16 * 512], F32, kind="ExternalOutput").ap()
        dbg_rows = nc.dram_tensor("dbg_rows", [P, KV], F32, kind="ExternalOutput").ap()

    ident_dram = nc.inline_tensor(np.eye(P, dtype=ml_dtypes.bfloat16), name="identc").ap()

    sb = SB(nc)
    R = sb.at(OFF_R, "R", (P, NT * D), F32)
    junk = sb.at(OFF_JUNK, "junk", (P, D), BF16)
    rows_bf = sb.at(OFF_ROWSB, "rows_bf", (P, D), BF16)
    rows_bf2 = sb.at(OFF_ROWSB2, "rows_bf2", (P, D), BF16)
    rows_bf3 = sb.at(OFF_ROWSB3, "rows_bf3", (P, 1280), BF16)
    rows_f = sb.at(OFF_ROWSF, "rows_f", (P, KV), F32)
    ident = sb.at(OFF_IDENT, "ident", (P, P), BF16)
    alpha_bc = sb.at(OFF_ABC, "alpha_bc", (P, NH), F32)
    negaa = sb.at(OFF_ABC + 64, "negaa", (P, NH), F32)
    masksb = sb.at(OFF_MASK, "masksb", (P, NT * 512), BF16)

    bq_row = rows_bf[0:1, 0:D]
    bo_row = rows_bf2[0:1, 0:D]
    bk_row = rows_bf3[0:1, 0:KV]
    bv_row = rows_bf3[0:1, KV:2 * KV]
    onesb_row = rows_bf3[0:1, 2 * KV:2 * KV + P]
    alpha_row = rows_f[0:1, 0:NH]
    onesf_row = rows_f[0:1, 16:16 + P]
    c_s1 = [rows_f[:, 256 + i:257 + i] for i in range(2)]
    c_s2 = [rows_f[:, 258 + i:259 + i] for i in range(2)]
    c_m = [rows_f[:, 260 + i:261 + i] for i in range(2)]
    c_var = [rows_f[:, 262 + i:263 + i] for i in range(2)]
    c_rstd = [rows_f[:, 264 + i:265 + i] for i in range(2)]
    c_nmr = [rows_f[:, 266 + i:267 + i] for i in range(2)]
    c_t0 = [rows_f[:, 268 + i:269 + i] for i in range(2)]
    c_rn = [rows_f[:, 272 + 16 * i:288 + 16 * i] for i in range(2)]
    mxc = [rows_f[:, 304 + 4 * i:308 + 4 * i] for i in range(2)]
    laccc = [rows_f[:, 312 + 4 * i:316 + 4 * i] for i in range(2)]
    c_negmax = [rows_f[:, 320 + i:321 + i] for i in range(2)]
    c_l = [rows_f[:, 322 + i:323 + i] for i in range(2)]
    c_rl = [rows_f[:, 324 + i:325 + i] for i in range(2)]
    c_mk = rows_f[:, 326:327]
    c_nt1 = [rows_f[:, 328 + i:329 + i] for i in range(2)]
    c_nt2 = [rows_f[:, 330 + i:331 + i] for i in range(2)]

    krT = sb.at(OFF_OVL + O_KRT, "krT", (P, NKVH * T), BF16)
    vbf = sb.at(OFF_OVL + O_VBF, "vbf", (P, NTF * KV), BF16)
    qrT = sb.at(OFF_OVL + O_QRT, "qrT", (P, NH * TOK), BF16)
    yT = sb.at(OFF_OVL + O_YT, "yT", (P, NH * TOK), BF16)
    rqcos = sb.at(OFF_OVL + O_ROPE, "rqcos", (P, NT * 64), F32)
    rqsin = sb.at(OFF_OVL + O_ROPE + 2 * KB, "rqsin", (P, NT * 64), F32)
    rkcos = sb.at(OFF_OVL + O_ROPE + 4 * KB, "rkcos", (P, NTF * 64), F32)
    rksin = sb.at(OFF_OVL + O_ROPE + 8 * KB, "rksin", (P, NTF * 64), F32)

    kvo = OFF_OVL + O_QRT
    xf2 = [sb.at(kvo + i * 8 * KB, f"xf{i}", (P, D), F32) for i in range(2)]
    lnbf2 = [sb.at(kvo + 16 * KB + i * 4 * KB, f"lnbf{i}", (P, D), BF16) for i in range(2)]
    lnfT2 = [sb.at(kvo + 24 * KB + i * 4 * KB, f"lnfT{i}", (P, D), BF16) for i in range(2)]
    wkbf = sb.at(kvo + 32 * KB, "wkbf", (P, 16 * KV), BF16)
    wvbf = sb.at(kvo + 48 * KB, "wvbf", (P, 16 * KV), BF16)
    kst2 = [sb.at(OFF_OVL + O_FREE + i * 2 * KB, f"kst{i}", (P, KV), F32) for i in range(2)]
    ksq = sb.at(OFF_OVL + O_FREE + 4 * KB, "ksq", (P, KV), F32)
    rtA = sb.at(OFF_OVL + O_FREE + 6 * KB, "rtA", (P, KV), F32)
    rtB = sb.at(OFF_OVL + O_FREE + 8 * KB, "rtB", (P, KV), F32)
    kbf = sb.at(OFF_OVL + O_FREE + 6 * KB, "kbf", (P, KV), BF16)

    qo = OFF_OVL + O_YT
    wqhalf = sb.at(qo, "wqhalf", (P, 16 * 1024), BF16)
    qlnbf = sb.at(OFF_JUNK, "qlnbf", (P, D), BF16)
    qlnfT = sb.at(OFF_OVL + O_FREE, "qlnfT", (P, D), BF16)
    qstage = sb.at(OFF_OVL + O_FREE, "qstage", (P, 1024), F32)
    qsq = sb.at(OFF_OVL + O_FREE + 4 * KB, "qsq", (P, KV), F32)
    qr1 = sb.at(OFF_OVL + O_FREE + 6 * KB, "qr1", (P, KV), F32)
    qr2 = sb.at(OFF_OVL + O_FREE + 8 * KB, "qr2", (P, KV), F32)
    qbf = sb.at(OFF_OVL + O_FREE + 4 * KB, "qbf", (P, 1024), BF16)

    Pbuf = sb.at(OFF_JUNK, "Pbuf", (P, D), BF16)
    PTbuf = sb.at(OFF_OVL + O_ROPE, "PTbuf", (P, PT_COLS), BF16)

    wobf2 = [sb.at(OFF_OVL + O_ROPE, "wobf0", (P, 16 * 512), BF16),
             sb.at(OFF_OVL + O_QRT, "wobf1", (P, 16 * 512), BF16)]

    f0_ln = sb.at(OFF_OVL + 0, "f0_ln", (P, D), F32)
    f0_xq = sb.at(OFF_OVL + 8 * KB, "f0_xq", (P, D), BF16)
    f0_tmp = sb.at(OFF_OVL + 8 * KB, "f0_tmp", (P, KV), F32)
    f0_xqT = sb.at(OFF_OVL + 12 * KB, "f0_xqT", (P, 16 * 1024), BF16)
    f0_wst = sb.at(OFF_OVL + 44 * KB, "f0_wst", (P, 4 * D), F32)
    f0_wqn = sb.at(OFF_OVL + 76 * KB, "f0_wqn", (P, 4 * D), BF16)
    f0_wsbc = sb.at(OFF_OVL + 92 * KB, "f0_wsbc", (P, KV), F32)
    f0_bbc = sb.at(OFF_OVL + 94 * KB, "f0_bbc", (P, D), F32)
    f0_wsr = sb.at(OFF_OVL + 94 * KB, "f0_wsr", (P, KV), F32)
    f0_wqT = sb.at(OFF_OVL + 102 * KB, "f0_wqT", (P, 16 * 512), BF16)

    LD = Sem(nc, "LD")
    WD = Sem(nc, "WD")
    TE = Sem(nc, "TE")
    VE = Sem(nc, "VE")
    SE = Sem(nc, "SE")
    ST = Sem(nc, "ST")
    WD2 = Sem(nc, "WD2")
    CC = Sem(nc, "CC")

    blk = nc.Block()
    block = blk.__enter__()

    def on_sync(f):
        block.sync(f)

    def on_gp(f):
        block.gpsimd(f)

    def on_vec(f):
        block.vector(f)

    def on_scl(f):
        block.scalar(f)

    def on_ten(f):
        block.tensor(f)

    _cur_scope = [None]

    def scope(name):
        if not _PROF:
            return
        if _cur_scope[0] is not None:
            _cur_scope[0].__exit__(None, None, None)
        cm = nc.named_scope(name)
        cm.__enter__()
        _cur_scope[0] = cm

    scope("p0")
    @on_sync
    def _(s):
        for j in range(NT):
            LD.dma(s, out=R[:, j * D:(j + 1) * D],
                               in_=IN.x_own[j * P:(j + 1) * P, :])
        LD.dma(s, out=ident[:, :], in_=ident_dram)
        LD.dma(s, out=alpha_row, in_=IN.alpha)
        LD.dma(s, 
            out=rqcos.ap().rearrange("p (j d) -> p j d", d=64),
            in_=IN.cos_own.rearrange("(j p) d -> p j d", p=P))
        LD.dma(s, 
            out=rqsin.ap().rearrange("p (j d) -> p j d", d=64),
            in_=IN.sin_own.rearrange("(j p) d -> p j d", p=P))
        LD.dma(s, 
            out=rkcos.ap().rearrange("p (j d) -> p j d", d=64),
            in_=IN.cos_full.rearrange("(j p) d -> p j d", p=P))
        LD.dma(s, 
            out=rksin.ap().rearrange("p (j d) -> p j d", d=64),
            in_=IN.sin_full.rearrange("(j p) d -> p j d", p=P))

    @on_gp
    def _(g):
        WD.dma(g, out=bq_row, in_=IN.bq)
        WD.dma(g, out=bk_row, in_=IN.bk)
        WD.dma(g, out=bv_row, in_=IN.bv)
        WD.dma(g, out=bo_row, in_=IN.bo)
        WD.dma(g, out=masksb.ap().rearrange("p (j k) -> p j k", k=512),
                           in_=IN.masks.rearrange("j p k -> p j k"))
        WD.dma(g, out=wkbf.ap().rearrange("p (c f) -> p c f", f=KV),
                           in_=IN.wk.rearrange("(c p) f -> p c f", p=P))
        WD.dma(g, out=wvbf.ap().rearrange("p (c f) -> p c f", f=KV),
                           in_=IN.wv.rearrange("(c p) f -> p c f", p=P))

    chk_ld0 = LD.n
    chk_wd0 = WD.n

    @on_vec
    def _(v):
        v.wait_ge(LD.h, chk_ld0)
        v.memset(onesb_row, 1.0)
        v.memset(onesf_row, 1.0)
        v.drain()
        VE.inc(v.tensor_scalar(out=onesf_row, in0=onesf_row, scalar1=1.0,
                               scalar2=None, op0=ALU.mult))

    with nc.psum_tensor("pc0", [P, NH], F32) as pc0:
        @on_ten
        def _(t):
            t.wait_ge(VE.h, VE.n)
            t.wait_ge(LD.h, chk_ld0)
            TE.inc(t.matmul(pc0[:, :], onesf_row, alpha_row, start=True, stop=True))

        @on_vec
        def _(v):
            v.wait_ge(TE.h, TE.n)
            v.tensor_copy(alpha_bc[:, :], pc0[:, :])
            v.drain()
            VE.inc(v.scalar_tensor_tensor(out=negaa[:, :], in0=alpha_bc[:, :],
                                          scalar=-1.0, in1=alpha_bc[:, :],
                                          op0=ALU.mult, op1=ALU.min))

    scope("p1_kv")
    with nc.psum_tensor("trp", [P, D], BF16) as trp, \
         nc.psum_tensor("kps", [P, KV], F32) as kps, \
         nc.psum_tensor("kps2", [P, KV], F32) as kps2, \
         nc.psum_tensor("vps", [P, KV], F32) as vps, \
         nc.psum_tensor("vps2", [P, KV], F32) as vps2:

        kpsl = [kps, kps2]
        vpsl = [vps, vps2]
        xf_free = [0, 0]
        lnbf_free = [0, 0]
        lnfT_free = [0, 0]
        trp_free = [0, 0]
        kps_free = [0, 0]
        vps_free = [0, 0]
        ktr_free = [0, 0]
        LNEPS = 1e-5

        for ft in range(NTF):
            bi = ft % 2

            @on_sync
            def _(s, ft=ft, bi=bi):
                s.wait_ge(SE.h, xf_free[bi])
                LD.dma(s, out=xf2[bi][:, :],
                                   in_=IN.x_full[ft * P:(ft + 1) * P, :])
            chk_x = LD.n

            @on_vec
            def _(v, bi=bi, chk_x=chk_x):
                v.wait_ge(LD.h, chk_x)
                VE.inc(v.reduce_sum(out=c_s1[bi], in_=xf2[bi][:, :], axis=AX))
            chk_v1 = VE.n

            @on_scl
            def _(scl, bi=bi, chk_x=chk_x):
                scl.wait_ge(LD.h, chk_x)
                SE.inc(scl.activation(out=junk[:, :], in_=xf2[bi][:, :],
                                      func=AF.Square, accum_out=c_s2[bi]))
            chk_sq = SE.n

            @on_vec
            def _(v, bi=bi, chk_sq=chk_sq):
                v.wait_ge(SE.h, chk_sq)
                v.drain()
                v.tensor_scalar(out=c_m[bi], in0=c_s1[bi], scalar1=1.0 / D, scalar2=None,
                                op0=ALU.mult)
                v.tensor_scalar(out=c_var[bi], in0=c_s2[bi], scalar1=1.0 / D,
                                scalar2=1e-5, op0=ALU.mult, op1=ALU.add)
                v.drain()
                v.scalar_tensor_tensor(out=c_t0[bi], in0=c_m[bi], scalar=1.0, in1=c_m[bi],
                                       op0=ALU.mult, op1=ALU.mult)
                v.drain()
                VE.inc(v.tensor_sub(c_var[bi], c_var[bi], c_t0[bi]))
            chk_v2 = VE.n

            @on_scl
            def _(scl, bi=bi, chk_v2=chk_v2):
                scl.wait_ge(VE.h, chk_v2)
                SE.inc(scl.activation(out=c_rstd[bi], in_=c_var[bi], func=AF.Sqrt))
            chk_s2 = SE.n

            @on_vec
            def _(v, bi=bi, chk_s2=chk_s2):
                v.wait_ge(SE.h, chk_s2)
                v.drain()
                v.reciprocal(c_nt1[bi], c_rstd[bi])
                v.drain()
                v.tensor_mul(c_nt2[bi], c_var[bi], c_nt1[bi])
                v.drain()
                v.tensor_add(c_rstd[bi], c_rstd[bi], c_nt2[bi])
                v.drain()
                v.tensor_scalar(out=c_rstd[bi], in0=c_rstd[bi], scalar1=0.5,
                                scalar2=None, op0=ALU.mult)
                v.drain()
                v.reciprocal(c_rstd[bi], c_rstd[bi])
                v.drain()
                VE.inc(v.scalar_tensor_tensor(out=c_nmr[bi], in0=c_m[bi], scalar=-1.0,
                                              in1=c_rstd[bi], op0=ALU.mult, op1=ALU.mult))
            chk_v3 = VE.n

            @on_scl
            def _(scl, bi=bi, chk_v3=chk_v3):
                scl.wait_ge(VE.h, chk_v3)
                scl.wait_ge(VE.h, lnbf_free[bi])
                SE.inc(scl.activation(out=lnbf2[bi][:, :], in_=xf2[bi][:, :],
                                      func=AF.Identity, bias=c_nmr[bi], scale=c_rstd[bi]))
            chk_ln = SE.n
            xf_free[bi] = SE.n

            @on_ten
            def _(t, bi=bi, chk_ln=chk_ln):
                t.wait_ge(SE.h, chk_ln)
                t.wait_ge(VE.h, trp_free[0])
                t.wait_ge(VE.h, trp_free[1])
                for c in range(16):
                    i = t.transpose(trp[:, c * P:(c + 1) * P],
                                    lnbf2[bi][:, c * P:(c + 1) * P], ident[:, :])
                TE.inc(i)
            chk_tr = TE.n

            @on_vec
            def _(v, bi=bi, chk_tr=chk_tr):
                v.wait_ge(TE.h, chk_tr)
                v.wait_ge(TE.h, lnfT_free[bi])
                VE.inc(v.tensor_copy(lnfT2[bi][:, :], trp[:, :]))
            chk_lnT = VE.n
            trp_free[0] = VE.n
            trp_free[1] = VE.n
            lnbf_free[bi] = VE.n

            @on_ten
            def _(t, ft=ft, bi=bi, chk_lnT=chk_lnT):
                t.wait_ge(VE.h, chk_lnT)
                t.wait_ge(WD.h, chk_wd0)
                t.wait_ge(VE.h, kps_free[bi])
                t.wait_ge(SE.h, vps_free[bi])
                for c in range(16):
                    t.matmul(kpsl[bi][:, :], lnfT2[bi][:, c * P:(c + 1) * P],
                             wkbf[:, c * KV:(c + 1) * KV], start=(c == 0), stop=False)
                t.matmul(kpsl[bi][:, :], onesb_row, bk_row, start=False, stop=True)
                for c in range(16):
                    t.matmul(vpsl[bi][:, :], lnfT2[bi][:, c * P:(c + 1) * P],
                             wvbf[:, c * KV:(c + 1) * KV], start=(c == 0), stop=False)
                i = t.matmul(vpsl[bi][:, :], onesb_row, bv_row, start=False, stop=True)
                TE.inc(i)
            chk_kv = TE.n
            lnfT_free[bi] = TE.n

            @on_scl
            def _(scl, ft=ft, bi=bi, chk_kv=chk_kv):
                scl.wait_ge(TE.h, chk_kv)
                SE.inc(scl.copy(vbf[:, ft * KV:(ft + 1) * KV], vpsl[bi][:, :]))
            vps_free[bi] = SE.n

            @on_vec
            def _(v, bi=bi, chk_kv=chk_kv):
                v.wait_ge(TE.h, chk_kv)
                VE.inc(v.tensor_copy(kst2[bi][:, :], kpsl[bi][:, :]))
            chk_kev = VE.n
            kps_free[bi] = VE.n

            @on_scl
            def _(scl, bi=bi, chk_kev=chk_kev):
                scl.wait_ge(VE.h, chk_kev)
                SE.inc(scl.activation(out=ksq[:, :], in_=kst2[bi][:, :], func=AF.Square))
            chk_ksq = SE.n

            @on_vec
            def _(v, bi=bi, ft=ft, chk_ksq=chk_ksq):
                v.wait_ge(SE.h, chk_ksq)
                VE.inc(v.reduce_sum(
                    out=c_rn[bi][:, 0:NKVH],
                    in_=ksq.ap().rearrange("p (g d) -> p g d", g=NKVH), axis=AX))
            chk_rn = VE.n

            @on_vec
            def _(v, bi=bi, chk_rn=chk_rn):
                v.drain()
                VE.inc(v.tensor_scalar(out=c_rn[bi][:, 0:NKVH], in0=c_rn[bi][:, 0:NKVH],
                                       scalar1=1e-12, scalar2=None, op0=ALU.max))
            chk_rnf = VE.n

            @on_scl
            def _(scl, bi=bi, chk_rnf=chk_rnf):
                scl.wait_ge(VE.h, chk_rnf)
                SE.inc(scl.activation(out=c_rn[bi][:, 4:4 + NKVH], in_=c_rn[bi][:, 0:NKVH],
                                      func=AF.Sqrt))
            chk_rs = SE.n

            @on_vec
            def _(v, bi=bi, ft=ft, chk_rs=chk_rs, ktf=ktr_free[bi]):
                v.wait_ge(SE.h, chk_rs)
                v.wait_ge(TE.h, ktf)
                if _SKIP_KNORM:
                    VE.inc(v.tensor_copy(kbf[:, :], kst2[bi][:, :]))
                    return
                v.reciprocal(c_rn[bi][:, 8:8 + NKVH], c_rn[bi][:, 4:4 + NKVH])
                v.drain()
                v.tensor_mul(c_rn[bi][:, 8:8 + NKVH], c_rn[bi][:, 0:NKVH],
                             c_rn[bi][:, 8:8 + NKVH])
                v.drain()
                v.tensor_add(c_rn[bi][:, 4:4 + NKVH], c_rn[bi][:, 4:4 + NKVH],
                             c_rn[bi][:, 8:8 + NKVH])
                v.drain()
                v.tensor_scalar(out=c_rn[bi][:, 4:4 + NKVH], in0=c_rn[bi][:, 4:4 + NKVH],
                                scalar1=0.5, scalar2=1e-6, op0=ALU.mult, op1=ALU.add)
                v.drain()
                v.reciprocal(c_rn[bi][:, 4:4 + NKVH], c_rn[bi][:, 4:4 + NKVH])
                v.drain()
                k3 = kst2[bi].ap().rearrange("p (g d) -> p g d", g=NKVH)
                v.tensor_tensor(
                    out=k3, in0=k3,
                    in1=c_rn[bi][:, 4:4 + NKVH, None].broadcast_to([P, NKVH, HD]),
                    op=ALU.mult)
                v.drain()
                x1 = k3[:, :, 0:64]
                x2 = k3[:, :, 64:128]
                cT = rkcos[:, ft * 64:(ft + 1) * 64][:, None, :].broadcast_to([P, NKVH, 64])
                sT = rksin[:, ft * 64:(ft + 1) * 64][:, None, :].broadcast_to([P, NKVH, 64])
                rA3 = rtA.ap().rearrange("p (g d) -> p g d", g=NKVH)[:, :, 0:64]
                rB3 = rtB.ap().rearrange("p (g d) -> p g d", g=NKVH)[:, :, 0:64]
                rC3 = ksq.ap().rearrange("p (g d) -> p g d", g=NKVH)[:, :, 0:64]
                v.tensor_mul(rA3, x1, cT)
                v.tensor_mul(rB3, x2, sT)
                v.tensor_mul(rC3, x1, sT)
                v.drain()
                v.tensor_sub(x1, rA3, rB3)
                v.drain()
                v.tensor_mul(rA3, x2, cT)
                v.drain()
                v.tensor_add(x2, rC3, rA3)
                v.drain()
                VE.inc(v.tensor_copy(kbf[:, :], kst2[bi][:, :]))
            chk_kbf = VE.n

            @on_ten
            def _(t, ft=ft, chk_kbf=chk_kbf):
                t.wait_ge(VE.h, chk_kbf)
                t.wait_ge(VE.h, trp_free[0])
                for g in range(NKVH):
                    i = t.transpose(trp[:, g * P:(g + 1) * P],
                                    kbf[:, g * P:(g + 1) * P], ident[:, :])
                TE.inc(i)
            chk_ktr = TE.n
            ktr_free[bi] = TE.n

            @on_vec
            def _(v, ft=ft, chk_ktr=chk_ktr):
                v.wait_ge(TE.h, chk_ktr)
                VE.inc(v.tensor_copy(
                    krT.ap().rearrange("p (g t) -> p g t", g=NKVH)[:, :, ft * P:(ft + 1) * P],
                    trp[:, 0:NKVH * P].rearrange("p (g q) -> p g q", g=NKVH)))
            trp_free[0] = VE.n

    def phase_barrier():
        te_n, ve_n, se_n, ld_n, wd_n, st_n = TE.n, VE.n, SE.n, LD.n, WD.n, ST.n
        wd2_n = WD2.n

        def bw(e):
            e.wait_ge(TE.h, te_n)
            e.wait_ge(VE.h, ve_n)
            e.wait_ge(SE.h, se_n)
            e.wait_ge(LD.h, ld_n)
            e.wait_ge(WD.h, wd_n)
            e.wait_ge(WD2.h, wd2_n)
            e.wait_ge(ST.h, st_n)

        on_sync(bw)
        on_gp(bw)
        on_vec(bw)
        on_scl(bw)
        on_ten(bw)

    WDs = [WD, WD2]

    phase_barrier()

    if _DEBUG:
        @on_gp
        def _(g):
            ST.dma(g, out=dbg_lnbf, in_=lnbf2[1][:, :])
            ST.dma(g, out=dbg_lnfT, in_=lnfT2[1][:, :])
            ST.dma(g, out=dbg_wk, in_=wkbf[:, :])
            ST.dma(g, out=dbg_kst, in_=kst2[1][:, :])
            ST.dma(g, out=dbg_xf, in_=xf2[1][:, :])
            ST.wait(g)
        phase_barrier()

    scope("p2_q")
    with nc.psum_tensor("trq", [P, D], BF16) as trq, \
         nc.psum_tensor("qps", [P, 1024], F32) as qps, \
         nc.psum_tensor("qps2", [P, 1024], F32) as qps2:
        qpsl = [qps, qps2]
        lnT_free = 0
        trq_free = 0
        qln_free_te = 0
        qps_free = [0, 0]
        qsq_free_te = 0
        wq_used = 0

        for hf in range(2):
            @on_gp
            def _(g, hf=hf, wq_used=wq_used):
                if hf > 0:
                    g.wait_ge(TE.h, wq_used)
                WD.dma(g, 
                    out=wqhalf.ap().rearrange("p (c f) -> p c f", f=1024),
                    in_=IN.wq[:, hf * 1024:(hf + 1) * 1024]
                        .rearrange("(c p) f -> p c f", p=P))
            chk_wq = WD.n

            for j in range(NT):
                bi = j % 2
                @on_vec
                def _(v, j=j, bi=bi):
                    VE.inc(v.reduce_sum(out=c_s1[bi], in_=R[:, j * D:(j + 1) * D],
                                        axis=AX))

                @on_scl
                def _(scl, j=j, bi=bi, tf=qln_free_te):
                    scl.wait_ge(TE.h, tf)
                    SE.inc(scl.activation(out=qlnbf[:, :], in_=R[:, j * D:(j + 1) * D],
                                          func=AF.Square, accum_out=c_s2[bi]))
                chk_sq = SE.n

                @on_vec
                def _(v, bi=bi, chk_sq=chk_sq):
                    v.wait_ge(SE.h, chk_sq)
                    v.drain()
                    v.tensor_scalar(out=c_m[bi], in0=c_s1[bi], scalar1=1.0 / D,
                                    scalar2=None, op0=ALU.mult)
                    v.tensor_scalar(out=c_var[bi], in0=c_s2[bi], scalar1=1.0 / D,
                                    scalar2=1e-5, op0=ALU.mult, op1=ALU.add)
                    v.drain()
                    v.scalar_tensor_tensor(out=c_t0[bi], in0=c_m[bi], scalar=1.0,
                                           in1=c_m[bi], op0=ALU.mult, op1=ALU.mult)
                    v.drain()
                    VE.inc(v.tensor_sub(c_var[bi], c_var[bi], c_t0[bi]))
                chk_v2 = VE.n

                @on_scl
                def _(scl, bi=bi, chk_v2=chk_v2):
                    scl.wait_ge(VE.h, chk_v2)
                    SE.inc(scl.activation(out=c_rstd[bi], in_=c_var[bi], func=AF.Sqrt))
                chk_s2 = SE.n

                @on_vec
                def _(v, bi=bi, chk_s2=chk_s2):
                    v.wait_ge(SE.h, chk_s2)
                    v.drain()
                    v.reciprocal(c_nt1[bi], c_rstd[bi])
                    v.drain()
                    v.tensor_mul(c_nt2[bi], c_var[bi], c_nt1[bi])
                    v.drain()
                    v.tensor_add(c_rstd[bi], c_rstd[bi], c_nt2[bi])
                    v.drain()
                    v.tensor_scalar(out=c_rstd[bi], in0=c_rstd[bi], scalar1=0.5,
                                    scalar2=None, op0=ALU.mult)
                    v.drain()
                    v.reciprocal(c_rstd[bi], c_rstd[bi])
                    v.drain()
                    VE.inc(v.scalar_tensor_tensor(out=c_nmr[bi], in0=c_m[bi],
                                                  scalar=-1.0, in1=c_rstd[bi],
                                                  op0=ALU.mult, op1=ALU.mult))
                chk_v3 = VE.n

                @on_scl
                def _(scl, j=j, bi=bi, chk_v3=chk_v3):
                    scl.wait_ge(VE.h, chk_v3)
                    SE.inc(scl.activation(out=qlnbf[:, :],
                                          in_=R[:, j * D:(j + 1) * D],
                                          func=AF.Identity, bias=c_nmr[bi],
                                          scale=c_rstd[bi]))
                chk_ln = SE.n

                @on_ten
                def _(t, bi=bi, chk_ln=chk_ln, trf=trq_free):
                    t.wait_ge(SE.h, chk_ln)
                    t.wait_ge(VE.h, trf)
                    for c in range(16):
                        i = t.transpose(trq[:, c * P:(c + 1) * P],
                                        qlnbf[:, c * P:(c + 1) * P], ident[:, :])
                    TE.inc(i)
                chk_tr = TE.n
                qln_free_te = TE.n

                @on_vec
                def _(v, bi=bi, chk_tr=chk_tr, lf=lnT_free, sf=qsq_free_te):
                    v.wait_ge(TE.h, chk_tr)
                    v.wait_ge(TE.h, lf)
                    v.wait_ge(TE.h, sf)
                    VE.inc(v.tensor_copy(qlnfT[:, :], trq[:, :]))
                chk_lnT = VE.n
                trq_free = VE.n

                @on_ten
                def _(t, bi=bi, hf=hf, chk_lnT=chk_lnT, chk_wq=chk_wq,
                      qf=qps_free[bi]):
                    t.wait_ge(VE.h, chk_lnT)
                    t.wait_ge(WD.h, chk_wq)
                    t.wait_ge(VE.h, qf)
                    for h2 in range(2):
                        sl = slice(h2 * KV, h2 * KV + KV)
                        for c in range(16):
                            t.matmul(qpsl[bi][:, sl],
                                     qlnfT[:, c * P:(c + 1) * P],
                                     wqhalf[:, c * 1024 + h2 * KV:
                                            c * 1024 + h2 * KV + KV],
                                     start=(c == 0), stop=False)
                        i = t.matmul(qpsl[bi][:, sl], onesb_row,
                                     bq_row[:, hf * 1024 + h2 * KV:
                                            hf * 1024 + h2 * KV + KV],
                                     start=False, stop=True)
                    TE.inc(i)
                chk_qmm = TE.n
                lnT_free = TE.n
                wq_used = TE.n

                @on_vec
                def _(v, bi=bi, chk_qmm=chk_qmm):
                    v.wait_ge(TE.h, chk_qmm)
                    VE.inc(v.tensor_copy(qstage[:, :], qpsl[bi][:, :]))
                chk_qev = VE.n
                qps_free[bi] = VE.n

                @on_scl
                def _(scl, bi=bi, chk_qev=chk_qev, qsqf=qsq_free_te):
                    scl.wait_ge(VE.h, chk_qev)
                    scl.wait_ge(TE.h, qsqf)
                    SE.inc(scl.activation(out=qsq[:, :], in_=qstage[:, 0:512],
                                          func=AF.Square))
                chk_qsq1 = SE.n

                @on_vec
                def _(v, bi=bi, chk_qsq1=chk_qsq1):
                    v.wait_ge(SE.h, chk_qsq1)
                    VE.inc(v.reduce_sum(
                        out=c_rn[bi][:, 0:4],
                        in_=qsq.ap().rearrange("p (h d) -> p h d", h=4), axis=AX))
                chk_rn1 = VE.n

                @on_scl
                def _(scl, bi=bi, chk_rn1=chk_rn1):
                    scl.wait_ge(VE.h, chk_rn1)
                    SE.inc(scl.activation(out=qsq[:, :], in_=qstage[:, 512:1024],
                                          func=AF.Square))
                chk_qsq2 = SE.n

                @on_vec
                def _(v, bi=bi, chk_qsq2=chk_qsq2):
                    v.wait_ge(SE.h, chk_qsq2)
                    VE.inc(v.reduce_sum(
                        out=c_rn[bi][:, 4:8],
                        in_=qsq.ap().rearrange("p (h d) -> p h d", h=4), axis=AX))
                chk_rn = VE.n

                @on_vec
                def _(v, bi=bi, chk_rn=chk_rn):
                    v.drain()
                    VE.inc(v.tensor_scalar(out=c_rn[bi][:, 0:8], in0=c_rn[bi][:, 0:8],
                                           scalar1=1e-12, scalar2=None, op0=ALU.max))
                chk_rnf = VE.n

                @on_scl
                def _(scl, bi=bi, chk_rnf=chk_rnf):
                    scl.wait_ge(VE.h, chk_rnf)
                    SE.inc(scl.activation(out=c_rn[bi][:, 8:16],
                                          in_=c_rn[bi][:, 0:8], func=AF.Sqrt))
                chk_rs = SE.n

                @on_vec
                def _(v, j=j, bi=bi, hf=hf, chk_rs=chk_rs):
                    v.wait_ge(SE.h, chk_rs)
                    v.reciprocal(qr1[:, 0:8], c_rn[bi][:, 8:16])
                    v.drain()
                    v.tensor_mul(qr1[:, 0:8], c_rn[bi][:, 0:8], qr1[:, 0:8])
                    v.drain()
                    v.tensor_add(c_rn[bi][:, 8:16], c_rn[bi][:, 8:16], qr1[:, 0:8])
                    v.drain()
                    v.tensor_scalar(out=c_rn[bi][:, 8:16], in0=c_rn[bi][:, 8:16],
                                    scalar1=0.5, scalar2=1e-6, op0=ALU.mult, op1=ALU.add)
                    v.drain()
                    v.reciprocal(c_rn[bi][:, 8:16], c_rn[bi][:, 8:16])
                    v.drain()
                    v.tensor_mul(c_rn[bi][:, 8:16], c_rn[bi][:, 8:16],
                                 alpha_bc[:, hf * 8:hf * 8 + 8])
                    v.drain()
                    q3 = qstage.ap().rearrange("p (h d) -> p h d", h=8)
                    v.tensor_tensor(
                        out=q3, in0=q3,
                        in1=c_rn[bi][:, 8:16, None].broadcast_to([P, 8, HD]),
                        op=ALU.mult)
                    v.drain()
                    x1 = q3[:, :, 0:64]
                    x2 = q3[:, :, 64:128]
                    cT = rqcos[:, j * 64:(j + 1) * 64][:, None, :].broadcast_to([P, 8, 64])
                    sT = rqsin[:, j * 64:(j + 1) * 64][:, None, :].broadcast_to([P, 8, 64])
                    rA3 = qr1.ap().rearrange("p (h d) -> p h d", h=8)[:, :, 0:64]
                    rB3 = qr2.ap().rearrange("p (h d) -> p h d", h=8)[:, :, 0:64]
                    rC3 = qsq.ap().rearrange("p (h d) -> p h d", h=8)[:, :, 0:64]
                    v.tensor_mul(rA3, x1, cT)
                    v.tensor_mul(rB3, x2, sT)
                    v.tensor_mul(rC3, x1, sT)
                    v.drain()
                    v.tensor_sub(x1, rA3, rB3)
                    v.drain()
                    v.tensor_mul(rA3, x2, cT)
                    v.drain()
                    v.tensor_add(x2, rC3, rA3)
                    v.drain()
                    VE.inc(v.tensor_copy(qbf[:, :], qstage[:, :]))
                chk_qbf = VE.n

                @on_ten
                def _(t, bi=bi, chk_qbf=chk_qbf, trf=trq_free):
                    t.wait_ge(VE.h, chk_qbf)
                    t.wait_ge(VE.h, trf)
                    for c in range(8):
                        i = t.transpose(trq[:, 1024 + c * P:1024 + (c + 1) * P],
                                        qbf[:, c * P:(c + 1) * P], ident[:, :])
                    TE.inc(i)
                chk_qtr = TE.n
                qsq_free_te = TE.n

                @on_vec
                def _(v, j=j, hf=hf, chk_qtr=chk_qtr):
                    v.wait_ge(TE.h, chk_qtr)
                    VE.inc(v.tensor_copy(
                        qrT.ap().rearrange("p (h t) -> p h t", h=NH)
                           [:, hf * 8:hf * 8 + 8, j * P:(j + 1) * P],
                        trq[:, 1024:2048].rearrange("p (h q) -> p h q", h=8)))
                trq_free = VE.n

    phase_barrier()

    scope("p3_attn")
    with nc.psum_tensor("S0", [P, 512], F32) as S0, \
         nc.psum_tensor("S1", [P, 512], F32) as S1, \
         nc.psum_tensor("S2", [P, 512], F32) as S2, \
         nc.psum_tensor("S3", [P, 512], F32) as S3, \
         nc.psum_tensor("PTp", [P, D], BF16) as PTp, \
         nc.psum_tensor("Yp", [P, 1024], F32) as Yp:
        Sb = [S0, S1, S2, S3]
        sbank_free = [0, 0, 0, 0]
        ring = 0
        Pb_free = 0
        PTp_free = 0
        Y_free = 0

        for h in range(NH):
            g = h // 4
            for j in range(NT):
                bi = j % 2
                E = EXT[j]
                banks = [(ring + e) % 4 for e in range(E)]
                ring = (ring + E) % 4

                @on_ten
                def _(t, h=h, j=j, g=g, E=E, banks=banks,
                      frees=[sbank_free[b] for b in banks]):
                    for b, f in zip(banks, frees):
                        t.wait_ge(SE.h, f)
                    for e in range(E):
                        i = t.matmul(
                            Sb[banks[e]][:, :],
                            qrT[:, h * TOK + j * P:h * TOK + (j + 1) * P],
                            krT[:, g * T + e * 512:g * T + (e + 1) * 512],
                            start=True, stop=True)
                    TE.inc(i)
                chk_smm = TE.n

                @on_vec
                def _(v, j=j, E=E, banks=banks, chk_smm=chk_smm):
                    v.wait_ge(TE.h, chk_smm)
                    VE.inc(v.tensor_tensor(out=Sb[banks[E - 1]][:, :],
                                           in0=Sb[banks[E - 1]][:, :],
                                           in1=masksb[:, j * 512:(j + 1) * 512],
                                           op=ALU.add))
                chk_mx = VE.n

                @on_scl
                def _(scl, h=h, bi=bi, E=E, banks=banks, chk_mx=chk_mx,
                      pbf=Pb_free):
                    scl.wait_ge(VE.h, chk_mx)
                    scl.wait_ge(TE.h, pbf)
                    for e in range(E):
                        i = scl.activation(out=Pbuf[:, e * 512:(e + 1) * 512],
                                           in_=Sb[banks[e]][:, :], func=AF.Exp,
                                           bias=negaa[:, h:h + 1],
                                           accum_out=laccc[bi][:, e:e + 1])
                    SE.inc(i)
                chk_exp = SE.n
                for b in banks:
                    sbank_free[b] = SE.n

                @on_vec
                def _(v, bi=bi, E=E, chk_exp=chk_exp):
                    v.wait_ge(SE.h, chk_exp)
                    v.reduce_sum(out=c_l[bi], in_=laccc[bi][:, 0:E], axis=AX)
                    v.drain()
                    v.reciprocal(c_rl[bi], c_l[bi])
                    v.drain()
                    VE.inc(v.tensor_scalar(out=Pbuf[:, 0:E * 512],
                                           in0=Pbuf[:, 0:E * 512],
                                           scalar1=c_rl[bi], scalar2=None,
                                           op0=ALU.mult))
                chk_ps = VE.n

                @on_ten
                def _(t, E=E, chk_ps=chk_ps, ptf=PTp_free):
                    t.wait_ge(VE.h, chk_ps)
                    t.wait_ge(VE.h, ptf)
                    for s in range(4 * E):
                        i = t.transpose(PTp[:, s * P:(s + 1) * P],
                                        Pbuf[:, s * P:(s + 1) * P], ident[:, :])
                    TE.inc(i)
                chk_pt = TE.n
                Pb_free = TE.n

                @on_vec
                def _(v, j=j, E=E, chk_pt=chk_pt):
                    v.wait_ge(TE.h, chk_pt)
                    for grp in range(E):
                        s0 = 4 * grp
                        w = PT_W[s0]
                        dst0 = PT_OFF[s0] + (j - AV_JMIN[s0]) * P
                        src = PTp[:, s0 * P:(s0 + 4) * P].rearrange(
                            "p (s q) -> p s q", s=4)
                        dst = PTbuf[:, dst0:dst0 + 4 * w].rearrange(
                            "p (s r) -> p s r", s=4)[:, :, 0:P]
                        i = v.tensor_copy(dst, src)
                    VE.inc(i)
                PTp_free = VE.n
            chk_ptall = VE.n

            @on_ten
            def _(t, h=h, g=g, chk_ptall=chk_ptall, yf=Y_free):
                t.wait_ge(VE.h, chk_ptall)
                t.wait_ge(VE.h, yf)
                first = [True, True]
                for s in range(16):
                    jm = AV_JMIN[s]
                    ranges = []
                    if jm * P < 512:
                        ranges.append((jm * P, 512))
                        ranges.append((512, 1024))
                    else:
                        ranges.append((jm * P, 1024))
                    for (a, b) in ranges:
                        bk = 0 if a < 512 else 1
                        st_flag = first[bk]
                        first[bk] = False
                        stop_flag = (s == 7 and bk == 0) or (s == 15 and bk == 1)
                        i = t.matmul(
                            Yp[:, a:b],
                            vbf[:, s * KV + g * P:s * KV + (g + 1) * P],
                            PTbuf[:, PT_OFF[s] + a - jm * P:
                                  PT_OFF[s] + b - jm * P],
                            start=st_flag, stop=stop_flag)
                TE.inc(i)
            chk_av = TE.n

            @on_vec
            def _(v, h=h, chk_av=chk_av):
                v.wait_ge(TE.h, chk_av)
                VE.inc(v.tensor_copy(yT[:, h * TOK:(h + 1) * TOK], Yp[:, :]))
            Y_free = VE.n

    phase_barrier()

    if _DEBUG:
        @on_gp
        def _(g):
            ST.dma(g, out=dbg_krT, in_=krT[:, :])
            ST.dma(g, out=dbg_v, in_=vbf[:, :])
            ST.dma(g, out=dbg_qrT, in_=qrT[:, :])
            ST.dma(g, out=dbg_yT, in_=yT[:, :])
            ST.wait(g)
        phase_barrier()

    scope("p4_oproj")
    with nc.psum_tensor("op0", [P, 512], F32) as op0q, \
         nc.psum_tensor("op1", [P, 512], F32) as op1q, \
         nc.psum_tensor("op2", [P, 512], F32) as op2q, \
         nc.psum_tensor("op3", [P, 512], F32) as op3q:
        opsl = [op0q, op1q, op2q, op3q]
        ops_free = [0, 0, 0, 0]
        wo_used = [0, 0]

        for qq in range(4):
            wbuf = qq % 2

            @on_gp
            def _(g, qq=qq, wbuf=wbuf, wu=wo_used[wbuf]):
                g.wait_ge(TE.h, wu)
                WD.dma(g, 
                    out=wobf2[wbuf].ap().rearrange("p (c f) -> p c f", f=512),
                    in_=IN.wo[:, qq * 512:(qq + 1) * 512]
                        .rearrange("(c p) f -> p c f", p=P))
            chk_wo = WD.n

            for i8 in range(NT):
                pb = i8 % 4

                @on_ten
                def _(t, qq=qq, i8=i8, pb=pb, wbuf=wbuf, chk_wo=chk_wo,
                      pf=ops_free[pb]):
                    t.wait_ge(WD.h, chk_wo)
                    t.wait_ge(VE.h, pf)
                    for c in range(16):
                        t.matmul(opsl[pb][:, :],
                                 yT[:, c * TOK + i8 * P:c * TOK + (i8 + 1) * P],
                                 wobf2[wbuf][:, c * 512:(c + 1) * 512],
                                 start=(c == 0), stop=False)
                    i = t.matmul(opsl[pb][:, :], onesb_row,
                                 bo_row[:, qq * 512:(qq + 1) * 512],
                                 start=False, stop=True)
                    TE.inc(i)
                chk_op = TE.n
                wo_used[wbuf] = TE.n

                @on_vec
                def _(v, qq=qq, i8=i8, pb=pb, chk_op=chk_op):
                    v.wait_ge(TE.h, chk_op)
                    sl = R[:, i8 * D + qq * 512:i8 * D + (qq + 1) * 512]
                    VE.inc(v.tensor_tensor(out=sl, in0=sl, in1=opsl[pb][:, :],
                                           op=ALU.add))
                ops_free[pb] = VE.n

    phase_barrier()

    scope("p5_ffn0")
    xs_cols = rows_f[:, 336:344]
    rxs_cols = rows_f[:, 344:352]
    ws_cols = rows_f[:, 352:356]
    rws_cols = rows_f[:, 356:360]
    wsrow = f0_wsr[0:1, 0:KV]
    fbrow = f0_wst[0:1, 0:D]

    with nc.psum_tensor("f0tr", [P, D], BF16) as f0tr, \
         nc.psum_tensor("f0bc", [P, KV], F32) as f0bc, \
         nc.psum_tensor("f0p0", [P, KV], F32) as f0p0, \
         nc.psum_tensor("f0p1", [P, KV], F32) as f0p1, \
         nc.psum_tensor("f0p2", [P, KV], F32) as f0p2, \
         nc.psum_tensor("f0p3", [P, KV], F32) as f0p3:
        f0ps = [f0p0, f0p1, f0p2, f0p3]
        f0ps_free = [0, 0, 0, 0]

        @on_sync
        def _(s):
            LD.dma(s, out=fbrow, in_=IN.fb)
        chk_fb = LD.n

        bc_ev = 0
        for qq in range(4):
            @on_ten
            def _(t, qq=qq, chk_fb=chk_fb, ev=bc_ev):
                t.wait_ge(LD.h, chk_fb)
                t.wait_ge(VE.h, ev)
                TE.inc(t.matmul(f0bc[:, :], onesf_row,
                                fbrow[:, qq * 512:(qq + 1) * 512],
                                start=True, stop=True))
            chk_b = TE.n

            @on_vec
            def _(v, qq=qq, chk_b=chk_b):
                v.wait_ge(TE.h, chk_b)
                VE.inc(v.tensor_copy(f0_bbc[:, qq * 512:(qq + 1) * 512], f0bc[:, :]))
            bc_ev = VE.n
        chk_bbc = VE.n

        ln_free_te = 0
        for j in range(NT):
            bi = j % 2
            @on_vec
            def _(v, j=j, bi=bi):
                VE.inc(v.reduce_sum(out=c_s1[bi], in_=R[:, j * D:(j + 1) * D], axis=AX))

            @on_scl
            def _(scl, j=j, bi=bi):
                SE.inc(scl.activation(out=qlnbf[:, :], in_=R[:, j * D:(j + 1) * D],
                                      func=AF.Square, accum_out=c_s2[bi]))
            chk_sq = SE.n

            @on_vec
            def _(v, bi=bi, chk_sq=chk_sq):
                v.wait_ge(SE.h, chk_sq)
                v.drain()
                v.tensor_scalar(out=c_m[bi], in0=c_s1[bi], scalar1=1.0 / D,
                                scalar2=None, op0=ALU.mult)
                v.tensor_scalar(out=c_var[bi], in0=c_s2[bi], scalar1=1.0 / D,
                                scalar2=1e-5, op0=ALU.mult, op1=ALU.add)
                v.drain()
                v.scalar_tensor_tensor(out=c_t0[bi], in0=c_m[bi], scalar=1.0,
                                       in1=c_m[bi], op0=ALU.mult, op1=ALU.mult)
                v.drain()
                VE.inc(v.tensor_sub(c_var[bi], c_var[bi], c_t0[bi]))
            chk_v2 = VE.n

            @on_scl
            def _(scl, bi=bi, chk_v2=chk_v2):
                scl.wait_ge(VE.h, chk_v2)
                SE.inc(scl.activation(out=c_rstd[bi], in_=c_var[bi], func=AF.Sqrt))
            chk_s2 = SE.n

            @on_vec
            def _(v, bi=bi, chk_s2=chk_s2):
                v.wait_ge(SE.h, chk_s2)
                v.drain()
                v.reciprocal(c_nt1[bi], c_rstd[bi])
                v.drain()
                v.tensor_mul(c_nt2[bi], c_var[bi], c_nt1[bi])
                v.drain()
                v.tensor_add(c_rstd[bi], c_rstd[bi], c_nt2[bi])
                v.drain()
                v.tensor_scalar(out=c_rstd[bi], in0=c_rstd[bi], scalar1=0.5,
                                scalar2=None, op0=ALU.mult)
                v.drain()
                v.reciprocal(c_rstd[bi], c_rstd[bi])
                v.drain()
                VE.inc(v.scalar_tensor_tensor(out=c_nmr[bi], in0=c_m[bi], scalar=-1.0,
                                              in1=c_rstd[bi], op0=ALU.mult,
                                              op1=ALU.mult))
            chk_v3 = VE.n

            @on_scl
            def _(scl, j=j, bi=bi, chk_v3=chk_v3, lf=ln_free_te):
                scl.wait_ge(VE.h, chk_v3)
                scl.wait_ge(TE.h, lf)
                SE.inc(scl.activation(out=f0_ln[:, :], in_=R[:, j * D:(j + 1) * D],
                                      func=AF.Identity, bias=c_nmr[bi],
                                      scale=c_rstd[bi]))
            chk_ln = SE.n

            @on_vec
            def _(v, j=j, bi=bi, chk_ln=chk_ln, xf=ln_free_te):
                v.wait_ge(SE.h, chk_ln)
                v.wait_ge(TE.h, xf)
                v.reduce_max(out=xs_cols[:, j:j + 1], in_=f0_ln[:, :], axis=AX,
                             apply_absolute_value=True)
                v.drain()
                v.tensor_scalar(out=xs_cols[:, j:j + 1], in0=xs_cols[:, j:j + 1],
                                scalar1=1.0 / 127.0, scalar2=1e-12,
                                op0=ALU.mult, op1=ALU.max)
                v.drain()
                v.reciprocal(rxs_cols[:, j:j + 1], xs_cols[:, j:j + 1])
                v.drain()
                v.tensor_scalar(out=f0_ln[:, :], in0=f0_ln[:, :],
                                scalar1=rxs_cols[:, j:j + 1], scalar2=MAGIC,
                                op0=ALU.mult, op1=ALU.add)
                v.drain()
                VE.inc(v.tensor_scalar(out=f0_xq[:, :], in0=f0_ln[:, :],
                                       scalar1=-MAGIC, scalar2=None, op0=ALU.add))
            chk_xq = VE.n

            @on_ten
            def _(t, j=j, chk_xq=chk_xq):
                t.wait_ge(VE.h, chk_xq)
                for c in range(16):
                    i = t.transpose(f0tr[:, c * P:(c + 1) * P],
                                    f0_xq[:, c * P:(c + 1) * P], ident[:, :])
                TE.inc(i)
            chk_xtr = TE.n
            ln_free_te = TE.n

            @on_vec
            def _(v, j=j, chk_xtr=chk_xtr):
                v.wait_ge(TE.h, chk_xtr)
                VE.inc(v.tensor_copy(
                    f0_xqT.ap().rearrange("p (c t) -> p c t", c=16)[:, :, j * P:(j + 1) * P],
                    f0tr.ap().rearrange("p (c q) -> p c q", c=16)))

        for j in range(NT):
            @on_vec
            def _(v, j=j, chk_bbc=chk_bbc):
                v.wait_ge(VE.h, chk_bbc)
                VE.inc(v.tensor_tensor(out=R[:, j * D:(j + 1) * D],
                                       in0=R[:, j * D:(j + 1) * D],
                                       in1=f0_bbc[:, :], op=ALU.add))

        wst_free = 0
        fb_done_te = TE.n
        wqn_free = 0
        for ot in range(4):
            @on_gp
            def _(g, ot=ot, wf=wst_free, fbd=fb_done_te):
                g.wait_ge(VE.h, wf)
                g.wait_ge(TE.h, fbd)
                WD.dma(g, 
                    out=f0_wst.ap().rearrange("p (c d) -> p c d", c=4),
                    in_=IN.fw[ot * 512:(ot + 1) * 512, :]
                        .rearrange("(c p) d -> p c d", p=P))
            chk_w = WD.n

            @on_vec
            def _(v, ot=ot, chk_w=chk_w, qf=wqn_free):
                v.wait_ge(WD.h, chk_w)
                v.wait_ge(TE.h, qf)
                for c in range(4):
                    v.reduce_max(out=ws_cols[:, c:c + 1],
                                 in_=f0_wst[:, c * D:(c + 1) * D], axis=AX,
                                 apply_absolute_value=True)
                v.drain()
                v.tensor_scalar(out=ws_cols[:, :], in0=ws_cols[:, :],
                                scalar1=1e-8, scalar2=1.0 / 127.0,
                                op0=ALU.max, op1=ALU.mult)
                v.drain()
                v.reciprocal(rws_cols[:, :], ws_cols[:, :])
                v.drain()
                for c in range(4):
                    v.tensor_scalar(out=f0_wst[:, c * D:(c + 1) * D],
                                    in0=f0_wst[:, c * D:(c + 1) * D],
                                    scalar1=rws_cols[:, c:c + 1], scalar2=MAGIC,
                                    op0=ALU.mult, op1=ALU.add)
                v.drain()
                for c in range(4):
                    i = v.tensor_scalar(out=f0_wqn[:, c * D:(c + 1) * D],
                                        in0=f0_wst[:, c * D:(c + 1) * D],
                                        scalar1=-MAGIC, scalar2=None, op0=ALU.add)
                VE.inc(i)
            chk_wq2 = VE.n
            wst_free = VE.n

            @on_gp
            def _(g, ot=ot, chk_wq2=chk_wq2):
                g.wait_ge(VE.h, chk_wq2)
                WD.dma(g, out=wsband, in_=ws_cols)
                g.wait_ge(WD.h, WD.n + 0)
                WD.dma(g, 
                    out=wsrow.rearrange("o (c p) -> o c p", c=4),
                    in_=wsband.rearrange("p c -> c p").unsqueeze(0),
                    allow_slow_non_contiguous=True)
            chk_wsr = WD.n

            @on_ten
            def _(t, chk_wsr=chk_wsr):
                t.wait_ge(WD.h, chk_wsr)
                t.wait_ge(VE.h, VE.n)
                TE.inc(t.matmul(f0bc[:, :], onesf_row, wsrow, start=True, stop=True))
            chk_wsbc = TE.n

            @on_vec
            def _(v, chk_wsbc=chk_wsbc):
                v.wait_ge(TE.h, chk_wsbc)
                VE.inc(v.tensor_copy(f0_wsbc[:, :], f0bc[:, :]))
            chk_wsbc2 = VE.n

            f0tr_free = VE.n
            for oc in range(4):
                @on_ten
                def _(t, ot=ot, oc=oc, chk_wq2=chk_wq2, trf=f0tr_free):
                    t.wait_ge(VE.h, chk_wq2)
                    t.wait_ge(VE.h, trf)
                    for dc in range(16):
                        i = t.transpose(
                            f0tr[:, dc * P:(dc + 1) * P],
                            f0_wqn[:, oc * D + dc * P:oc * D + (dc + 1) * P],
                            ident[:, :])
                    TE.inc(i)
                chk_wtr = TE.n

                @on_vec
                def _(v, oc=oc, chk_wtr=chk_wtr):
                    v.wait_ge(TE.h, chk_wtr)
                    VE.inc(v.tensor_copy(
                        f0_wqT.ap().rearrange("p (d o) -> p d o", d=16)[:, :, oc * P:(oc + 1) * P],
                        f0tr.ap().rearrange("p (d q) -> p d q", d=16)))
                f0tr_free = VE.n
            chk_wqT = VE.n
            wqn_free = TE.n

            for i8 in range(NT):
                pb = i8 % 4

                @on_ten
                def _(t, i8=i8, pb=pb, chk_wqT=chk_wqT, pf=f0ps_free[pb]):
                    t.wait_ge(VE.h, chk_wqT)
                    t.wait_ge(VE.h, pf)
                    for c in range(16):
                        i = t.matmul(f0ps[pb][:, :],
                                     f0_xqT[:, c * 1024 + i8 * P:c * 1024 + (i8 + 1) * P],
                                     f0_wqT[:, c * 512:(c + 1) * 512],
                                     start=(c == 0), stop=(c == 15))
                    TE.inc(i)
                chk_mm = TE.n

                @on_vec
                def _(v, ot=ot, i8=i8, pb=pb, chk_mm=chk_mm, chk_wsbc2=chk_wsbc2):
                    v.wait_ge(TE.h, chk_mm)
                    v.wait_ge(VE.h, chk_wsbc2)
                    v.drain()
                    v.scalar_tensor_tensor(out=f0_tmp[:, :], in0=f0ps[pb][:, :],
                                           scalar=xs_cols[:, i8:i8 + 1],
                                           in1=f0_wsbc[:, :],
                                           op0=ALU.mult, op1=ALU.mult)
                    v.drain()
                    sl = R[:, i8 * D + ot * 512:i8 * D + (ot + 1) * 512]
                    VE.inc(v.tensor_tensor(out=sl, in0=sl, in1=f0_tmp[:, :],
                                           op=ALU.add))
                f0ps_free[pb] = VE.n

    if _DEBUG:
        @on_gp
        def _(g):
            g.wait_ge(VE.h, VE.n); g.wait_ge(TE.h, TE.n); g.wait_ge(SE.h, SE.n)
            ST.dma(g, out=dbg_xqT, in_=f0_xqT[:, :])
            ST.dma(g, out=dbg_wqT, in_=f0_wqT[:, :])
            ST.dma(g, out=dbg_rows, in_=rows_f[:, :])
            ST.wait(g)

    phase_barrier()

    scope("p6_lattn")
    a1_lnT = sb.at(OFF_OVL + 0, "a1_lnT", (P, 16 * 512), BF16)
    a1_e = sb.at(OFF_OVL + 0, "a1_e", (P, 16 * 512), BF16)
    a1_wo = sb.at(OFF_OVL + 0, "a1_wo", (P, 16 * 512), BF16)
    a1_q = sb.at(OFF_OVL + 16 * KB, "a1_q", (P, 16 * 512), F32)
    a1_k = sb.at(OFF_OVL + 48 * KB, "a1_k", (P, 16 * 512), F32)
    a1_o = sb.at(OFF_OVL + 48 * KB, "a1_o", (P, 16 * 512), BF16)
    a1_v = sb.at(OFF_OVL + 80 * KB, "a1_v", (P, 16 * 512), F32)
    a1_wst = [sb.at(OFF_MASK + i * 4 * KB, f"a1_wst{i}", (P, 16 * P), BF16)
              for i in range(2)]
    a1_meta = sb.at(OFF_OVL + 112 * KB, "a1_meta", (P, 512), F32)
    a1_hrow = sb.at(OFF_OVL + 0, "a1_hrow", (16, 16 * P), F32)
    a1_hsel = sb.at(OFF_OVL + 116 * KB, "a1_hsel", (P, 256), BF16)
    a1_pub = sb.at(OFF_OVL + 114 * KB, "a1_pub", (P, 256), F32)
    a1_sta = sb.at(OFF_ROWSB, "a1_sta", (P, 512), F32)
    a1_ini = sb.at(OFF_OVL + 115 * KB, "a1_ini", (P, 256), F32)
    a1_t1 = sb.at(OFF_JUNK, "a1_t1", (P, 512), F32)
    a1_t2 = sb.at(OFF_JUNK + 2 * KB, "a1_t2", (P, 512), F32)
    a1_scr = sb.at(OFF_JUNK, "a1_scr", (P, 1024), F32)
    dcol = alpha_bc
    lnd_row = rows_f[0:1, 400:416]
    gxrow = rows_f[0:1, 144:400]

    hsel_np = np.zeros((P, 256), dtype=ml_dtypes.bfloat16)
    for _h in range(16):
        hsel_np[:, _h * 16 + _h] = 1.0
    hrow_np = np.zeros((16, 16 * P), dtype=np.float32)
    for _h in range(16):
        hrow_np[_h, _h * P:(_h + 1) * P] = 1.0
    hsel_dram = nc.inline_tensor(hsel_np, name="hselc").ap()
    hrow_dram = nc.inline_tensor(hrow_np, name="hrowc").ap()

    a1_spill = nc.dram_tensor("a1_spill", [3 * 2048, 512], F32).ap()
    cc_in = nc.dram_tensor("cc_in", [P, 256], F32).ap()
    cc_out = nc.dram_tensor("cc_out", [2 * P, 256], F32).ap()

    with nc.psum_tensor("a1tr", [P, D], BF16) as a1tr, \
         nc.psum_tensor("a1p0", [P, 512], F32) as a1p0, \
         nc.psum_tensor("a1p1", [P, 512], F32) as a1p1, \
         nc.psum_tensor("a1p2", [P, 512], F32) as a1p2, \
         nc.psum_tensor("a1p3", [P, 512], F32) as a1p3, \
         nc.psum_tensor("a1dn", [P, 512], F32) as a1dn, \
         nc.psum_tensor("a1bc", [P, 512], F32) as a1bc:
        a1ps = [a1p0, a1p1, a1p2, a1p3]
        a1ps_free = [0, 0, 0, 0]

        @on_sync
        def _(s):
            LD.dma(s, out=rows_f[0:1, 432:448], in_=IN.decay)
        chk_dp = LD.n

        @on_scl
        def _(scl, chk_dp=chk_dp):
            scl.wait_ge(LD.h, chk_dp)
            SE.inc(scl.activation(out=rows_f[0:1, 448:464], in_=rows_f[0:1, 432:448],
                                  func=AF.Sigmoid))
        chk_sg = SE.n

        @on_vec
        def _(v, chk_sg=chk_sg):
            v.wait_ge(SE.h, chk_sg)
            VE.inc(v.tensor_scalar(out=rows_f[0:1, 464:480], in0=rows_f[0:1, 448:464],
                                   scalar1=1e-30, scalar2=None, op0=ALU.max))
        chk_dmx = VE.n

        @on_scl
        def _(scl, chk_dmx=chk_dmx):
            scl.wait_ge(VE.h, chk_dmx)
            SE.inc(scl.activation(out=lnd_row, in_=rows_f[0:1, 464:480], func=AF.Ln))
        chk_lnd = SE.n

        @on_ten
        def _(t, chk_sg=chk_sg):
            t.wait_ge(SE.h, chk_sg)
            t.wait_ge(VE.h, VE.n)
            TE.inc(t.matmul(a1dn[:, 0:16], onesf_row, rows_f[0:1, 448:464],
                            start=True, stop=True))
        chk_dc = TE.n

        @on_vec
        def _(v, chk_dc=chk_dc):
            v.wait_ge(TE.h, chk_dc)
            VE.inc(v.tensor_copy(dcol[:, :], a1dn[:, 0:16]))
        chk_dcol = VE.n

        wu = [0, 0]
        spill_reload_chk = 0
        for half in range(2):
            trf = VE.n
            lnf = TE.n
            for jj in range(4):
                j = half * 4 + jj
                bi = j % 2
                @on_vec
                def _(v, j=j, bi=bi, srk=spill_reload_chk):
                    if j % 4 == 0:
                        v.wait_ge(ST.h, srk)
                    VE.inc(v.reduce_sum(out=c_s1[bi], in_=R[:, j * D:(j + 1) * D],
                                        axis=AX))

                @on_scl
                def _(scl, j=j, bi=bi, lf=lnf, hb=VE.n):
                    scl.wait_ge(TE.h, lf)
                    scl.wait_ge(VE.h, hb)
                    SE.inc(scl.activation(out=qlnbf[:, :], in_=R[:, j * D:(j + 1) * D],
                                          func=AF.Square, accum_out=c_s2[bi]))
                chk_sq = SE.n

                @on_vec
                def _(v, bi=bi, chk_sq=chk_sq):
                    v.wait_ge(SE.h, chk_sq)
                    v.drain()
                    v.tensor_scalar(out=c_m[bi], in0=c_s1[bi], scalar1=1.0 / D,
                                    scalar2=None, op0=ALU.mult)
                    v.tensor_scalar(out=c_var[bi], in0=c_s2[bi], scalar1=1.0 / D,
                                    scalar2=1e-5, op0=ALU.mult, op1=ALU.add)
                    v.drain()
                    v.scalar_tensor_tensor(out=c_t0[bi], in0=c_m[bi], scalar=1.0,
                                           in1=c_m[bi], op0=ALU.mult, op1=ALU.mult)
                    v.drain()
                    VE.inc(v.tensor_sub(c_var[bi], c_var[bi], c_t0[bi]))
                chk_v2 = VE.n

                @on_scl
                def _(scl, bi=bi, chk_v2=chk_v2):
                    scl.wait_ge(VE.h, chk_v2)
                    SE.inc(scl.activation(out=c_rstd[bi], in_=c_var[bi], func=AF.Sqrt))
                chk_s2 = SE.n

                @on_vec
                def _(v, bi=bi, chk_s2=chk_s2):
                    v.wait_ge(SE.h, chk_s2)
                    v.drain()
                    v.reciprocal(c_nt1[bi], c_rstd[bi])
                    v.drain()
                    v.tensor_mul(c_nt2[bi], c_var[bi], c_nt1[bi])
                    v.drain()
                    v.tensor_add(c_rstd[bi], c_rstd[bi], c_nt2[bi])
                    v.drain()
                    v.tensor_scalar(out=c_rstd[bi], in0=c_rstd[bi], scalar1=0.5,
                                    scalar2=None, op0=ALU.mult)
                    v.drain()
                    v.reciprocal(c_rstd[bi], c_rstd[bi])
                    v.drain()
                    VE.inc(v.scalar_tensor_tensor(out=c_nmr[bi], in0=c_m[bi],
                                                  scalar=-1.0, in1=c_rstd[bi],
                                                  op0=ALU.mult, op1=ALU.mult))
                chk_v3 = VE.n

                @on_scl
                def _(scl, j=j, bi=bi, chk_v3=chk_v3, lf=lnf):
                    scl.wait_ge(VE.h, chk_v3)
                    scl.wait_ge(TE.h, lf)
                    SE.inc(scl.activation(out=qlnbf[:, :],
                                          in_=R[:, j * D:(j + 1) * D],
                                          func=AF.Identity, bias=c_nmr[bi],
                                          scale=c_rstd[bi]))
                chk_ln = SE.n

                @on_ten
                def _(t, chk_ln=chk_ln, trf=trf):
                    t.wait_ge(SE.h, chk_ln)
                    t.wait_ge(VE.h, trf)
                    for c in range(16):
                        i = t.transpose(a1tr[:, c * P:(c + 1) * P],
                                        qlnbf[:, c * P:(c + 1) * P], ident[:, :])
                    TE.inc(i)
                chk_tr = TE.n
                lnf = TE.n

                @on_vec
                def _(v, jj=jj, chk_tr=chk_tr):
                    v.wait_ge(TE.h, chk_tr)
                    VE.inc(v.tensor_copy(
                        a1_lnT.ap().rearrange("p (c t) -> p c t", c=16)
                             [:, :, jj * P:(jj + 1) * P],
                        a1tr.ap().rearrange("p (c q) -> p c q", c=16)))
                trf = VE.n
            chk_lnTall = VE.n

            for fc in range(48):
                wb = fc % 2
                pb = fc % 4

                @on_gp
                def _(g, half=half, fc=fc, wb=wb, wuv=wu[wb], srk=spill_reload_chk):
                    g.wait_ge(TE.h, wuv)
                    g.wait_ge(ST.h, srk)
                    WDs[wb].dma(g, 
                        out=a1_wst[wb].ap().rearrange("p (c f) -> p c f", f=P),
                        in_=IN.wqkv[fc * D:(fc + 1) * D, :]
                            .rearrange("(c p) f -> p c f", p=P))
                chk_wl = WDs[wb].n

                @on_ten
                def _(t, fc=fc, wb=wb, pb=pb, chk_wl=chk_wl, chk_lnTall=chk_lnTall,
                      pf=a1ps_free[pb]):
                    t.wait_ge(WDs[wb].h, chk_wl)
                    t.wait_ge(VE.h, chk_lnTall)
                    t.wait_ge(VE.h, pf)
                    for c in range(16):
                        i = t.matmul(a1ps[pb][:, :],
                                     a1_wst[wb][:, c * P:(c + 1) * P],
                                     a1_lnT[:, c * 512:(c + 1) * 512],
                                     start=(c == 0), stop=(c == 15))
                    TE.inc(i)
                chk_mm = TE.n
                wu[wb] = TE.n

                if fc < 32:
                    dstbuf = a1_q if fc < 16 else a1_k
                    dc = fc % 16

                    @on_vec
                    def _(v, pb=pb, chk_mm=chk_mm):
                        v.wait_ge(TE.h, chk_mm)
                        v.drain()
                        v.tensor_scalar(out=a1_t1[:, :], in0=a1ps[pb][:, :],
                                        scalar1=0.0, scalar2=None, op0=ALU.max)
                        v.drain()
                        VE.inc(v.tensor_sub(a1_t2[:, :], a1ps[pb][:, :], a1_t1[:, :]))
                    chk_p1 = VE.n

                    @on_scl
                    def _(scl, dc=dc, dstbuf=dstbuf, chk_p1=chk_p1):
                        scl.wait_ge(VE.h, chk_p1)
                        SE.inc(scl.activation(out=dstbuf[:, dc * 512:(dc + 1) * 512],
                                              in_=a1_t2[:, :], func=AF.Exp))
                    chk_p2 = SE.n

                    @on_vec
                    def _(v, dc=dc, dstbuf=dstbuf, chk_p2=chk_p2):
                        v.wait_ge(SE.h, chk_p2)
                        sl = dstbuf[:, dc * 512:(dc + 1) * 512]
                        VE.inc(v.tensor_tensor(out=sl, in0=sl, in1=a1_t1[:, :],
                                               op=ALU.add))
                    a1ps_free[pb] = VE.n
                else:
                    dc = fc - 32

                    @on_vec
                    def _(v, pb=pb, dc=dc, chk_mm=chk_mm):
                        v.wait_ge(TE.h, chk_mm)
                        VE.inc(v.tensor_copy(a1_v[:, dc * 512:(dc + 1) * 512],
                                             a1ps[pb][:, :]))
                    a1ps_free[pb] = VE.n
            chk_qkv = VE.n

            for dc in range(16):
                @on_vec
                def _(v, dc=dc, chk_qkv=chk_qkv):
                    if dc == 0:
                        v.wait_ge(VE.h, chk_qkv)
                        v.drain()
                    sl = a1_v[:, dc * 512:(dc + 1) * 512]
                    VE.inc(v.tensor_tensor(out=sl,
                                           in0=a1_k[:, dc * 512:(dc + 1) * 512],
                                           in1=sl, op=ALU.mult))
            chk_kv = VE.n

            @on_vec
            def _(v, half=half, chk_kv=chk_kv, chk_dcol=chk_dcol):
                v.wait_ge(VE.h, chk_kv)
                v.wait_ge(VE.h, chk_dcol)
                jf = a1_scr.ap()
                for s, buf in ((0, a1_v), (1, a1_k)):
                    for cc in range(4):
                        gc = half * 4 + cc
                        for hg in range(2):
                            v.drain()
                            for hh in range(8):
                                h = hg * 8 + hh
                                v.tensor_tensor_scan(
                                    out=jf[:, hh * P:(hh + 1) * P],
                                    data0=dcol[:, h:h + 1].broadcast_to([P, P]),
                                    data1=buf[:, h * 512 + cc * P:
                                              h * 512 + (cc + 1) * P],
                                    initial=0.0, op0=ALU.mult, op1=ALU.add)
                            v.drain()
                            src8 = jf.rearrange("p (a b) -> p a b", a=8)[:, :, P - 1:P]
                            dst8 = a1_pub.ap().rearrange(
                                "p (s h c) -> p s h c", s=2, h=16)[
                                :, s:s + 1, hg * 8:hg * 8 + 8, gc:gc + 1]
                            v.tensor_copy(dst8, src8.unsqueeze(1))
                i = v.tensor_scalar(out=a1_pub[:, 0:1], in0=a1_pub[:, 0:1],
                                    scalar1=1.0, scalar2=None, op0=ALU.mult)
                v.drain()
                VE.inc(i)
            chk_pub = VE.n

            if half == 0:
                @on_sync
                def _(s, chk=chk_pub):
                    s.wait_ge(VE.h, chk)
                    ST.dma(s, out=a1_spill[0:2048, :], in_=a1_q[:, :])
                    ST.dma(s, out=a1_spill[2048:4096, :], in_=a1_k[:, :])
                    ST.dma(s, out=a1_spill[4096:6144, :], in_=a1_v[:, :])
                spill_reload_chk = ST.n

        chk_pub_all = VE.n

        @on_gp
        def _(g, chk=chk_pub_all):
            g.wait_ge(VE.h, chk)
            WD.dma(g, out=cc_in, in_=a1_pub[:, :])
            g.wait_ge(WD.h, WD.n)
            g.collective_compute(
                "AllGather", ALU.bypass,
                replica_groups=[[0, 1], [2, 3], [4, 5], [6, 7]],
                ins=[cc_in], outs=[cc_out],
            ).then_inc(CC.h, 1)
            CC.n += 1
            g.wait_ge(CC.h, CC.n)
            WD.dma(g, out=a1_sta[:, 0:256], in_=cc_out[0:P, :])
            WD.dma(g, out=a1_sta[:, 256:512], in_=cc_out[P:2 * P, :])
        chk_sta = WD.n

        meta_chk = 0
        for j in range(NT):
            @on_sync
            def _(s, j=j, mk=meta_chk):
                s.wait_ge(TE.h, wu[0])
                s.wait_ge(TE.h, wu[1])
                s.wait_ge(VE.h, mk)
                LD.dma(s, out=a1_meta[0:1, :],
                                   in_=IN.gapexp[j:j + 1, :])
            chk_meta = LD.n

            @on_vec
            def _(v, j=j, chk_meta=chk_meta, chk_lnd=chk_lnd):
                v.wait_ge(LD.h, chk_meta)
                v.wait_ge(SE.h, chk_lnd)
                lr = lnd_row[:, None, :, None].broadcast_to([1, 2, 16, 8])
                v.tensor_tensor(
                    out=gxrow.rearrange("o (r h c) -> o r h c", r=2, h=16),
                    in0=a1_meta[0:1, 0:256].rearrange("o (r h c) -> o r h c", r=2, h=16),
                    in1=lr, op=ALU.mult)
                v.drain()
                VE.inc(v.tensor_tensor(out=gxrow, in0=gxrow,
                                       in1=a1_meta[0:1, 256:512], op=ALU.add))
            chk_gx = VE.n

            @on_scl
            def _(scl, chk_gx=chk_gx):
                scl.wait_ge(VE.h, chk_gx)
                SE.inc(scl.activation(out=gxrow, in_=gxrow, func=AF.Exp))
            chk_cf = SE.n

            @on_ten
            def _(t, chk_cf=chk_cf, ev=VE.n):
                t.wait_ge(SE.h, chk_cf)
                t.wait_ge(VE.h, ev)
                TE.inc(t.matmul(a1bc[:, 0:256], onesf_row, gxrow,
                                start=True, stop=True))
            chk_cbc = TE.n

            @on_vec
            def _(v, j=j, chk_cbc=chk_cbc, chk_sta=chk_sta):
                v.wait_ge(TE.h, chk_cbc)
                v.wait_ge(WD.h, chk_sta)
                for s in range(2):
                    sl = a1_sta.ap().rearrange("p (r s h c) -> p r s h c", r=2, s=2,
                                               h=16)[:, :, s, :, :]
                    v.tensor_tensor(
                        out=a1_scr[:, 0:256].rearrange("p (r h c) -> p r h c",
                                                       r=2, h=16),
                        in0=sl, in1=a1bc[:, 0:256].rearrange(
                            "p (r h c) -> p r h c", r=2, h=16),
                        op=ALU.mult)
                    v.drain()
                    v.reduce_sum(out=a1_scr[:, 256:288],
                                 in_=a1_scr[:, 0:256].rearrange(
                                     "p (g c) -> p g c", g=32), axis=AX)
                    v.drain()
                    v.tensor_add(a1_ini[:, j * 32 + s * 16:j * 32 + (s + 1) * 16],
                                 a1_scr[:, 256:272], a1_scr[:, 272:288])
                    v.drain()
                i = v.tensor_scalar(out=a1_ini[:, j * 32:j * 32 + 1],
                                    in0=a1_ini[:, j * 32:j * 32 + 1],
                                    scalar1=1.0, scalar2=None, op0=ALU.mult)
                VE.inc(i)
            meta_chk = VE.n
        chk_ini = VE.n

        oproj_wu = 0
        for half in (1, 0):
            @on_sync
            def _(s, te=TE.n, ve=VE.n):
                s.wait_ge(TE.h, te)
                s.wait_ge(VE.h, ve)
                LD.dma(s, out=a1_hsel[:, :], in_=hsel_dram)
            chk_hs = LD.n
            if half == 0:
                @on_sync
                def _(s, ou=oproj_wu, vv=VE.n, ss=SE.n):
                    s.wait_ge(TE.h, ou)
                    s.wait_ge(VE.h, vv)
                    s.wait_ge(SE.h, ss)
                    LD.dma(s, out=a1_q[:, :], in_=a1_spill[0:2048, :])
                    LD.dma(s, out=a1_k[:, :], in_=a1_spill[2048:4096, :])
                    LD.dma(s, out=a1_v[:, :], in_=a1_spill[4096:6144, :])
                chk_rel = LD.n
            else:
                chk_rel = LD.n

            @on_vec
            def _(v, half=half, chk_rel=chk_rel, chk_ini=chk_ini):
                v.wait_ge(LD.h, chk_rel)
                v.wait_ge(VE.h, chk_ini)
                for s, buf in ((0, a1_v), (1, a1_k)):
                    for cc in range(4):
                        j = half * 4 + cc
                        for h in range(16):
                            sl = buf[:, h * 512 + cc * P:h * 512 + (cc + 1) * P]
                            v.tensor_tensor_scan(
                                out=sl,
                                data0=dcol[:, h:h + 1].broadcast_to([P, P]),
                                data1=sl,
                                initial=a1_ini[:, j * 32 + s * 16 + h:
                                               j * 32 + s * 16 + h + 1],
                                op0=ALU.mult, op1=ALU.add)
                v.drain()
                i = v.tensor_scalar(out=c_t0[0], in0=c_t0[0],
                                    scalar1=1.0, scalar2=None, op0=ALU.mult)
                VE.inc(i)
            chk_ts = VE.n

            for dc in range(16):
                @on_vec
                def _(v, dc=dc, chk_ts=chk_ts, ou=oproj_wu):
                    if dc == 0:
                        v.wait_ge(VE.h, chk_ts)
                        v.wait_ge(TE.h, ou)
                    VE.inc(v.tensor_tensor(out=a1_e[:, dc * 512:(dc + 1) * 512],
                                           in0=a1_q[:, dc * 512:(dc + 1) * 512],
                                           in1=a1_k[:, dc * 512:(dc + 1) * 512],
                                           op=ALU.mult))
            chk_e = VE.n

            @on_ten
            def _(t, chk_e=chk_e, chk_hs=chk_hs, dnf=VE.n):
                t.wait_ge(VE.h, chk_e)
                t.wait_ge(LD.h, chk_hs)
                for h in range(16):
                    i = t.matmul(a1dn[0:16, :], a1_hsel[:, h * 16:(h + 1) * 16],
                                 a1_e[:, h * 512:(h + 1) * 512],
                                 start=(h == 0), stop=(h == 15))
                TE.inc(i)
            chk_dn = TE.n

            @on_sync
            def _(s, chk_dn=chk_dn):
                s.wait_ge(TE.h, chk_dn)
                LD.dma(s, out=a1_hrow[:, :], in_=hrow_dram)
            chk_hr = LD.n

            @on_vec
            def _(v, chk_dn=chk_dn):
                v.wait_ge(TE.h, chk_dn)
                v.tensor_copy(a1_t2[0:16, :], a1dn[0:16, :])
                v.drain()
                v.tensor_scalar(out=a1_t2[0:16, :], in0=a1_t2[0:16, :],
                                scalar1=1e-6, scalar2=None, op0=ALU.max)
                v.drain()
                VE.inc(v.reciprocal(a1_t2[0:16, :], a1_t2[0:16, :]))
            chk_rd = VE.n

            for h in range(16):
                @on_ten
                def _(t, h=h, chk_rd=chk_rd, bf=VE.n, chk_hr=chk_hr):
                    if h == 0:
                        t.wait_ge(LD.h, chk_hr)
                    t.wait_ge(VE.h, chk_rd)
                    t.wait_ge(VE.h, bf)
                    TE.inc(t.matmul(a1bc[:, :], a1_hrow[:, h * P:(h + 1) * P],
                                    a1_t2[0:16, :], start=True, stop=True))
                chk_bc = TE.n

                @on_vec
                def _(v, h=h, chk_bc=chk_bc):
                    v.wait_ge(TE.h, chk_bc)
                    sl = a1_q[:, h * 512:(h + 1) * 512]
                    v.tensor_tensor(out=sl, in0=sl,
                                    in1=a1_v[:, h * 512:(h + 1) * 512], op=ALU.mult)
                    v.drain()
                    v.tensor_tensor(out=sl, in0=sl, in1=a1bc[:, :], op=ALU.mult)
                    v.drain()
                    VE.inc(v.tensor_copy(a1_o[:, h * 512:(h + 1) * 512], sl))
            chk_out = VE.n

            wo_used = TE.n
            for oq in range(4):
                @on_gp
                def _(g, oq=oq, wo_used=wo_used, chk_out=chk_out):
                    g.wait_ge(TE.h, wo_used)
                    g.wait_ge(VE.h, chk_out)
                    WDs[oq % 2].dma(g, 
                        out=a1_wo.ap().rearrange("p (c f) -> p c f", f=512),
                        in_=IN.wout[oq * D:(oq + 1) * D, :]
                            .rearrange("(c p) f -> p c f", p=P))
                chk_wo = WDs[oq % 2].n

                for it in range(4):
                    i8 = half * 4 + it
                    pb = it % 4

                    @on_ten
                    def _(t, oq=oq, it=it, pb=pb, oq2=oq % 2, chk_wo=chk_wo,
                          chk_out=chk_out, pf=a1ps_free[pb]):
                        t.wait_ge(WDs[oq2].h, chk_wo)
                        t.wait_ge(VE.h, chk_out)
                        t.wait_ge(VE.h, pf)
                        for c in range(16):
                            i = t.matmul(a1ps[pb][:, :],
                                         a1_o[:, c * 512 + it * P:c * 512 + (it + 1) * P],
                                         a1_wo[:, c * 512:(c + 1) * 512],
                                         start=(c == 0), stop=(c == 15))
                        TE.inc(i)
                    chk_om = TE.n
                    wo_used = TE.n

                    @on_vec
                    def _(v, i8=i8, oq=oq, pb=pb, chk_om=chk_om):
                        v.wait_ge(TE.h, chk_om)
                        sl = R[:, i8 * D + oq * 512:i8 * D + (oq + 1) * 512]
                        VE.inc(v.tensor_tensor(out=sl, in0=sl, in1=a1ps[pb][:, :],
                                               op=ALU.add))
                    a1ps_free[pb] = VE.n
            oproj_wu = TE.n

    phase_barrier()

    scope("p7_ffn1")
    r_spill = nc.dram_tensor("r_spill", [TOK, D], F32).ap()
    SQ2 = float(np.sqrt(2.0))
    SC_DN = float(64.0 * np.sqrt(2.0))

    x1stage = sb.at(0, "x1stage", (P, 16 * 512), F32)
    h7 = [sb.at(0, "h7a", (P, 64 * 512), BF16),
          sb.at(147296, "h7b", (P, 64 * 512), BF16)]
    ln3 = sb.at(OFF_MASK, "ln3", (P, 4 * D), BF16)
    jdump = sb.at(64 * KB, "jdump7", (P, D), BF16)
    zt7 = [sb.at(97 * KB + 512 + i * KB, f"zt7_{i}", (P, 512), BF16)
           for i in range(2)]
    bscr = sb.at(99 * KB + 512, "bscr", (P, 512), F32)
    bscrb = sb.at(99 * KB + 512, "bscrb", (P, 512), BF16)
    x1T = [sb.at(101 * KB + 512 + i * 16 * KB, f"x1T{i}", (P, 16 * 512), BF16)
           for i in range(2)]
    sil7 = [sb.at(64 * KB + i * 2 * KB, f"sil7_{i}", (P, 512), F32)
            for i in range(2)]
    utmp7 = sb.at(68 * KB, "utmp7", (P, 512), F32)
    etmp7 = [sb.at(64 * KB + i * 2 * KB, f"etmp7_{i}", (P, 512), F32)
             for i in range(2)]
    su7 = sb.at(70 * KB, "su7", (P, 64), F32)
    sg64 = sb.at(70 * KB + 256, "sg64", (P, 64), F32)
    sudp7 = sb.at(70 * KB + 512, "sudp7", (P, 64), F32)
    suug7 = sb.at(70 * KB + 768, "suug7", (P, 16), F32)
    hbf7 = sb.at(71 * KB, "hbf7", (P, P), BF16)
    hbf27 = sb.at(71 * KB + 256, "hbf27", (P, P), BF16)
    wupb = [sb.at(OFF_MASK + d * 4 * KB, f"wupb{d}", (P, 16 * P), BF16)
            for d in range(2)]
    wgab = [sb.at(OFF_MASK + 8 * KB + d * 4 * KB, f"wgab{d}", (P, 16 * P), BF16)
            for d in range(2)]
    wdb7 = [sb.at(97 * KB + 512 + d * KB, f"wdb7_{d}", (P, 512), BF16)
            for d in range(4)]
    castb = [sb.at(OFF_MASK + d * 4 * KB, f"castb{d}", (P, 4 * 512), BF16)
             for d in range(2)]
    wdn_bf = nc.dram_tensor("wdn_bf", [64 * 4 * P, 512], BF16).ap()
    wdprow = sb.at(101 * KB + 512, "wdprow", (1, D), F32)
    wdpbc = sb.at(101 * KB + 512, "wdpbc", (P, D), F32)
    rrel7 = sb.at(109 * KB + 512, "rrel7", (P, 16 * 512), F32)

    with nc.psum_tensor("ztp0", [P, 512], BF16) as ztp0, \
         nc.psum_tensor("ztp1", [P, 512], BF16) as ztp1, \
         nc.psum_tensor("fps0", [P, 512], F32) as fps0, \
         nc.psum_tensor("fps1", [P, 512], F32) as fps1, \
         nc.psum_tensor("psU0", [P, 512], F32) as psU0, \
         nc.psum_tensor("psU1", [P, 512], F32) as psU1, \
         nc.psum_tensor("psG0", [P, 512], F32) as psG0, \
         nc.psum_tensor("psG1", [P, 512], F32) as psG1:
        ztp = [ztp0, ztp1]
        fps = [fps0, fps1]
        psU = [psU0, psU1]
        psG = [psG0, psG1]

        @on_sync
        def _(s):
            for j in range(NT):
                LD.dma(s, out=r_spill[j * P:(j + 1) * P, :],
                       in_=R[:, j * D:(j + 1) * D])
            LD.dma(s, out=su7[:, :], in_=IN.su)
            LD.dma(s, out=sg64[:, :], in_=IN.sg)
            LD.dma(s, out=sudp7[:, :], in_=IN.sudp)
            LD.dma(s, out=suug7[:, :], in_=IN.suug)
        chk_spill = LD.n

        @on_gp
        def _(g):
            WD.dma(g, out=hbf7[:, :], in_=IN.hbf)
        chk_hbf = WD.n

        @on_vec
        def _(v, chk_spill=chk_spill, chk_hbf=chk_hbf):
            v.wait_ge(LD.h, chk_spill)
            v.wait_ge(WD.h, chk_hbf)
            v.tensor_scalar(out=sg64[:, :], in0=sg64[:, :], scalar1=64.0,
                            scalar2=None, op0=ALU.mult)
            v.tensor_scalar(out=hbf27[:, :], in0=hbf7[:, :], scalar1=0.5,
                            scalar2=None, op0=ALU.mult)
            v.drain()
            VE.inc(v.tensor_scalar(out=c_t0[0], in0=c_t0[0], scalar1=1.0,
                                   scalar2=None, op0=ALU.mult))
        chk_c7 = VE.n

        x1T_done = [0, 0]
        ztf = [0, 0]
        zsl = [0, 0]
        fpf = [0, 0]
        tr_prev = 0
        for half in range(2):
            for jj in range(4):
                j = half * 4 + jj
                bi = j % 2

                @on_vec
                def _(v, j=j, bi=bi):
                    VE.inc(v.reduce_sum(out=c_s1[bi], in_=R[:, j * D:(j + 1) * D],
                                        axis=AX))

                @on_scl
                def _(scl, j=j, bi=bi):
                    SE.inc(scl.activation(out=jdump[:, :],
                                          in_=R[:, j * D:(j + 1) * D],
                                          func=AF.Square, accum_out=c_s2[bi]))
                chk_sq = SE.n

                @on_vec
                def _(v, bi=bi, chk_sq=chk_sq):
                    v.wait_ge(SE.h, chk_sq)
                    v.drain()
                    v.tensor_scalar(out=c_m[bi], in0=c_s1[bi], scalar1=1.0 / D,
                                    scalar2=None, op0=ALU.mult)
                    v.tensor_scalar(out=c_var[bi], in0=c_s2[bi], scalar1=1.0 / D,
                                    scalar2=1e-5, op0=ALU.mult, op1=ALU.add)
                    v.drain()
                    v.scalar_tensor_tensor(out=c_t0[bi], in0=c_m[bi], scalar=1.0,
                                           in1=c_m[bi], op0=ALU.mult, op1=ALU.mult)
                    v.drain()
                    VE.inc(v.tensor_sub(c_var[bi], c_var[bi], c_t0[bi]))
                chk_v2 = VE.n

                @on_scl
                def _(scl, bi=bi, chk_v2=chk_v2):
                    scl.wait_ge(VE.h, chk_v2)
                    SE.inc(scl.activation(out=c_rstd[bi], in_=c_var[bi],
                                          func=AF.Sqrt))
                chk_s2 = SE.n

                @on_vec
                def _(v, bi=bi, chk_s2=chk_s2):
                    v.wait_ge(SE.h, chk_s2)
                    v.drain()
                    v.reciprocal(c_nt1[bi], c_rstd[bi])
                    v.drain()
                    v.tensor_mul(c_nt2[bi], c_var[bi], c_nt1[bi])
                    v.drain()
                    v.tensor_add(c_rstd[bi], c_rstd[bi], c_nt2[bi])
                    v.drain()
                    v.tensor_scalar(out=c_rstd[bi], in0=c_rstd[bi], scalar1=0.5,
                                    scalar2=None, op0=ALU.mult)
                    v.drain()
                    v.reciprocal(c_rstd[bi], c_rstd[bi])
                    v.drain()
                    VE.inc(v.scalar_tensor_tensor(out=c_nmr[bi], in0=c_m[bi],
                                                  scalar=-1.0, in1=c_rstd[bi],
                                                  op0=ALU.mult, op1=ALU.mult))
                chk_v3 = VE.n

                @on_scl
                def _(scl, j=j, jj=jj, bi=bi, chk_v3=chk_v3, lf=tr_prev):
                    scl.wait_ge(VE.h, chk_v3)
                    scl.wait_ge(TE.h, lf)
                    SE.inc(scl.activation(out=ln3[:, jj * D:(jj + 1) * D],
                                          in_=R[:, j * D:(j + 1) * D],
                                          func=AF.Identity, bias=c_nmr[bi],
                                          scale=c_rstd[bi]))
            chk_ln4 = SE.n

            for c in range(16):
                sl = c % 2

                @on_ten
                def _(t, c=c, sl=sl, chk_ln4=chk_ln4, ztfree=ztf[sl]):
                    t.wait_ge(SE.h, chk_ln4)
                    t.wait_ge(VE.h, ztfree)
                    for jj in range(4):
                        i = t.transpose(ztp[sl][:, jj * P:(jj + 1) * P],
                                        ln3[:, jj * D + c * P:jj * D + (c + 1) * P],
                                        ident[:, :])
                    TE.inc(i)
                chk_tr = TE.n

                @on_vec
                def _(v, c=c, sl=sl, chk_tr=chk_tr, zfree=zsl[sl]):
                    v.wait_ge(TE.h, max(chk_tr, zfree))
                    VE.inc(v.tensor_scalar(out=zt7[sl][:, :], in0=ztp[sl][:, :],
                                           scalar1=suug7[:, c:c + 1], scalar2=None,
                                           op0=ALU.mult))
                ztf[sl] = VE.n
                chk_zev = VE.n

                @on_ten
                def _(t, half=half, c=c, sl=sl, chk_zev=chk_zev,
                      ffree=fpf[sl], chk_hbf=chk_hbf):
                    if half == 0 and c == 0:
                        t.wait_ge(WD.h, chk_hbf)
                    t.wait_ge(VE.h, max(chk_zev, ffree))
                    TE.inc(t.matmul(fps[sl][:, :], hbf7[:, :], zt7[sl][:, :],
                                    start=True, stop=True))
                zsl[sl] = TE.n
                chk_fm = TE.n

                @on_vec
                def _(v, c=c, sl=sl, chk_fm=chk_fm):
                    v.wait_ge(TE.h, chk_fm)
                    VE.inc(v.tensor_scalar(out=x1stage[:, c * 512:(c + 1) * 512],
                                           in0=fps[sl][:, :], scalar1=SQ2,
                                           scalar2=None, op0=ALU.mult))
                fpf[sl] = VE.n
            tr_prev = TE.n

            @on_vec
            def _(v, half=half):
                v.drain()
                for bit in (1, 2, 4):
                    for a in range(16):
                        if a & bit:
                            continue
                        b = a ^ bit
                        v.tensor_sub(x1stage[:, b * 512:(b + 1) * 512],
                                     x1stage[:, a * 512:(a + 1) * 512],
                                     x1stage[:, b * 512:(b + 1) * 512])
                    v.drain()
                    for a in range(16):
                        if a & bit:
                            continue
                        b = a ^ bit
                        v.scalar_tensor_tensor(
                            out=x1stage[:, a * 512:(a + 1) * 512],
                            in0=x1stage[:, a * 512:(a + 1) * 512], scalar=2.0,
                            in1=x1stage[:, b * 512:(b + 1) * 512],
                            op0=ALU.mult, op1=ALU.subtract)
                    v.drain()
                for a in range(8):
                    b = a + 8
                    A = x1stage[:, a * 512:(a + 1) * 512]
                    Bs = x1stage[:, b * 512:(b + 1) * 512]
                    v.tensor_sub(x1T[half][:, b * 512:(b + 1) * 512], A, Bs)
                    v.tensor_add(x1T[half][:, a * 512:(a + 1) * 512], A, Bs)
                v.drain()
                VE.inc(v.tensor_scalar(out=c_t0[0], in0=c_t0[0], scalar1=1.0,
                                       scalar2=None, op0=ALU.mult))
            x1T_done[half] = VE.n

        tr_all = TE.n
        x1_all = max(x1T_done[0], x1T_done[1])

        scope("p7_ug")
        wub_used = [tr_all, tr_all]
        psUf = [0, 0]
        psGf = [0, 0]
        silf = [0, 0]
        for o in range(64):
            d = o % 2

            @on_gp
            def _(g, o=o, d=d, wu=wub_used[d]):
                g.wait_ge(TE.h, wu)
                WDs[d].dma(g, out=wupb[d].ap().rearrange("p (k f) -> p k f", f=P),
                           in_=IN.wup[o * D:(o + 1) * D, :]
                               .rearrange("(k p) f -> p k f", p=P))
                WDs[d].dma(g, out=wgab[d].ap().rearrange("p (k f) -> p k f", f=P),
                           in_=IN.wgate[o * D:(o + 1) * D, :]
                               .rearrange("(k p) f -> p k f", p=P))
            chk_w = WDs[d].n

            for half in range(2):
                s_i = o * 2 + half
                bb = s_i % 2

                @on_ten
                def _(t, o=o, half=half, d=d, bb=bb, chk_w=chk_w,
                      uf=psUf[bb], gf=psGf[bb], x1k=x1_all):
                    t.wait_ge(WDs[d].h, chk_w)
                    t.wait_ge(VE.h, max(uf, x1k))
                    t.wait_ge(SE.h, gf)
                    for k in range(16):
                        t.matmul(psU[bb][:, :], wupb[d][:, k * P:(k + 1) * P],
                                 x1T[half][:, k * 512:(k + 1) * 512],
                                 start=(k == 0), stop=(k == 15))
                    for k in range(16):
                        i = t.matmul(psG[bb][:, :], wgab[d][:, k * P:(k + 1) * P],
                                     x1T[half][:, k * 512:(k + 1) * 512],
                                     start=(k == 0), stop=(k == 15))
                    TE.inc(i)
                wub_used[d] = TE.n
                chk_ug = TE.n

                @on_scl
                def _(scl, o=o, half=half, bb=bb, chk_ug=chk_ug, sf=silf[bb],
                      chk_c7=chk_c7):
                    if o == 0 and half == 0:
                        scl.wait_ge(VE.h, chk_c7)
                    scl.wait_ge(TE.h, chk_ug)
                    scl.wait_ge(VE.h, sf)
                    SE.inc(scl.activation(out=sil7[bb][:, :], in_=psG[bb][:, :],
                                          func=AF.Silu, scale=sg64[:, o:o + 1]))
                psGf[bb] = SE.n
                chk_sil = SE.n

                @on_vec
                def _(v, o=o, half=half, bb=bb, chk_ug=chk_ug, chk_sil=chk_sil):
                    v.wait_ge(TE.h, chk_ug)
                    v.drain()
                    v.tensor_scalar(out=utmp7[:, :], in0=psU[bb][:, :],
                                    scalar1=su7[:, o:o + 1], scalar2=64.0,
                                    op0=ALU.mult, op1=ALU.mult)
                    v.wait_ge(SE.h, chk_sil)
                    v.drain()
                    VE.inc(v.tensor_tensor(
                        out=h7[half][:, o * 512:(o + 1) * 512],
                        in0=utmp7[:, :], in1=sil7[bb][:, :], op=ALU.mult))
                psUf[bb] = VE.n
                silf[bb] = VE.n
        ug_all = TE.n

        scope("p7_fwht2")
        hd2b = [fps0, fps1, psU0, psU1]
        hd2fv = [fpf[0], fpf[1], psUf[0], psUf[1]]
        hd2fs = [0, 0, 0, 0]
        hd2_done = [0, 0]
        bf_all = 0
        for half in range(2):
            @on_vec
            def _(v, half=half):
                v.drain()
                for c in range(64):
                    v.tensor_scalar(out=h7[half][:, c * 512:(c + 1) * 512],
                                    in0=h7[half][:, c * 512:(c + 1) * 512],
                                    scalar1=sudp7[:, c:c + 1], scalar2=None,
                                    op0=ALU.mult)
                v.drain()
                for bit in (1, 2, 4, 8, 16, 32):
                    for a in range(64):
                        if a & bit:
                            continue
                        b = a ^ bit
                        v.tensor_sub(h7[half][:, b * 512:(b + 1) * 512],
                                     h7[half][:, a * 512:(a + 1) * 512],
                                     h7[half][:, b * 512:(b + 1) * 512])
                    v.drain()
                    for a in range(64):
                        if a & bit:
                            continue
                        b = a ^ bit
                        v.scalar_tensor_tensor(
                            out=h7[half][:, a * 512:(a + 1) * 512],
                            in0=h7[half][:, a * 512:(a + 1) * 512], scalar=2.0,
                            in1=h7[half][:, b * 512:(b + 1) * 512],
                            op0=ALU.mult, op1=ALU.subtract)
                    v.drain()
                VE.inc(v.tensor_scalar(out=c_t0[1], in0=c_t0[1], scalar1=1.0,
                                       scalar2=None, op0=ALU.mult))
            chk_bf = VE.n
            bf_all = VE.n

            for c in range(64):
                sl = c % 4

                @on_ten
                def _(t, half=half, c=c, sl=sl, chk_bf=chk_bf,
                      fv=hd2fv[sl], fs=hd2fs[sl]):
                    t.wait_ge(VE.h, max(chk_bf, fv))
                    t.wait_ge(SE.h, fs)
                    TE.inc(t.matmul(hd2b[sl][:, :], hbf27[:, :],
                                    h7[half][:, c * 512:(c + 1) * 512],
                                    start=True, stop=True))
                chk_h2 = TE.n

                @on_scl
                def _(scl, half=half, c=c, sl=sl, chk_h2=chk_h2):
                    scl.wait_ge(TE.h, chk_h2)
                    SE.inc(scl.copy(h7[half][:, c * 512:(c + 1) * 512],
                                    hd2b[sl][:, :]))
                hd2fs[sl] = SE.n
            hd2_done[half] = SE.n

        @on_sync
        def _(s, ug_all=ug_all):
            s.wait_ge(TE.h, ug_all)
            LD.dma(s, out=wdprow[0:1, :], in_=IN.wdp)
        chk_wdp = LD.n

        bc_se0 = hd2fs[0]
        bc_ev7 = 0
        for qq in range(4):
            @on_ten
            def _(t, qq=qq, chk_wdp=chk_wdp, ev=bc_ev7, se0=bc_se0):
                t.wait_ge(LD.h, chk_wdp)
                t.wait_ge(SE.h, se0)
                t.wait_ge(VE.h, ev)
                TE.inc(t.matmul(fps0[:, :], onesf_row,
                                wdprow[0:1, qq * 512:(qq + 1) * 512],
                                start=True, stop=True))
            chk_bm = TE.n

            @on_vec
            def _(v, qq=qq, chk_bm=chk_bm):
                v.wait_ge(TE.h, chk_bm)
                VE.inc(v.tensor_copy(wdpbc[:, qq * 512:(qq + 1) * 512],
                                     fps0[:, :]))
            bc_ev7 = VE.n
        chk_wbc = VE.n

        cast_ld = [0, 0]
        cast_at = []
        for i in range(64):
            d = i % 2

            @on_gp
            def _(g, i=i, d=d, lf=cast_ld[d], ug_all=ug_all):
                g.wait_ge(TE.h, ug_all)
                g.wait_ge(LD.h, lf)
                WDs[d].dma(g, out=castb[d].ap().rearrange("p (c f) -> p c f",
                                                          f=512),
                           in_=IN.wdown[i * 512:(i + 1) * 512, :]
                               .rearrange("(c p) f -> p c f", p=P))
            chk_ci = WDs[d].n

            @on_sync
            def _(s, i=i, d=d, chk_ci=chk_ci):
                s.wait_ge(WDs[d].h, chk_ci)
                LD.dma(s, out=wdn_bf[i * 512:(i + 1) * 512, :]
                           .rearrange("(c p) f -> p c f", p=P),
                       in_=castb[d].ap().rearrange("p (c f) -> p c f", f=512))
            cast_ld[d] = LD.n
            cast_at.append(LD.n)

    scope("p7_down")
    with nc.psum_tensor("psD0", [P, 512], F32) as psD0, \
         nc.psum_tensor("psD1", [P, 512], F32) as psD1, \
         nc.psum_tensor("psD2", [P, 512], F32) as psD2, \
         nc.psum_tensor("psD3", [P, 512], F32) as psD3, \
         nc.psum_tensor("psD4", [P, 512], F32) as psD4, \
         nc.psum_tensor("psD5", [P, 512], F32) as psD5, \
         nc.psum_tensor("psD6", [P, 512], F32) as psD6, \
         nc.psum_tensor("psD7", [P, 512], F32) as psD7:
        psD = [psD0, psD1, psD2, psD3, psD4, psD5, psD6, psD7]
        hd2_all = max(hd2_done[0], hd2_done[1])
        wd_used = [ug_all, ug_all, ug_all, ug_all]
        st_free = [0, 0]
        ep_ve = chk_wbc
        rr_free = [0, 0]
        for oh in range(4):
            rp = oh % 2

            @on_sync
            def _(s, oh=oh, rp=rp, ug_all=ug_all, epf=rr_free[rp]):
                s.wait_ge(TE.h, ug_all)
                s.wait_ge(VE.h, epf)
                for i8 in range(NT):
                    LD.dma(s, out=rrel7[:, (rp * 8 + i8) * 512:
                                        (rp * 8 + i8 + 1) * 512],
                           in_=r_spill[i8 * P:(i8 + 1) * P,
                                       oh * 512:(oh + 1) * 512])
            chk_rr = LD.n

            for k in range(64):
                d = k % 4
                sd = k % 2

                @on_gp
                def _(g, oh=oh, k=k, d=d, sd=sd, wu=wd_used[d], bf_all=bf_all,
                      ca=cast_at[k], first=(oh == 0 and k < 4)):
                    g.wait_ge(TE.h, wu)
                    g.wait_ge(LD.h, ca)
                    if first:
                        g.wait_ge(VE.h, bf_all)
                    WDs[sd].dma(g, out=wdb7[d][:, :],
                                in_=wdn_bf[(k * 4 + oh) * P:
                                           (k * 4 + oh + 1) * P, :])
                chk_wd = WDs[sd].n

                @on_ten
                def _(t, k=k, d=d, sd=sd, chk_wd=chk_wd, hd=hd2_all, ep=ep_ve):
                    t.wait_ge(WDs[sd].h, chk_wd)
                    if k == 0:
                        t.wait_ge(SE.h, hd)
                        t.wait_ge(VE.h, ep)
                    for half in range(2):
                        for tt in range(4):
                            i = t.matmul(
                                psD[half * 4 + tt][:, :],
                                h7[half][:, k * 512 + tt * P:
                                         k * 512 + (tt + 1) * P],
                                wdb7[d][:, :],
                                start=(k == 0), stop=(k == 63))
                    TE.inc(i)
                wd_used[d] = TE.n
            chk_acc = TE.n

            for i8 in range(NT):
                par = i8 % 2

                @on_vec
                def _(v, oh=oh, i8=i8, par=par, rp=rp, chk_acc=chk_acc,
                      chk_rr=chk_rr, stf=st_free[par]):
                    v.wait_ge(TE.h, chk_acc)
                    v.wait_ge(LD.h, chk_rr)
                    v.wait_ge(ST.h, stf)
                    v.drain()
                    v.scalar_tensor_tensor(
                        out=etmp7[par][:, :], in0=psD[i8][:, :], scalar=SC_DN,
                        in1=wdpbc[:, oh * 512:(oh + 1) * 512],
                        op0=ALU.mult, op1=ALU.mult)
                    v.drain()
                    VE.inc(v.tensor_tensor(
                        out=etmp7[par][:, :], in0=etmp7[par][:, :],
                        in1=rrel7[:, (rp * 8 + i8) * 512:
                                  (rp * 8 + i8 + 1) * 512],
                        op=ALU.add))
                chk_ep = VE.n

                @on_sync
                def _(s, oh=oh, i8=i8, par=par, chk_ep=chk_ep):
                    s.wait_ge(VE.h, chk_ep)
                    ST.dma(s, out=y_out[i8 * P:(i8 + 1) * P,
                                        oh * 512:(oh + 1) * 512],
                           in_=etmp7[par][:, :])
                st_free[par] = ST.n
            ep_ve = VE.n
            rr_free[rp] = VE.n

    @on_sync
    def _(s):
        ST.wait(s)

    @on_gp
    def _(g):
        ST.wait(g)

    if _cur_scope[0] is not None:
        _cur_scope[0].__exit__(None, None, None)
    blk.__exit__(None, None, None)
    return nc, set(_declared)


def kernel(**inputs):
    in_maps = _prep_inputs(inputs)
    if "nc" not in _CACHE:
        _CACHE["nc"], _CACHE["used"] = build_program()
    nc, used = _CACHE["nc"], _CACHE["used"]
    in_maps = [{k: v for k, v in m.items() if k in used} for m in in_maps]
    res = run_bass_kernel_spmd(nc, in_maps, core_ids=list(range(NCORE)))
    return _scatter_output(res.results)
